# version 07
# baseline (speedup 1.0000x reference)
import contextlib
import numpy as np
import ml_dtypes
import concourse.bass as bass
import concourse.mybir as mybir
import concourse.bass_utils as bass_utils

F32 = mybir.dt.float32
BF16 = mybir.dt.bfloat16
AX = mybir.AxisListType.X
ALU = mybir.AluOpType
ACT = mybir.ActivationFunctionType

T = 4096
E = 1024
H = 16
HD = 64
NCH = 8
NTT = 32
NG = 8
NQC = 16
NJB = 8
FF = 4096
LN2C = float(np.log(2.0))


def consts_np():
    c = {}
    invf = (1.0 / 10000.0 ** (np.arange(0, HD, 2) / HD)).astype(np.float64)
    t = np.arange(T, dtype=np.float64)
    d = np.arange(HD)
    ang = t[None, :] * invf[d % 32][:, None]
    cos64, sin64 = np.cos(ang), np.sin(ang)
    c["COS"] = np.concatenate([cos64, cos64], 0).astype(ml_dtypes.bfloat16)
    c["SIN"] = np.concatenate([sin64, sin64], 0).astype(ml_dtypes.bfloat16)
    rot = np.zeros((HD, HD), np.float32)
    for i in range(HD // 2):
        rot[2 * i + 1, 2 * i] = -1.0
        rot[2 * i, 2 * i + 1] = 1.0
    z = np.zeros_like(rot)
    c["ROT"] = np.block([[rot, z], [z, rot]]).astype(ml_dtypes.bfloat16)
    kap = np.arange(128)[:, None]
    qr = np.arange(256)[None, :]
    c["DTAB"] = np.concatenate(
        [np.maximum(kap + 128 * dd - qr, 0) for dd in range(4)],
        axis=1).astype(np.float32)
    c["QREL"] = np.tile(np.arange(256, dtype=np.float32)[None, :], (65, 1))
    c["KP"] = (kap + 256 * np.arange(1, 16)[None, :]).astype(np.float32)
    c["ONESF"] = np.ones((1, 128), np.float32)
    return c


class Prog:
    def __init__(self, nc):
        self.nc = nc
        self.ops = []
        self.sem = {}

    def new_sem(self, stack, name):
        h = stack.enter_context(self.nc.semaphore(name))
        self.sem[name] = [h, 0]

    def inc(self, name, n=1):
        s = self.sem[name]
        s[1] += n
        return s[1]

    def val(self, name):
        return self.sem[name][1]

    def hdl(self, name):
        return self.sem[name][0]

    def add(self, eng, fn):
        self.ops.append((eng, fn))

    def emit(self, block):
        for eng_name in ("sync", "gpsimd", "scalar", "vector", "tensor"):
            ops = [f for e, f in self.ops if e == eng_name]

            def body(eng, ops=ops):
                for f in ops:
                    f(eng)
            getattr(block, eng_name)(body)


def build_nc(taps=(), upto="full", ngroups=NG):
    nc = bass.Bass("TRN2", target_bir_lowering=False)
    CN = consts_np()

    x_d = nc.dram_tensor("x", [T, E], F32, kind="ExternalInput").ap()
    y_d = nc.dram_tensor("y", [T, E], F32, kind="ExternalOutput").ap()
    ccol_d = nc.dram_tensor("ccol", [128, NCH], F32, kind="ExternalInput").ap()
    wmod_d = nc.dram_tensor("wmod", [E, 6 * E], F32, kind="ExternalInput").ap()
    wT_d = {w: nc.dram_tensor(w + "T", [E, E], F32, kind="ExternalInput").ap()
            for w in ("wq", "wk", "wv", "wg", "wo")}
    wdT_d = nc.dram_tensor("wdT", [E, H], F32, kind="ExternalInput").ap()
    bd_d = nc.dram_tensor("bdrow", [1, H], F32, kind="ExternalInput").ap()
    vec_d = {v: nc.dram_tensor(v + "col", [128, NCH], F32, kind="ExternalInput").ap()
             for v in ("bq", "bk", "bg", "bo", "gamma", "beta")}
    bvrow_d = nc.dram_tensor("bvrow", [1, E], F32, kind="ExternalInput").ap()
    wfcT_d = nc.dram_tensor("wfcT", [E, FF], F32, kind="ExternalInput").ap()
    wprT_d = nc.dram_tensor("wprT", [FF, E], F32, kind="ExternalInput").ap()

    def scr_t(name, shape, dt):
        kind = "ExternalOutput" if name in taps else "Internal"
        return nc.dram_tensor(name, shape, dt, kind=kind).ap()

    g2s = scr_t("g2s", [E, T], BF16)
    x1s = scr_t("x1s", [T, E], BF16)
    h2s = scr_t("h2s", [E, T], BF16)

    tap_d = {}

    def tap(name, shape, dt):
        tap_d[name] = nc.dram_tensor(name, shape, dt, kind="ExternalOutput").ap()
        return tap_d[name]

    cst = {k: nc.inline_tensor(v, name="c_" + k).ap() for k, v in CN.items()}

    P = Prog(nc)
    stack = contextlib.ExitStack()
    with stack:
        for s in ("s_x", "s_w", "s_tp", "s_scr", "s_out",
                  "s_mm", "s_sc", "s_vc", "s_av"):
            P.new_sem(stack, s)
        SX, SW, STP, SSCR, SOUT = (P.hdl(s) for s in
                                   ("s_x", "s_w", "s_tp", "s_scr", "s_out"))
        SMM, SSC, SVC, SAV = (P.hdl(s) for s in ("s_mm", "s_sc", "s_vc", "s_av"))

        psS0 = stack.enter_context(nc.psum_tensor("psS0", [128, 1024], F32))
        psS1 = stack.enter_context(nc.psum_tensor("psS1", [128, 1024], F32))
        psS = (psS0, psS1)
        psA0 = psS0[:, 0:512]
        psA1 = psS1[:, 0:512]
        psB0 = psS0[:, 512:1024]
        psB1 = psS1[:, 512:1024]
        psOA = stack.enter_context(nc.psum_tensor("psOA", [128, 512], F32))
        psOB = stack.enter_context(nc.psum_tensor("psOB", [128, 512], F32))
        psG1 = stack.enter_context(nc.psum_tensor("psG1", [128, 512], F32))
        psG2 = stack.enter_context(nc.psum_tensor("psG2", [128, 512], F32))

        sb = lambda n, sh, dt: stack.enter_context(nc.sbuf_tensor("sb_" + n, sh, dt))
        ccol = sb("ccol", [128, NCH], F32)
        pvec = {v: sb(v, [128, NCH], F32) for v in
                ("bq", "bk", "bg", "bo", "gamma", "beta")}
        bvrow = sb("bvrow", [1, E], F32)
        wdT = sb("wdT", [128, NCH * H], F32)
        bdrow = sb("bdrow", [1, H], F32)
        ONESF = sb("ONESF", [1, 128], F32)
        onescol = sb("onescol", [128, 1], F32)
        epscol = sb("epscol", [128, 1], F32)
        negones = sb("negones", [128, 1], F32)
        halfcol = sb("halfcol", [128, 1], F32)
        n5l2 = sb("n5l2", [1, 1], F32)
        modT = sb("modT", [128, 48], F32)
        gbrow = sb("gbrow", [1, E], F32)
        cap1 = sb("cap1", [128, NCH], F32)
        cbp1 = sb("cbp1", [128, NCH], F32)
        gacl = sb("gacl", [128, NCH], F32)
        gbcl = sb("gbcl", [128, NCH], F32)
        negbg = sb("negbg", [128, NCH], F32)
        halfbg = sb("halfbg", [128, NCH], F32)
        hmean = sb("hmean", [128, NCH], F32)
        ldrow = sb("ldrow", [1, H], F32)
        pdtmp = sb("pdtmp", [1, H], F32)
        smallp = sb("smallp", [128, 16], F32)
        silu_c = sb("silu_c", [128, NCH], BF16)

        def mcol(vi, c):
            return modT[:, vi * 8 + c: vi * 8 + c + 1]

        def f0(sy):
            for name, dst in (("COS", COS), ("SIN", SIN), ("ROT", ROTm),
                              ("DTAB", DTAB), ("QREL", QREL), ("KP", KP),
                              ("ONESF", ONESF)):
                sy.dma_start(out=dst[:], in_=cst[name]).then_inc(SX, 16)
            sy.dma_start(out=ccol[:], in_=ccol_d).then_inc(SX, 16)
            for v in pvec:
                sy.dma_start(out=pvec[v][:], in_=vec_d[v]).then_inc(SX, 16)
            sy.dma_start(out=bvrow[:], in_=bvrow_d).then_inc(SX, 16)
            sy.dma_start(out=bdrow[:], in_=bd_d).then_inc(SX, 16)
            for c in range(NCH):
                sy.dma_start(out=wdT[:, c * H:(c + 1) * H],
                             in_=wdT_d[c * 128:(c + 1) * 128, :]).then_inc(SX, 16)
        P.add("sync", f0)
        ld_done = P.inc("s_x", 16 * (7 + 1 + 6 + 2 + NCH))

        def f0v(v):
            v.memset(onescol[:], 1.0)
            v.memset(epscol[:], 1e-5)
            v.memset(negones[:], -1.0)
            v.memset(halfcol[:], 0.5)
            v.memset(n5l2[:], -5.0 * LN2C).then_inc(SVC, 1)
        P.add("vector", f0v)
        P.inc("s_vc")

        def f0s(sc):
            sc.wait_ge(SX, ld_done)
            sc.activation(smallp[:, 0:8], ccol[:], ACT.Exp, bias=0.0,
                          scale=-1.0).then_inc(SSC, 1)
        P.add("scalar", f0s)
        e_done = P.inc("s_sc")

        def f0v2(v):
            v.wait_ge(SSC, e_done)
            v.tensor_scalar(smallp[:, 0:8], smallp[:, 0:8], 1.0, None, ALU.add)
            v.drain()
            v.reciprocal(smallp[:, 8:16], smallp[:, 0:8])
            v.drain()
            v.tensor_tensor(silu_c[:], ccol[:], smallp[:, 8:16],
                            ALU.mult).then_inc(SVC, 1)
        P.add("vector", f0v2)
        silu_done = P.inc("s_vc")

        if "dbg0" in taps:
            t_cc = tap("ccolv", [128, NCH], F32)
            t_sp = tap("smallpv", [128, 16], F32)
            t_si = tap("siluv", [128, NCH], BF16)

            def ftap0(sy, n=silu_done, t_cc=t_cc, t_sp=t_sp, t_si=t_si):
                sy.wait_ge(SVC, n)
                sy.dma_start(out=t_cc, in_=ccol[:]).then_inc(SOUT, 16)
                sy.dma_start(out=t_sp, in_=smallp[:]).then_inc(SOUT, 16)
                sy.dma_start(out=t_si, in_=silu_c[:]).then_inc(SOUT, 16)
            P.add("sync", ftap0)
            P.inc("s_out", 48)

        with (
            nc.sbuf_tensor("sb_hT", [128, NCH * T], BF16) as hT,
            contextlib.ExitStack() as gstk,
            contextlib.ExitStack() as p1s,
        ):
            ga_ = lambda n, sh, dt: gstk.enter_context(nc.sbuf_tensor("sb_" + n, sh, dt))
            COS = ga_("COS", [128, T], BF16)
            SIN = ga_("SIN", [128, T], BF16)
            ROTm = ga_("ROTm", [128, 128], BF16)
            DTAB = ga_("DTAB", [128, 1024], F32)
            QREL = ga_("QREL", [65, 256], F32)
            KP = ga_("KP", [128, 15], F32)
            a1 = lambda n, sh, dt: p1s.enter_context(nc.sbuf_tensor("sb_" + n, sh, dt))
            xf = a1("xf", [128, 2 * E], F32)
            xn = a1("xn", [128, 2 * E], BF16)
            lntmp = a1("lntmp", [128, E], F32)
            lnv = a1("lnv", [128, 8], F32)
            wmst = a1("wmst", [128, 2 * 3072], BF16)
            modrow = a1("modrow", [1, 6 * E], F32)
            gbtmp = a1("gbtmp", [1, E], F32)

            slots6 = [(psS0, 0), (psS1, 0), (psS0, 512),
                      (psS1, 512), (psOA, 0), (psOB, 0)]
            for half in range(2):
                off = half * 3072
                for c in range(NCH):
                    par = c % 2
                    dst = wmst[:, par * 3072:(par + 1) * 3072]
                    mm_need = P.val("s_mm")

                    def fw(g, dst=dst, c=c, off=off, mm_need=mm_need):
                        if mm_need:
                            g.wait_ge(SMM, mm_need)
                        g.dma_start(out=dst,
                                    in_=wmod_d[c * 128:(c + 1) * 128, off:off + 3072]
                                    ).then_inc(SW, 16)
                    P.add("gpsimd", fw)
                    w_done = P.inc("s_w", 16)
                    ev_prev = P.val("s_vc")

                    def fm(t, c=c, par=par, w_done=w_done, ev_prev=ev_prev,
                           silu_done=silu_done):
                        t.wait_ge(SW, w_done)
                        t.wait_ge(SVC, max(silu_done, ev_prev if c == 0 else 0))
                        for n in range(6):
                            s6t, s6c = slots6[n]
                            ins = t.matmul(
                                s6t[0:1, s6c:s6c + 512], silu_c[:, c:c + 1],
                                wmst[:, par * 3072 + n * 512:par * 3072 + (n + 1) * 512],
                                start=(c == 0), stop=(c == NCH - 1))
                            if n == 5:
                                ins.then_inc(SMM, 1)
                    P.add("tensor", fm)
                    P.inc("s_mm")
                mm_done = P.val("s_mm")

                def fe(v, off=off, mm_done=mm_done):
                    v.wait_ge(SMM, mm_done)
                    for n in range(6):
                        s6t, s6c = slots6[n]
                        ins = v.tensor_copy(modrow[:, off + n * 512:off + (n + 1) * 512],
                                            s6t[0:1, s6c:s6c + 512])
                        if n == 5:
                            ins.then_inc(SVC, 1)
                P.add("vector", fe)
                P.inc("s_vc")
            ev_done = P.val("s_vc")

            def fmt(t, ev_done=ev_done):
                t.wait_ge(SVC, ev_done)
                for n in range(48):
                    ins = t.matmul(psG1[:, n:n + 1],
                                   modrow[0:1, n * 128:(n + 1) * 128],
                                   ONESF[0:1, 0:1], start=True, stop=True)
                    if n == 47:
                        ins.then_inc(SMM, 1)
            P.add("tensor", fmt)
            mt_done = P.inc("s_mm")

            def fmod(v, mt_done=mt_done):
                v.wait_ge(SMM, mt_done)
                v.tensor_copy(modT[:], psG1[:, 0:48])
                v.drain()
                v.tensor_scalar(cap1[:], modT[:, 8:16], 1.0, None, ALU.add)
                v.tensor_scalar(cbp1[:], modT[:, 32:40], 1.0, None, ALU.add)
                v.drain()
                for src, dstv in ((modT[:, 16:24], gacl), (modT[:, 40:48], gbcl)):
                    v.tensor_single_scalar(smallp[:, 0:8], src, 0.0, ALU.is_ge)
                    v.drain()
                    v.tensor_scalar(smallp[:, 0:8], smallp[:, 0:8], 2.0,
                                    negones[:, 0:1], ALU.mult, ALU.add)
                    v.drain()
                    v.tensor_single_scalar(smallp[:, 8:16], src, -1.0, ALU.mult)
                    v.drain()
                    v.tensor_tensor(smallp[:, 8:16], smallp[:, 8:16], src, ALU.max)
                    v.drain()
                    v.tensor_single_scalar(smallp[:, 8:16], smallp[:, 8:16],
                                           0.25, ALU.max)
                    v.drain()
                    v.tensor_tensor(dstv[:], smallp[:, 0:8], smallp[:, 8:16],
                                    ALU.mult)
                    v.drain()
                v.tensor_single_scalar(negbg[:], pvec["bg"][:], -1.0,
                                       ALU.mult)
                v.tensor_single_scalar(halfbg[:], pvec["bg"][:], 0.5, ALU.mult)
                src_r = modrow[0:1, 5 * E:6 * E]
                v.tensor_single_scalar(gbrow[:], src_r, 0.0, ALU.is_ge)
                v.drain()
                v.tensor_scalar(gbrow[:], gbrow[:], 2.0, negones[0:1, 0:1],
                                ALU.mult, ALU.add)
                v.drain()
                v.tensor_single_scalar(gbtmp[:], src_r, -1.0, ALU.mult)
                v.drain()
                v.tensor_tensor(gbtmp[:], gbtmp[:], src_r, ALU.max)
                v.drain()
                v.tensor_single_scalar(gbtmp[:], gbtmp[:], 0.25, ALU.max)
                v.drain()
                v.tensor_tensor(gbrow[:], gbrow[:], gbtmp[:],
                                ALU.mult).then_inc(SVC, 1)
            P.add("vector", fmod)
            mod_done = P.inc("s_vc")

            for tt in range(NTT):
                par = tt % 2
                xfp = xf[:, par * E:(par + 1) * E]
                xnp = xn[:, par * E:(par + 1) * E]
                vc_need = P.val("s_vc") if tt >= 2 else 0

                def fl(sy, tt=tt, xfp=xfp, vc_need=vc_need):
                    if vc_need:
                        sy.wait_ge(SVC, vc_need)
                    sy.dma_start(out=xfp, in_=x_d[tt * 128:(tt + 1) * 128, :]
                                 ).then_inc(SX, 16)
                P.add("sync", fl)
                x_done = P.inc("s_x", 16)

                def fln(v, xfp=xfp, x_done=x_done, tt=tt):
                    v.wait_ge(SX, x_done)
                    v.reduce_sum(lnv[:, 0:1], xfp, axis=AX)
                    v.tensor_tensor(lntmp[:], xfp, xfp, ALU.mult)
                    v.reduce_sum(lnv[:, 1:2], lntmp[:], axis=AX).then_inc(SVC, 1)
                P.add("vector", fln)
                st_done = P.inc("s_vc")

                def frs_a(sc, st_done=st_done):
                    sc.wait_ge(SVC, st_done)
                    sc.activation(lnv[:, 5:6], lnv[:, 0:1], ACT.Copy,
                                  bias=0.0, scale=1.0 / E)
                    sc.activation(lnv[:, 2:3], lnv[:, 0:1], ACT.Square,
                                  bias=0.0, scale=1.0 / E).then_inc(SSC, 1)
                P.add("scalar", frs_a)
                rsa = P.inc("s_sc")

                def fem(v, rsa=rsa):
                    v.wait_ge(SSC, rsa)
                    v.tensor_scalar(lnv[:, 3:4], lnv[:, 2:3], -1.0,
                                    epscol[:, 0:1], ALU.mult,
                                    ALU.add).then_inc(SVC, 1)
                P.add("vector", fem)
                emv = P.inc("s_vc")

                def frs_b(sc, emv=emv):
                    sc.wait_ge(SVC, emv)
                    sc.activation(lnv[:, 3:4], lnv[:, 1:2], ACT.Ln,
                                  bias=lnv[:, 3:4], scale=1.0 / E)
                    sc.drain()
                    sc.activation(lnv[:, 4:5], lnv[:, 3:4], ACT.Exp,
                                  bias=0.0, scale=-0.5).then_inc(SSC, 1)
                P.add("scalar", frs_b)
                rs_done = P.inc("s_sc")
                tp_prev = P.val("s_tp") if tt >= 2 else 0

                def fxn(v, xfp=xfp, xnp=xnp, rs_done=rs_done, tp_prev=tp_prev):
                    v.wait_ge(SSC, rs_done)
                    if tp_prev:
                        v.wait_ge(STP, tp_prev)
                    v.tensor_scalar(xnp, xfp, lnv[:, 5:6], lnv[:, 4:5],
                                    ALU.subtract, ALU.mult).then_inc(SVC, 1)
                P.add("vector", fxn)
                xn_done = P.inc("s_vc")

                def ftp(sq, tt=tt, xnp=xnp, xn_done=xn_done):
                    sq.wait_ge(SVC, xn_done)
                    for c in range(NCH):
                        sq.dma_start_transpose(
                            hT[:, c * T + tt * 128:c * T + (tt + 1) * 128],
                            xnp[:, c * 128:(c + 1) * 128]).then_inc(STP, 16)
                P.add("scalar", ftp)
                P.inc("s_tp", 16 * NCH)
            tp_all = P.val("s_tp")

            for c in range(NCH):
                def fa(v, c=c, tp_all=tp_all):
                    if c == 0:
                        v.wait_ge(STP, tp_all)
                    v.tensor_scalar(hT[:, c * T:(c + 1) * T],
                                    hT[:, c * T:(c + 1) * T],
                                    cap1[:, c:c + 1], mcol(0, c),
                                    ALU.mult, ALU.add)
                    v.reduce_sum(hmean[:, c:c + 1], hT[:, c * T:(c + 1) * T], axis=AX)
                    v.drain()
                    v.tensor_single_scalar(hmean[:, c:c + 1], hmean[:, c:c + 1],
                                           1.0 / T, ALU.mult).then_inc(SVC, 1)
                P.add("vector", fa)
                P.inc("s_vc")
            h_done = P.val("s_vc")

            if "hT" in taps:
                tpp = tap("hT", [128, NCH * T], BF16)

                def ftap(sy, n=h_done, tpp=tpp):
                    sy.wait_ge(SVC, n)
                    sy.dma_start(out=tpp, in_=hT[:]).then_inc(SOUT, 16)
                P.add("sync", ftap)
                P.inc("s_out", 16)

            def fpd_mm(t, h_done=h_done):
                t.wait_ge(SVC, h_done)
                for c in range(NCH):
                    ins = t.matmul(psG1[0:1, 0:H], hmean[:, c:c + 1],
                                   wdT[:, c * H:(c + 1) * H],
                                   start=(c == 0), stop=(c == NCH - 1))
                    if c == NCH - 1:
                        ins.then_inc(SMM, 1)
            P.add("tensor", fpd_mm)
            pdm = P.inc("s_mm")

            def fz(v, pdm=pdm):
                v.wait_ge(SMM, pdm)
                v.tensor_tensor(pdtmp[:], psG1[0:1, 0:H], bdrow[:],
                                ALU.add).then_inc(SVC, 1)
            P.add("vector", fz)
            z_done = P.inc("s_vc")

            def fpe(sc, z_done=z_done):
                sc.wait_ge(SVC, z_done)
                sc.activation(ldrow[:], pdtmp[:], ACT.Exp, bias=0.0,
                              scale=-1.0).then_inc(SSC, 1)
            P.add("scalar", fpe)
            pe_done = P.inc("s_sc")

            def fsg(v, pe_done=pe_done):
                v.wait_ge(SSC, pe_done)
                v.tensor_scalar(ldrow[:], ldrow[:], 1.0, None, ALU.add)
                v.drain()
                v.reciprocal(pdtmp[:], ldrow[:])
                v.drain()
                v.nop().then_inc(SVC, 1)
            P.add("vector", fsg)
            pd_done = P.inc("s_vc")

            def fu(sc, pd_done=pd_done):
                sc.wait_ge(SVC, pd_done)
                sc.activation(ldrow[:], pdtmp[:], ACT.Exp, bias=n5l2[0:1, 0:1],
                              scale=-8.0 * LN2C).then_inc(SSC, 1)
            P.add("scalar", fu)
            u_done = P.inc("s_sc")

            def fv2(v, u_done=u_done):
                v.wait_ge(SSC, u_done)
                v.tensor_scalar(ldrow[:], ldrow[:], -1.0, onescol[0:1, 0:1],
                                ALU.mult, ALU.add).then_inc(SVC, 1)
            P.add("vector", fv2)
            v_done = P.inc("s_vc")

            def fl2(sc, v_done=v_done):
                sc.wait_ge(SVC, v_done)
                sc.activation(ldrow[:], ldrow[:], ACT.Ln, bias=0.0,
                              scale=1.0).then_inc(SSC, 1)
            P.add("scalar", fl2)
            ld_chain = P.inc("s_sc")

            if "ld" in taps:
                tpp = tap("ld", [1, H], F32)

                def ftap(sy, n=ld_chain, tpp=tpp):
                    sy.wait_ge(SSC, n)
                    sy.dma_start(out=tpp, in_=ldrow[:]).then_inc(SOUT, 16)
                P.add("sync", ftap)
                P.inc("s_out", 16)
            if "dbg1" in taps:
                t_mod = tap("modT", [128, 48], F32)
                t_hm = tap("hmean", [128, NCH], F32)
                t_pd = tap("pdt", [1, H], F32)
                t_sc = tap("siluc", [128, NCH], BF16)

                def ftap2(sy, n=ld_chain):
                    sy.wait_ge(SSC, n)
                    sy.dma_start(out=t_mod, in_=modT[:]).then_inc(SOUT, 16)
                    sy.dma_start(out=t_hm, in_=hmean[:]).then_inc(SOUT, 16)
                    sy.dma_start(out=t_pd, in_=pdtmp[:]).then_inc(SOUT, 16)
                    sy.dma_start(out=t_sc, in_=silu_c[:]).then_inc(SOUT, 16)
                P.add("sync", ftap2)
                P.inc("s_out", 64)

            p1s.close()

            NPAR = 1
            with contextlib.ExitStack() as ast:
                aa = lambda n, sh, dt: ast.enter_context(nc.sbuf_tensor("sb_" + n, sh, dt))
                QT = [aa(f"QT{i}", [128, T], BF16) for i in range(NPAR)]
                KT = [aa(f"KT{i}", [128, T], BF16) for i in range(NPAR)]
                GT = [aa(f"GT{i}", [128, T], BF16) for i in range(NPAR)]
                VT = [aa(f"VT{i}", [128, 2 * 48 * 65], BF16) for i in range(NPAR)]
                WSL = aa("WSL", [128, 4 * E], BF16)
                UBm = aa("UBm", [128, 2048], BF16)
                Opack = aa("Opack", [128, T], BF16)
                DensA = aa("DensA", [1, T], F32)
                DensB = aa("DensB", [1, T], F32)
                tmp65 = aa("tmp65", [65, 256], F32)
                g2b = aa("g2b", [128, 2 * 512], BF16)
                EB = [aa(f"EB{i}", [128, 2 * 2 * 256], BF16) for i in range(NPAR)]
                Bb = [aa(f"Bb{i}", [65, 2 * 256], F32) for i in range(NPAR)]
                BV = [aa(f"BV{i}", [128, 2 * 15], F32) for i in range(NPAR)]
                ldbc = [aa(f"ldbc{i}", [128, 2], F32) for i in range(NPAR)]
                facm = [aa(f"facm{i}", [128, 2], F32) for i in range(NPAR)]
                rtmp = aa("rtmp", [128, 512], F32)
                qraw = aa("qraw", [128, 512], BF16)
                gtmp = aa("gtmp", [128, 512], F32)
                gtmp2 = aa("gtmp2", [128, 512], BF16)
                bvrep = aa("bvrep", [128, 128], F32)
                ebt = aa("ebt", [128, 256], F32)
                stat = aa("stat", [128, 36], F32)
                coef = aa("coef", [128, 2], F32)
                rdenA_t = aa("rdenA", [1, 256], F32)
                rdenB_t = aa("rdenB", [1, 256], F32)
                lnden_t = aa("lnden", [1, 256], F32)
                rdenA = rdenA_t[:]
                rdenB = rdenB_t[:]
                scl1 = aa("scl1", [1, 8], F32)

                def gen_group(g, par):
                    av_need = P.val("s_av")

                    def fw(gp, g=g, av_need=av_need, ld_chain=ld_chain):
                        if g == 0:
                            gp.wait_ge(SSC, ld_chain)
                        if av_need:
                            gp.wait_ge(SAV, av_need)
                        for wi, w in enumerate(("wq", "wk", "wv", "wg")):
                            for c in range(NCH):
                                gp.dma_start(
                                    out=WSL[:, wi * E + c * 128:wi * E + (c + 1) * 128],
                                    in_=wT_d[w][c * 128:(c + 1) * 128,
                                                g * 128:(g + 1) * 128]).then_inc(SW, 16)
                    P.add("gpsimd", fw)
                    w_done = P.inc("s_w", 16 * 32)
                    vc_prev = P.val("s_vc")

                    def fld(t, g=g, vc_prev=vc_prev, ld_chain=ld_chain):
                        t.wait_ge(SSC, ld_chain)
                        t.wait_ge(SVC, vc_prev)
                        for hh in range(2):
                            ins = t.matmul(psG2[:, hh:hh + 1], ONESF[:],
                                           ldrow[0:1, 2 * g + hh:2 * g + hh + 1],
                                           start=True, stop=True)
                            if hh == 1:
                                ins.then_inc(SMM, 1)
                    P.add("tensor", fld)
                    ldm = P.inc("s_mm")

                    def fldv(v, par=par, ldm=ldm):
                        v.wait_ge(SMM, ldm)
                        v.tensor_copy(ldbc[par][:], psG2[:, 0:2]).then_inc(SVC, 1)
                    P.add("vector", fldv)
                    ldv = P.inc("s_vc")

                    for hh in range(2):
                        sc_tab = P.val("s_sc")

                        def ft_v(v, par=par, hh=hh, ldv=ldv, sc_tab=sc_tab):
                            if hh == 0:
                                v.wait_ge(SVC, ldv)
                            else:
                                v.wait_ge(SSC, sc_tab)
                            v.tensor_scalar(BV[par][:, hh * 15:(hh + 1) * 15],
                                            KP[:], ldbc[par][:, hh:hh + 1], None,
                                            ALU.mult)
                            v.tensor_single_scalar(coef[:, 0:1],
                                                   ldbc[par][:, hh:hh + 1], -1.0,
                                                   ALU.mult)
                            v.drain()
                            v.tensor_scalar(tmp65[:], QREL[:], coef[0:65, 0:1],
                                            None, ALU.mult).then_inc(SVC, 1)
                        P.add("vector", ft_v)
                        tv = P.inc("s_vc")

                        def ft_s(sc, par=par, hh=hh, tv=tv):
                            sc.wait_ge(SVC, tv)
                            sc.activation(Bb[par][:, hh * 256:(hh + 1) * 256],
                                          tmp65[:], ACT.Exp, bias=0.0, scale=1.0)
                            sc.activation(facm[par][:, hh:hh + 1],
                                          ldbc[par][:, hh:hh + 1], ACT.Exp,
                                          bias=0.0,
                                          scale=128.0).then_inc(SSC, 1)
                        P.add("scalar", ft_s)
                        P.inc("s_sc")
                        for dd in range(2):
                            sc_need = P.val("s_sc")

                            def feb_v(v, par=par, hh=hh, dd=dd, sc_need=sc_need):
                                if dd > 0 or hh > 0:
                                    v.wait_ge(SSC, sc_need)
                                v.tensor_scalar(ebt[:], DTAB[:, dd * 256:(dd + 1) * 256],
                                                ldbc[par][:, hh:hh + 1], None,
                                                ALU.mult).then_inc(SVC, 1)
                            P.add("vector", feb_v)
                            ebv = P.inc("s_vc")

                            def feb_s(sc, par=par, hh=hh, dd=dd, ebv=ebv):
                                sc.wait_ge(SVC, ebv)
                                sc.activation(
                                    EB[par][:, (hh * 2 + dd) * 256:(hh * 2 + dd + 1) * 256],
                                    ebt[:], ACT.Exp, bias=0.0,
                                    scale=1.0).then_inc(SSC, 1)
                            P.add("scalar", feb_s)
                            P.inc("s_sc")

                    def fbv(t, g=g):
                        t.matmul(psG1[:, 0:128], ONESF[:],
                                 bvrow[0:1, g * 128:(g + 1) * 128],
                                 start=True, stop=True).then_inc(SMM, 1)
                    P.add("tensor", fbv)
                    bvm = P.inc("s_mm")

                    def fbvv(v, bvm=bvm, par=par):
                        v.wait_ge(SMM, bvm)
                        v.tensor_copy(bvrep[:], psG1[:, 0:128])
                        v.memset(VT[par][:], 1.0).then_inc(SVC, 1)
                    P.add("vector", fbvv)
                    bvv = P.inc("s_vc")

                    for wi, dstl in ((0, QT), (1, KT), (3, GT)):
                        for n in range(8):
                            ev_prev = P.val("s_vc")

                            def fp(t, wi=wi, n=n, w_done=w_done, h_done=h_done,
                                   ev_prev=ev_prev):
                                t.wait_ge(SW, w_done)
                                t.wait_ge(SVC, ev_prev)
                                for c in range(NCH):
                                    ins = t.matmul(
                                        psG1[:],
                                        WSL[:, wi * E + c * 128:wi * E + (c + 1) * 128],
                                        hT[:, c * T + n * 512:c * T + (n + 1) * 512],
                                        start=(c == 0), stop=(c == NCH - 1))
                                    if c == NCH - 1:
                                        ins.then_inc(SMM, 1)
                            P.add("tensor", fp)
                            pm = P.inc("s_mm")

                            if wi < 2:
                                bias = pvec["bq"] if wi == 0 else pvec["bk"]

                                def fev(v, g=g, pm=pm, bias=bias):
                                    v.wait_ge(SMM, pm)
                                    v.tensor_scalar(qraw[:], psG1[:],
                                                    bias[:, g:g + 1], None,
                                                    ALU.add).then_inc(SVC, 1)
                                P.add("vector", fev)
                                ev = P.inc("s_vc")

                                def frot(t, ev=ev):
                                    t.wait_ge(SVC, ev)
                                    t.matmul(psG2[:], ROTm[:], qraw[:],
                                             start=True, stop=True).then_inc(SMM, 1)
                                P.add("tensor", frot)
                                rm = P.inc("s_mm")

                                def fcomb(v, par=par, n=n, dstl=dstl, rm=rm):
                                    v.wait_ge(SMM, rm)
                                    dst = dstl[par][:, n * 512:(n + 1) * 512]
                                    v.tensor_tensor(dst, qraw[:],
                                                    COS[:, n * 512:(n + 1) * 512],
                                                    ALU.mult)
                                    v.tensor_tensor(rtmp[:], psG2[:],
                                                    SIN[:, n * 512:(n + 1) * 512],
                                                    ALU.mult)
                                    v.tensor_tensor(dst, dst, rtmp[:],
                                                    ALU.add).then_inc(SVC, 1)
                                P.add("vector", fcomb)
                                P.inc("s_vc")
                            else:
                                def fge(sc, g=g, pm=pm):
                                    sc.wait_ge(SMM, pm)
                                    sc.activation(gtmp2[:], psG1[:], ACT.Tanh,
                                                  bias=halfbg[:, g:g + 1],
                                                  scale=0.5).then_inc(SSC, 1)
                                P.add("scalar", fge)
                                ge = P.inc("s_sc")

                                def fgv(v, par=par, n=n, g=g, ge=ge):
                                    v.wait_ge(SSC, ge)
                                    v.tensor_scalar(gtmp[:], gtmp2[:], 0.5,
                                                    halfcol[:, 0:1], ALU.mult,
                                                    ALU.add)
                                    v.scalar_tensor_tensor(
                                        GT[par][:, n * 512:(n + 1) * 512],
                                        psG1[:], pvec["bg"][:, g:g + 1], gtmp[:],
                                        ALU.add, ALU.mult).then_inc(SVC, 1)
                                P.add("vector", fgv)
                                P.inc("s_vc")

                    for tt in range(NTT):
                        ev_prev = P.val("s_vc")

                        def fvm(t, tt=tt, w_done=w_done, ev_prev=ev_prev):
                            if tt == 0:
                                t.wait_ge(SW, w_done)
                            t.wait_ge(SVC, ev_prev)
                            for c in range(NCH):
                                ins = t.matmul(
                                    psG2[:, 0:128],
                                    hT[:, c * T + tt * 128:c * T + (tt + 1) * 128],
                                    WSL[:, 2 * E + c * 128:2 * E + (c + 1) * 128],
                                    start=(c == 0), stop=(c == NCH - 1))
                                if c == NCH - 1:
                                    ins.then_inc(SMM, 1)
                        P.add("tensor", fvm)
                        vm = P.inc("s_mm")

                        def fve(v, par=par, tt=tt, vm=vm, bvv=bvv):
                            v.wait_ge(SMM, vm)
                            if tt == 0:
                                v.wait_ge(SVC, bvv)
                            for hh in range(2):
                                slot = hh * 48 + tt
                                ins = v.tensor_tensor(
                                    VT[par][:, slot * 65:slot * 65 + 64],
                                    psG2[:, hh * 64:(hh + 1) * 64],
                                    bvrep[:, hh * 64:(hh + 1) * 64], ALU.add)
                                if hh == 1:
                                    ins.then_inc(SVC, 1)
                        P.add("vector", fve)
                        P.inc("s_vc")

                    tabs_done = P.val("s_sc")

                    def fvs(v, par=par, tabs_done=tabs_done):
                        v.wait_ge(SSC, tabs_done)
                        for hh in range(2):
                            for jo in range(16):
                                j = 2 * jo + 1
                                src = (hh * 48 + j) * 65
                                dstc = (hh * 48 + 32 + jo) * 65
                                ins = v.tensor_scalar(
                                    VT[par][:, dstc:dstc + 65],
                                    VT[par][:, src:src + 65],
                                    facm[par][:, hh:hh + 1],
                                    None, ALU.mult)
                                if hh == 1 and jo == 15:
                                    ins.then_inc(SVC, 1)
                    P.add("vector", fvs)
                    P.inc("s_vc")
                    return P.val("s_vc")

                def flash_group(g, par, gen_done):
                    evac_last = {0: 0, 1: 0}
                    first_av = {}
                    exp_done = {0: [], 1: []}
                    av_done = {0: [], 1: []}
                    pending_av = []
                    gb = 0
                    for qc in range(NQC):
                        for b in range(16):
                            j0 = 2 * b
                            kind = ("past" if b < qc else
                                    "band" if b == qc else "fut")
                            bank = gb % 2
                            sc_need = exp_done[0][gb - 1] if gb >= 1 else 0

                            def fS(t, par=par, qc=qc, j0=j0, bank=bank,
                                   sc_need=sc_need,
                                   first=(qc == 0 and b == 0),
                                   gen_done=gen_done):
                                if first:
                                    t.wait_ge(SVC, gen_done)
                                if sc_need:
                                    t.wait_ge(SSC, sc_need)
                                for m in range(2):
                                    j = j0 + m
                                    for hh in (0, 1):
                                        ps = psS[bank]
                                        cb = hh * 512 + m * 256
                                        for kh in range(2):
                                            ins = t.matmul(
                                                ps[kh * 64:(kh + 1) * 64,
                                                   cb:cb + 256],
                                                KT[par][hh * 64:(hh + 1) * 64,
                                                        j * 128 + kh * 64:
                                                        j * 128 + (kh + 1) * 64],
                                                QT[par][hh * 64:(hh + 1) * 64,
                                                        qc * 256:(qc + 1) * 256],
                                                start=True, stop=True,
                                                tile_position=(hh * 64, kh * 64))
                                            if m == 1 and hh == 1 and kh == 1:
                                                ins.then_inc(SMM, 1)
                            P.add("tensor", fS)
                            smm = P.inc("s_mm")

                            avn = av_done[1][gb - 1] if gb >= 1 else 0

                            def fexp(sc, par=par, qc=qc, b=b,
                                     bank=bank, kind=kind, smm=smm, avn=avn):
                                sc.wait_ge(SMM, smm)
                                if avn:
                                    sc.wait_ge(SAV, avn)
                                ubase = bank * 1024
                                if kind == "fut":
                                    s = b - qc
                                    for hh in range(2):
                                        ins = sc.activation(
                                            UBm[:, ubase + hh * 512:
                                                ubase + (hh + 1) * 512],
                                            psS[bank][:, hh * 512:(hh + 1) * 512],
                                            ACT.Exp,
                                            bias=BV[par][:, hh * 15 + s - 1:
                                                         hh * 15 + s],
                                            scale=0.125)
                                        if hh == 1:
                                            ins.then_inc(SSC, 1)
                                else:
                                    sc.activation(
                                        UBm[:, ubase:ubase + 1024],
                                        psS[bank][:], ACT.Exp, bias=0.0,
                                        scale=0.125).then_inc(SSC, 1)
                            P.add("scalar", fexp)
                            edone = P.inc("s_sc")
                            exp_done[0].append(edone)
                            exp_done[1].append(edone)

                            for hh, psO in ((0, psOA), (1, psOB)):
                                eb_done = 0
                                if kind == "band":
                                    def feb(v, par=par, hh=hh, bank=bank,
                                            edone=edone):
                                        v.wait_ge(SSC, edone)
                                        for m in range(2):
                                            ub = bank * 1024 + hh * 512 + m * 256
                                            u = UBm[:, ub:ub + 256]
                                            ins = v.tensor_tensor(
                                                u, u,
                                                EB[par][:, (hh * 2 + m) * 256:
                                                        (hh * 2 + m + 1) * 256],
                                                ALU.mult)
                                            if m == 1:
                                                ins.then_inc(SVC, 1)
                                    P.add("vector", feb)
                                    eb_done = P.inc("s_vc")

                                ev_need = evac_last[hh]

                                def fav(t, par=par, hh=hh, psO=psO, qc=qc, b=b,
                                        j0=j0, bank=bank, kind=kind, edone=edone,
                                        eb_done=eb_done, ev_need=ev_need,
                                        sp=first_av):
                                    t.wait_ge(SSC, edone)
                                    if eb_done:
                                        t.wait_ge(SVC, eb_done)
                                    if ev_need:
                                        t.wait_ge(SVC, ev_need)
                                    for m in range(2):
                                        j = j0 + m
                                        ub = bank * 1024 + hh * 512 + m * 256
                                        u = UBm[:, ub:ub + 256]
                                        if kind == "fut":
                                            slot = (hh * 48 + 32 + j // 2
                                                    if (j % 2) else hh * 48 + j)
                                            dst = psO[0:65, 256:512]
                                            fk = (hh, "f", qc)
                                        else:
                                            slot = hh * 48 + j
                                            dst = psO[0:65, 0:256]
                                            fk = (hh, "p", qc)
                                        st = fk not in sp
                                        sp[fk] = True
                                        ins = t.matmul(
                                            dst, VT[par][:, slot * 65:(slot + 1) * 65],
                                            u, start=st, stop=True)
                                        if m == 1:
                                            ins.then_inc(SAV, 1)
                                pending_av.append(fav)
                                av_done[hh].append(P.inc("s_av"))
                            while len(pending_av) > 2:
                                P.add("tensor", pending_av.pop(0))
                            gb += 1
                        while pending_av:
                            P.add("tensor", pending_av.pop(0))
                        avm = P.val("s_av")

                        for hh, psO, dens in ((0, psOA, DensA), (1, psOB, DensB)):
                            has_fut = qc < NQC - 1

                            def fevac(v, par=par, hh=hh, psO=psO, dens=dens,
                                      qc=qc, avm=avm, has_fut=has_fut):
                                v.wait_ge(SAV, avm)
                                num = Opack[hh * 64:(hh + 1) * 64,
                                            qc * 256:(qc + 1) * 256]
                                dn = dens[0:1, qc * 256:(qc + 1) * 256]
                                if has_fut:
                                    v.tensor_tensor(tmp65[:], psO[0:65, 256:512],
                                                    Bb[par][:, hh * 256:(hh + 1) * 256],
                                                    ALU.mult)
                                    v.tensor_tensor(num, tmp65[0:64, :],
                                                    psO[0:64, 0:256], ALU.add)
                                    v.tensor_tensor(dn, tmp65[64:65, :],
                                                    psO[64:65, 0:256],
                                                    ALU.add).then_inc(SVC, 1)
                                else:
                                    v.tensor_copy(num, psO[0:64, 0:256])
                                    v.tensor_copy(dn, psO[64:65, 0:256]).then_inc(SVC, 1)
                            P.add("vector", fevac)
                            evac_last[hh] = P.inc("s_vc")
                    return max(evac_last.values())

                def division_group(g, par, evac_done):
                    def fz0(v, evac_done=evac_done):
                        v.wait_ge(SVC, evac_done)
                        v.memset(stat[:], 0.0).then_inc(SVC, 1)
                    P.add("vector", fz0)
                    P.inc("s_vc")
                    for ch in range(16):
                        bc_prev = P.val("s_mm")

                        def fsrd(sc, ch=ch, evac_done=evac_done, bc_prev=bc_prev):
                            if ch == 0:
                                sc.wait_ge(SVC, evac_done)
                            else:
                                sc.wait_ge(SMM, bc_prev)
                            lntmp_r = lnden_t[:]
                            sc.activation(lntmp_r, DensA[0:1, ch * 256:(ch + 1) * 256],
                                          ACT.Ln, bias=0.0, scale=1.0)
                            sc.drain()
                            sc.activation(rdenA, lntmp_r, ACT.Exp, bias=0.0,
                                          scale=-1.0)
                            sc.drain()
                            sc.activation(lntmp_r, DensB[0:1, ch * 256:(ch + 1) * 256],
                                          ACT.Ln, bias=0.0, scale=1.0)
                            sc.drain()
                            sc.activation(rdenB, lntmp_r, ACT.Exp, bias=0.0,
                                          scale=-1.0).then_inc(SSC, 1)
                        P.add("scalar", fsrd)
                        rdv = P.inc("s_sc")

                        od_prev = P.val("s_vc")

                        def fbc(t, rdv=rdv, od_prev=od_prev):
                            t.wait_ge(SSC, rdv)
                            t.wait_ge(SVC, od_prev)
                            t.matmul(psG2[0:64, 0:256], ONESF[0:1, 0:64],
                                     rdenA, start=True, stop=True)
                            t.matmul(psG2[64:128, 0:256], ONESF[0:1, 0:64],
                                     rdenB, start=True, stop=True,
                                     tile_position=(0, 64)).then_inc(SMM, 1)
                        P.add("tensor", fbc)
                        bcm = P.inc("s_mm")

                        def fod(v, ch=ch, bcm=bcm):
                            v.wait_ge(SMM, bcm)
                            sl = Opack[:, ch * 256:(ch + 1) * 256]
                            v.tensor_tensor(sl, sl, psG2[:, 0:256], ALU.mult)
                            v.reduce_sum(stat[:, 2 + 2 * ch:3 + 2 * ch], sl, axis=AX)
                            v.tensor_tensor(rtmp[:, 0:256], sl, sl, ALU.mult)
                            v.reduce_sum(stat[:, 3 + 2 * ch:4 + 2 * ch],
                                         rtmp[:, 0:256], axis=AX).then_inc(SVC, 1)
                        P.add("vector", fod)
                        P.inc("s_vc")

                    def fstat(v):
                        v.drain()
                        v.reduce_sum(stat[:, 0:1], stat[:, 2:34:2], axis=AX)
                        v.reduce_sum(stat[:, 1:2], stat[:, 3:35:2],
                                     axis=AX).then_inc(SVC, 1)
                    P.add("vector", fstat)
                    p1done = P.inc("s_vc")

                    def fsm(t, p1done=p1done):
                        t.wait_ge(SVC, p1done)
                        t.matmul(psG2[0:1, 0:2], onescol[0:64, 0:1],
                                 stat[0:64, 0:2], start=True, stop=True)
                        t.matmul(psG2[0:1, 2:4], onescol[64:128, 0:1],
                                 stat[64:128, 0:2], start=True, stop=True,
                                 tile_position=(64, 0)).then_inc(SMM, 1)
                    P.add("tensor", fsm)
                    smm2 = P.inc("s_mm")

                    def fmu(v, smm2=smm2):
                        v.wait_ge(SMM, smm2)
                        v.tensor_single_scalar(scl1[0:1, 0:1], psG2[0:1, 0:1],
                                               1.0 / (T * HD), ALU.mult)
                        v.tensor_single_scalar(scl1[0:1, 2:3], psG2[0:1, 1:2],
                                               1.0 / (T * HD), ALU.mult)
                        v.tensor_single_scalar(scl1[0:1, 1:2], psG2[0:1, 2:3],
                                               1.0 / (T * HD), ALU.mult)
                        v.tensor_single_scalar(scl1[0:1, 3:4], psG2[0:1, 3:4],
                                               1.0 / (T * HD), ALU.mult)
                        v.drain()
                        v.tensor_tensor(scl1[0:1, 4:5], scl1[0:1, 0:1],
                                        scl1[0:1, 0:1], ALU.mult)
                        v.tensor_tensor(scl1[0:1, 5:6], scl1[0:1, 1:2],
                                        scl1[0:1, 1:2], ALU.mult)
                        v.drain()
                        v.tensor_tensor(scl1[0:1, 4:5], scl1[0:1, 2:3],
                                        scl1[0:1, 4:5], ALU.subtract)
                        v.tensor_tensor(scl1[0:1, 5:6], scl1[0:1, 3:4],
                                        scl1[0:1, 5:6], ALU.subtract)
                        v.drain()
                        v.nop().then_inc(SVC, 1)
                    P.add("vector", fmu)
                    muv = P.inc("s_vc")

                    def frs2(sc, muv=muv):
                        sc.wait_ge(SVC, muv)
                        sc.activation(scl1[0:1, 6:7], scl1[0:1, 4:5], ACT.Ln,
                                      bias=epscol[0:1, 0:1], scale=1.0)
                        sc.activation(scl1[0:1, 7:8], scl1[0:1, 5:6], ACT.Ln,
                                      bias=epscol[0:1, 0:1], scale=1.0)
                        sc.drain()
                        sc.activation(scl1[0:1, 6:7], scl1[0:1, 6:7], ACT.Exp,
                                      bias=0.0, scale=-0.5)
                        sc.activation(scl1[0:1, 7:8], scl1[0:1, 7:8], ACT.Exp,
                                      bias=0.0, scale=-0.5).then_inc(SSC, 1)
                    P.add("scalar", frs2)
                    rsv = P.inc("s_sc")

                    def fcb(t, rsv=rsv):
                        t.wait_ge(SSC, rsv)
                        t.matmul(psG2[0:64, 0:1], ONESF[0:1, 0:64],
                                 scl1[0:1, 0:1], start=True, stop=True,
                                 tile_position=(0, 0))
                        t.matmul(psG2[64:128, 0:1], ONESF[0:1, 0:64],
                                 scl1[0:1, 1:2], start=True, stop=True,
                                 tile_position=(0, 64))
                        t.matmul(psG2[0:64, 1:2], ONESF[0:1, 0:64],
                                 scl1[0:1, 6:7], start=True, stop=True,
                                 tile_position=(0, 0))
                        t.matmul(psG2[64:128, 1:2], ONESF[0:1, 0:64],
                                 scl1[0:1, 7:8], start=True, stop=True,
                                 tile_position=(0, 64)).then_inc(SMM, 1)
                    P.add("tensor", fcb)
                    cbm = P.inc("s_mm")

                    def fcoef(v, g=g, cbm=cbm):
                        v.wait_ge(SMM, cbm)
                        v.tensor_tensor(coef[:, 0:1], pvec["gamma"][:, g:g + 1],
                                        psG2[:, 1:2], ALU.mult)
                        v.drain()
                        v.tensor_tensor(coef[:, 1:2], psG2[:, 0:1], coef[:, 0:1],
                                        ALU.mult)
                        v.drain()
                        v.tensor_tensor(coef[:, 1:2], pvec["beta"][:, g:g + 1],
                                        coef[:, 1:2], ALU.subtract)
                        v.drain()
                        v.nop().then_inc(SVC, 1)
                    P.add("vector", fcoef)
                    cov = P.inc("s_vc")

                    for ch in range(8):
                        gp = ch % 2
                        scr_need = P.val("s_scr") if ch >= 2 else 0

                        def fg2(v, g=g, par=par, ch=ch, gp=gp, cov=cov,
                                scr_need=scr_need):
                            if ch == 0:
                                v.wait_ge(SVC, cov)
                            if scr_need:
                                v.wait_ge(SSCR, scr_need)
                            sl = Opack[:, ch * 512:(ch + 1) * 512]
                            v.tensor_scalar(rtmp[:], sl, coef[:, 0:1],
                                            coef[:, 1:2], ALU.mult, ALU.add)
                            v.tensor_tensor(g2b[:, gp * 512:(gp + 1) * 512],
                                            rtmp[:],
                                            GT[par][:, ch * 512:(ch + 1) * 512],
                                            ALU.mult).then_inc(SVC, 1)
                        P.add("vector", fg2)
                        g2v = P.inc("s_vc")

                        def fsc(sy, g=g, ch=ch, gp=gp, g2v=g2v):
                            sy.wait_ge(SVC, g2v)
                            sy.dma_start(out=g2s[g * 128:(g + 1) * 128,
                                                 ch * 512:(ch + 1) * 512],
                                         in_=g2b[:, gp * 512:(gp + 1) * 512]
                                         ).then_inc(SSCR, 16)
                        P.add("sync", fsc)
                        P.inc("s_scr", 16)

                groups = range(ngroups) if upto != "p1" else range(0)
                for g in groups:
                    par = 0
                    gen_done = gen_group(g, par)
                    if g == 0 and "dbg2" in taps:
                        t_wsl = tap("WSL", [128, 4 * E], BF16)
                        t_ldbc = tap("ldbc", [128, 2], F32)
                        t_bb = tap("Bb", [65, 2 * 256], F32)
                        t_bv = tap("BVt", [128, 2 * 15], F32)
                        t_eb = tap("EBt", [128, 2 * 2 * 256], BF16)

                        def ftapg(sy, n=gen_done, t_wsl=t_wsl, t_ldbc=t_ldbc,
                                  t_bb=t_bb, t_bv=t_bv, t_eb=t_eb):
                            sy.wait_ge(SVC, n)
                            sy.dma_start(out=t_wsl, in_=WSL[:]).then_inc(SOUT, 16)
                            sy.dma_start(out=t_ldbc, in_=ldbc[0][:]).then_inc(SOUT, 16)
                            sy.dma_start(out=t_bb, in_=Bb[0][:]).then_inc(SOUT, 16)
                            sy.dma_start(out=t_bv, in_=BV[0][:]).then_inc(SOUT, 16)
                            sy.dma_start(out=t_eb, in_=EB[0][:]).then_inc(SOUT, 16)
                        P.add("sync", ftapg)
                        P.inc("s_out", 80)
                    ev_done = flash_group(g, par, gen_done)
                    if g == 0 and "Opre" in taps:
                        t_opre = tap("Opre", [128, T], BF16)
                        t_dpre = tap("Dpre", [2, T], F32)

                        def ftappre(sy, n=ev_done, t_opre=t_opre, t_dpre=t_dpre):
                            sy.wait_ge(SVC, n)
                            sy.dma_start(out=t_opre, in_=Opack[:]).then_inc(SOUT, 16)
                            sy.dma_start(out=t_dpre[0:1, :], in_=DensA[:]).then_inc(SOUT, 16)
                            sy.dma_start(out=t_dpre[1:2, :], in_=DensB[:]).then_inc(SOUT, 16)
                        P.add("sync", ftappre)
                        tapn = P.inc("s_out", 48)

                        def fgate(v, tapn=tapn):
                            v.wait_ge(SOUT, tapn)
                        P.add("vector", fgate)
                        P.add("scalar", lambda sc, tapn=tapn: sc.wait_ge(SOUT, tapn))
                    division_group(g, par, ev_done)

                    if g == 0:
                        for nm, buf in (("QT", QT[0]), ("KT", KT[0]),
                                        ("GT", GT[0]), ("VT", VT[0])):
                            if nm in taps:
                                shp = [128, T] if nm != "VT" else [128, 2 * 48 * 65]
                                tpp = tap(nm, shp, BF16)

                                def ftap(sy, tpp=tpp, buf=buf, n=gen_done):
                                    sy.wait_ge(SVC, n)
                                    sy.dma_start(out=tpp, in_=buf[:]).then_inc(SOUT, 16)
                                P.add("sync", ftap)
                                P.inc("s_out", 16)
                        if "Opack" in taps:
                            tpo = tap("Opack", [128, T], BF16)
                            tpd = tap("Dens", [2, T], F32)
                            nvc = P.val("s_vc")

                            def ftap(sy, n=nvc, tpo=tpo, tpd=tpd):
                                sy.wait_ge(SVC, n)
                                sy.dma_start(out=tpo, in_=Opack[:]).then_inc(SOUT, 16)
                                sy.dma_start(out=tpd[0:1, :], in_=DensA[:]).then_inc(SOUT, 16)
                                sy.dma_start(out=tpd[1:2, :], in_=DensB[:]).then_inc(SOUT, 16)
                            P.add("sync", ftap)
                            P.inc("s_out", 48)
                g2_all = P.val("s_scr")

            with contextlib.ExitStack() as wst:
                wa = lambda n, sh, dt: wst.enter_context(nc.sbuf_tensor("sb_" + n, sh, dt))
                woT = wa("woT", [128, NCH * E], BF16)
                g2rd = wa("g2rd", [128, 2 * NCH * 512], BF16)
                attb = wa("attb", [128, 2 * 512], BF16)
                attT = wa("attT", [128, 2 * 4096], BF16)
                xfw = wa("xfw", [128, 2 * E], F32)
                x1f = wa("x1f", [128, E], F32)
                xn2 = wa("xn2", [128, E], BF16)
                x1bf = wa("x1bf", [128, 2 * E], BF16)
                h2blk = wa("h2blk", [128, 2 * E], BF16)
                lnw = wa("lnw", [128, 8], F32)
                lntw = wa("lntw", [128, E], F32)

                do_wo = upto in ("wo", "full")
                if do_wo:
                    vcg = P.val("s_vc")

                    def fdummy(t, vcg=vcg):
                        t.wait_ge(SVC, vcg)
                        t.matmul(psG1[0:1, 0:1], ONESF[0:1, 0:1], ONESF[0:1, 0:1],
                                 start=True, stop=True).then_inc(SMM, 1)
                    P.add("tensor", fdummy)
                    P.inc("s_mm")

                    def fwo_w(gp, n=g2_all):
                        gp.wait_ge(SSCR, n)
                        gp.dma_start(out=woT[:].rearrange("p (c j) -> p c j", c=NCH),
                                     in_=wT_d["wo"].rearrange("(c p) j -> p c j",
                                                              p=128)).then_inc(SW, 16)
                    P.add("gpsimd", fwo_w)
                    wo_w = P.inc("s_w", 16)

                for tc in (range(NCH) if do_wo else range(0)):
                    gpar = tc % 2
                    g2need = P.val("s_mm") if tc >= 2 else 0

                    def fg2r(gp, tc=tc, gpar=gpar, g2need=g2need, n=g2_all):
                        if tc == 0:
                            gp.wait_ge(SSCR, n)
                        if g2need:
                            gp.wait_ge(SMM, g2need)
                        gp.dma_start(
                            out=g2rd[:, gpar * NCH * 512:(gpar + 1) * NCH * 512
                                     ].rearrange("p (c t) -> p c t", c=NCH),
                            in_=g2s[:, tc * 512:(tc + 1) * 512].rearrange(
                                "(c p) t -> p c t", p=128)).then_inc(SW, 16)
                    P.add("gpsimd", fg2r)
                    g2r = P.inc("s_w", 16)

                    if tc == 0 and "dbg3" in taps:
                        t_wo = tap("woTv", [128, NCH * E], BF16)
                        t_g2r = tap("g2rdv", [128, NCH * 512], BF16)

                        def ftap3(sy, g2r=g2r, wo_w=wo_w, t_wo=t_wo, t_g2r=t_g2r):
                            sy.wait_ge(SW, max(g2r, wo_w))
                            sy.dma_start(out=t_wo, in_=woT[:]).then_inc(SOUT, 16)
                            sy.dma_start(out=t_g2r, in_=g2rd[:, 0:NCH * 512]).then_inc(SOUT, 16)
                        P.add("sync", ftap3)
                        P.inc("s_out", 32)

                    for dt in range(NCH):
                        apar = dt % 2
                        vc_prev = P.val("s_vc")

                        def fwm(t, tc=tc, dt=dt, gpar=gpar, g2r=g2r, wo_w=wo_w,
                                vc_prev=vc_prev):
                            t.wait_ge(SW, max(g2r, wo_w))
                            t.wait_ge(SVC, vc_prev)
                            dstp = psG1[:] if dt % 2 == 0 else psG2[:]
                            for c in range(NCH):
                                ins = t.matmul(
                                    dstp,
                                    woT[:, c * E + dt * 128:c * E + (dt + 1) * 128],
                                    g2rd[:, gpar * NCH * 512 + c * 512:
                                         gpar * NCH * 512 + (c + 1) * 512],
                                    start=(c == 0), stop=(c == NCH - 1))
                                if c == NCH - 1:
                                    ins.then_inc(SMM, 1)
                        P.add("tensor", fwm)
                        wm = P.inc("s_mm")

                        tp_need = P.val("s_tp") if (tc * NCH + dt) >= 2 else 0

                        def fae(v, dt=dt, apar=apar, wm=wm, tp_need=tp_need):
                            v.wait_ge(SMM, wm)
                            if tp_need:
                                v.wait_ge(STP, tp_need)
                            dstp = psG1[:] if dt % 2 == 0 else psG2[:]
                            v.tensor_scalar(attb[:, apar * 512:(apar + 1) * 512],
                                            dstp, pvec["bo"][:, dt:dt + 1],
                                            gacl[:, dt:dt + 1], ALU.add,
                                            ALU.mult).then_inc(SVC, 1)
                        P.add("vector", fae)
                        ae = P.inc("s_vc")

                        if tc == 0 and dt == 0 and "dbg5" in taps:
                            t_ab = tap("attbv", [128, 512], BF16)

                            def ftap5(sy, ae=ae, t_ab=t_ab):
                                sy.wait_ge(SVC, ae)
                                sy.dma_start(out=t_ab, in_=attb[:, 0:512]).then_inc(SOUT, 16)
                            P.add("sync", ftap5)
                            P.inc("s_out", 16)

                        def ftr(sq, tc=tc, dt=dt, apar=apar, ae=ae):
                            sq.wait_ge(SVC, ae)
                            for ti in range(4):
                                sq.dma_start_transpose(
                                    attT[:, (tc % 2) * 4096 + ti * 1024 + dt * 128:
                                         (tc % 2) * 4096 + ti * 1024 + (dt + 1) * 128],
                                    attb[:, apar * 512 + ti * 128:
                                         apar * 512 + (ti + 1) * 128]
                                ).then_inc(STP, 16)
                        P.add("sync", ftr)
                        P.inc("s_tp", 64)
                    tr_all = P.val("s_tp")

                    if tc == 0 and "dbg4" in taps:
                        t_att = tap("attTv", [128, 4096], BF16)

                        def ftap4(sy, tr_all=tr_all, t_att=t_att):
                            sy.wait_ge(STP, tr_all)
                            sy.dma_start(out=t_att, in_=attT[:, 0:4096]).then_inc(SOUT, 16)
                        P.add("sync", ftap4)
                        P.inc("s_out", 16)

                    for ti in range(4):
                        tt = tc * 4 + ti
                        xpar = tt % 2
                        vc_prev = P.val("s_vc") if tt >= 2 else 0

                        def fxl(sy, tt=tt, xpar=xpar, vc_prev=vc_prev):
                            if vc_prev:
                                sy.wait_ge(SVC, vc_prev)
                            sy.dma_start(out=xfw[:, xpar * E:(xpar + 1) * E],
                                         in_=x_d[tt * 128:(tt + 1) * 128, :]
                                         ).then_inc(SX, 16)
                        P.add("sync", fxl)
                        xd = P.inc("s_x", 16)

                        def fx1(v, tc=tc, ti=ti, xpar=xpar, xd=xd, tr_all=tr_all):
                            v.wait_ge(SX, xd)
                            v.wait_ge(STP, tr_all)
                            v.tensor_tensor(x1f[:], xfw[:, xpar * E:(xpar + 1) * E],
                                            attT[:, (tc % 2) * 4096 + ti * 1024:
                                                 (tc % 2) * 4096 + (ti + 1) * 1024],
                                            ALU.add)
                            v.reduce_sum(lnw[:, 0:1], x1f[:], axis=AX)
                            v.tensor_tensor(lntw[:], x1f[:], x1f[:], ALU.mult)
                            v.reduce_sum(lnw[:, 1:2], lntw[:],
                                         axis=AX).then_inc(SVC, 1)
                        P.add("vector", fx1)
                        stv = P.inc("s_vc")

                        def fr2a(sc, stv=stv):
                            sc.wait_ge(SVC, stv)
                            sc.activation(lnw[:, 5:6], lnw[:, 0:1], ACT.Copy,
                                          bias=0.0, scale=1.0 / E)
                            sc.activation(lnw[:, 2:3], lnw[:, 0:1], ACT.Square,
                                          bias=0.0, scale=1.0 / E).then_inc(SSC, 1)
                        P.add("scalar", fr2a)
                        r2a = P.inc("s_sc")

                        def fem2(v, r2a=r2a):
                            v.wait_ge(SSC, r2a)
                            v.tensor_scalar(lnw[:, 3:4], lnw[:, 2:3], -1.0,
                                            epscol[:, 0:1], ALU.mult,
                                            ALU.add).then_inc(SVC, 1)
                        P.add("vector", fem2)
                        em2 = P.inc("s_vc")

                        def fr2b(sc, em2=em2):
                            sc.wait_ge(SVC, em2)
                            sc.activation(lnw[:, 3:4], lnw[:, 1:2], ACT.Ln,
                                          bias=lnw[:, 3:4], scale=1.0 / E)
                            sc.drain()
                            sc.activation(lnw[:, 4:5], lnw[:, 3:4], ACT.Exp,
                                          bias=0.0, scale=-0.5).then_inc(SSC, 1)
                        P.add("scalar", fr2b)
                        rs2 = P.inc("s_sc")
                        tp_prev = P.val("s_tp")

                        scr_rd = P.val("s_scr") if tt >= 2 else 0

                        def fn2(v, rs2=rs2, tp_prev=tp_prev, tt=tt, scr_rd=scr_rd):
                            v.wait_ge(SSC, rs2)
                            if tt >= 2:
                                v.wait_ge(STP, tp_prev)
                            if scr_rd:
                                v.wait_ge(SSCR, scr_rd)
                            v.tensor_scalar(xn2[:], x1f[:], lnw[:, 5:6],
                                            lnw[:, 4:5], ALU.subtract, ALU.mult)
                            v.tensor_copy(x1bf[:, (tt % 2) * E:(tt % 2 + 1) * E], x1f[:]).then_inc(SVC, 1)
                        P.add("vector", fn2)
                        n2 = P.inc("s_vc")

                        bpar = tt % 2
                        scr_prev = P.val("s_scr") if tt >= 2 else 0

                        def ftp2(sq, bpar=bpar, n2=n2, scr_prev=scr_prev):
                            sq.wait_ge(SVC, n2)
                            if scr_prev:
                                sq.wait_ge(SSCR, scr_prev)
                            for c in range(NCH):
                                sq.dma_start_transpose(
                                    h2blk[:, bpar * E + c * 128:bpar * E + (c + 1) * 128],
                                    xn2[:, c * 128:(c + 1) * 128]).then_inc(STP, 16)
                        P.add("scalar", ftp2)
                        tp2 = P.inc("s_tp", 16 * 8)

                        def faff(v, bpar=bpar, tp2=tp2):
                            v.wait_ge(STP, tp2)
                            for c in range(NCH):
                                ins = v.tensor_scalar(
                                    h2blk[:, bpar * E + c * 128:bpar * E + (c + 1) * 128],
                                    h2blk[:, bpar * E + c * 128:bpar * E + (c + 1) * 128],
                                    cbp1[:, c:c + 1], mcol(3, c), ALU.mult, ALU.add)
                                if c == NCH - 1:
                                    ins.then_inc(SVC, 1)
                        P.add("vector", faff)
                        aff = P.inc("s_vc")

                        def fsw(sy, tt=tt, bpar=bpar, aff=aff):
                            sy.wait_ge(SVC, aff)
                            sy.dma_start(
                                out=h2s[:, tt * 128:(tt + 1) * 128].rearrange(
                                    "(c p) t -> p c t", p=128),
                                in_=h2blk[:, bpar * E:(bpar + 1) * E].rearrange(
                                    "p (c t) -> p c t", c=NCH)).then_inc(SSCR, 16)
                            sy.dma_start(
                                out=x1s[tt * 128:(tt + 1) * 128, :],
                                in_=x1bf[:, (tt % 2) * E:(tt % 2 + 1) * E]).then_inc(SSCR, 16)
                        P.add("sync", fsw)
                        P.inc("s_scr", 32)
                if "dbg6" in taps:
                    t_wo2 = tap("woTv2", [128, NCH * E], BF16)

                    def ftap6(sy, n=P.val("s_scr"), t_wo2=t_wo2):
                        sy.wait_ge(SSCR, n)
                        sy.dma_start(out=t_wo2, in_=woT[:]).then_inc(SOUT, 16)
                    P.add("sync", ftap6)
                    P.inc("s_out", 16)
                wo_all = P.val("s_scr")

        with contextlib.ExitStack() as fst:
            fa_ = lambda n, sh, dt: fst.enter_context(nc.sbuf_tensor("sb_" + n, sh, dt))
            WFC = fa_("WFC", [128, NCH * FF], BF16)
            WPR = fa_("WPR", [128, 32 * E], BF16)
            H2R = fa_("H2R", [128, NCH * 512], BF16)
            A2 = fa_("A2", [128, 32 * 512], BF16)
            RL = fa_("RL", [128, 512], BF16)
            YB = fa_("YB", [128, 512], F32)
            YH = fa_("YH", [128, 512], F32)
            SG = [fa_(f"SG{i}", [128, 512], F32) for i in range(2)]
            ACCB = fa_("ACCB", [128, 512], F32)
            SNT = fa_("SNT", [128, 4096], BF16)
            SNB = RL
            GBR = fa_("GBR", [128, E], F32)
            X1T = fa_("X1T", [128, E], BF16)
            OUTB = fa_("OUTB", [128, E], F32)

            do_ffn = upto == "full"

            def ffw(gp, n=wo_all):
                gp.wait_ge(SSCR, n)
                gp.dma_start(out=WFC[:].rearrange("p (c f) -> p c f", c=NCH),
                             in_=wfcT_d.rearrange("(c p) f -> p c f", p=128)
                             ).then_inc(SW, 16)
                gp.dma_start(out=WPR[:].rearrange("p (f d) -> p f d", f=32),
                             in_=wprT_d.rearrange("(f p) d -> p f d", p=128)
                             ).then_inc(SW, 16)
            if do_ffn:
                P.add("gpsimd", ffw)
                ff_w = P.inc("s_w", 32)
            else:
                ff_w = 0

            vc_gate = P.val("s_vc")

            def fgb(t, n=wo_all, vc_gate=vc_gate):
                t.wait_ge(SVC, vc_gate)
                t.matmul(psG1[:], ONESF[:], gbrow[0:1, 0:512],
                         start=True, stop=True)
                t.matmul(psG2[:], ONESF[:], gbrow[0:1, 512:1024],
                         start=True, stop=True).then_inc(SMM, 1)
            if do_ffn:
                P.add("tensor", fgb)
                gbm = P.inc("s_mm")

                def fgbv(v, gbm=gbm):
                    v.wait_ge(SMM, gbm)
                    v.tensor_copy(GBR[:, 0:512], psG1[:])
                    v.tensor_copy(GBR[:, 512:1024], psG2[:]).then_inc(SVC, 1)
                P.add("vector", fgbv)
                P.inc("s_vc")
            else:
                vall = P.val("s_vc")
                sall = P.val("s_sc")
                mall = P.val("s_mm")
                scrall = P.val("s_scr")

                def fdz(v, vall=vall):
                    v.wait_ge(SVC, vall)
                    v.memset(OUTB[:], 0.0).then_inc(SVC, 1)
                P.add("vector", fdz)
                dz = P.inc("s_vc")

                def fdy(sy, dz=dz, sall=sall, mall=mall, scrall=scrall):
                    sy.wait_ge(SVC, dz)
                    sy.wait_ge(SSC, sall)
                    sy.wait_ge(SMM, mall)
                    if scrall:
                        sy.wait_ge(SSCR, scrall)
                    for tt in range(NTT):
                        sy.dma_start(out=y_d[tt * 128:(tt + 1) * 128, :],
                                     in_=OUTB[:]).then_inc(SOUT, 16)
                P.add("sync", fdy)
                P.inc("s_out", 16 * NTT)

            for tc in (range(NCH) if do_ffn else range(0)):
                hpar = 0
                h2need = P.val("s_mm") if tc >= 1 else 0

                def fh2(gp, tc=tc, hpar=hpar, h2need=h2need, n=wo_all):
                    if tc == 0:
                        gp.wait_ge(SSCR, n)
                    if h2need:
                        gp.wait_ge(SMM, h2need)
                    gp.dma_start(
                        out=H2R[:, hpar * NCH * 512:(hpar + 1) * NCH * 512
                                ].rearrange("p (c t) -> p c t", c=NCH),
                        in_=h2s[:, tc * 512:(tc + 1) * 512].rearrange(
                            "(c p) t -> p c t", p=128)).then_inc(SW, 16)
                P.add("gpsimd", fh2)
                h2r = P.inc("s_w", 16)

                for ft in range(32):
                    fpar = ft % 4
                    sc_prev = P.val("s_sc") if ft >= 4 else 0
                    ymm_prev = P.val("s_mm") if (tc >= 1 and ft == 0) else 0

                    def ffc(t, tc=tc, ft=ft, fpar=fpar, h2r=h2r, ff_w=ff_w,
                            hpar=hpar, sc_prev=sc_prev):
                        if ft == 0:
                            t.wait_ge(SW, max(h2r, ff_w))
                        if sc_prev:
                            t.wait_ge(SSC, sc_prev)
                        dstp = (psA0, psA1, psB0, psB1)[fpar]
                        for c in range(NCH):
                            ins = t.matmul(
                                dstp,
                                WFC[:, c * FF + ft * 128:c * FF + (ft + 1) * 128],
                                H2R[:, hpar * NCH * 512 + c * 512:
                                     hpar * NCH * 512 + (c + 1) * 512],
                                start=(c == 0), stop=(c == NCH - 1))
                            if c == NCH - 1:
                                ins.then_inc(SMM, 1)
                    P.add("tensor", ffc)
                    fmm = P.inc("s_mm")
                    vc_prev = P.val("s_vc") if ft >= 1 else 0

                    tp_need = P.val("s_tp") if (tc >= 1 and ft == 0) else 0

                    def frl(sc, fpar=fpar, fmm=fmm, vc_prev=vc_prev, tp_need=tp_need):
                        sc.wait_ge(SMM, fmm)
                        if vc_prev:
                            sc.wait_ge(SVC, vc_prev)
                        if tp_need:
                            sc.wait_ge(STP, tp_need)
                        dstp = (psA0, psA1, psB0, psB1)[fpar]
                        sc.activation(RL[:], dstp, ACT.Relu, bias=0.0,
                                      scale=1.0).then_inc(SSC, 1)
                    P.add("scalar", frl)
                    rl = P.inc("s_sc")
                    ym_prev = P.val("s_mm") if tc >= 1 and ft < 8 else 0

                    def fsq(v, ft=ft, rl=rl, ym_prev=ym_prev):
                        v.wait_ge(SSC, rl)
                        if ym_prev:
                            v.wait_ge(SMM, ym_prev)
                        v.tensor_tensor(A2[:, ft * 512:(ft + 1) * 512], RL[:],
                                        RL[:], ALU.mult).then_inc(SVC, 1)
                    P.add("vector", fsq)
                    P.inc("s_vc")
                a2done = P.val("s_vc")

                for dt in range(NCH):
                    opar = dt % 2
                    prevy = P.val("s_vc")

                    def fy2(t, dt=dt, a2done=a2done, prevy=prevy):
                        t.wait_ge(SVC, max(a2done, prevy))
                        for ft in range(32):
                            ins = t.matmul(
                                psG1[:],
                                WPR[:, ft * E + dt * 128:ft * E + (dt + 1) * 128],
                                A2[:, ft * 512:(ft + 1) * 512],
                                start=(ft == 0), stop=(ft == 31))
                            if ft == 31:
                                ins.then_inc(SMM, 1)
                    P.add("tensor", fy2)
                    ym = P.inc("s_mm")

                    def fs0(sc, ym=ym):
                        sc.wait_ge(SMM, ym)
                        sc.activation(SG[0][:], psG1[:], ACT.Sign,
                                      bias=negones[:, 0:1],
                                      scale=1.0).then_inc(SSC, 1)
                    P.add("scalar", fs0)
                    s0 = P.inc("s_sc")

                    def fv0(v, s0=s0):
                        v.wait_ge(SSC, s0)
                        v.tensor_copy(YB[:], psG1[:])
                        v.tensor_single_scalar(YH[:], YB[:], 0.5, ALU.subtract)
                        v.tensor_copy(ACCB[:], SG[0][:]).then_inc(SVC, 1)
                    P.add("vector", fv0)
                    P.inc("s_vc")

                    for st in range(4):
                        sprev = st % 2
                        snew = (st + 1) % 2

                        def fms(v, sprev=sprev):
                            v.scalar_tensor_tensor(YB[:], YB[:], 0.9, YH[:],
                                                   ALU.mult, ALU.add)
                            v.scalar_tensor_tensor(YB[:], SG[sprev][:], -0.5,
                                                   YB[:], ALU.mult,
                                                   ALU.add).then_inc(SVC, 1)
                        P.add("vector", fms)
                        mv = P.inc("s_vc")

                        def fss(sc, snew=snew, mv=mv):
                            sc.wait_ge(SVC, mv)
                            sc.activation(SG[snew][:], YB[:], ACT.Sign,
                                          bias=negones[:, 0:1],
                                          scale=1.0).then_inc(SSC, 1)
                        P.add("scalar", fss)
                        ss = P.inc("s_sc")

                        def facc(v, snew=snew, ss=ss):
                            v.wait_ge(SSC, ss)
                            v.tensor_tensor(ACCB[:], ACCB[:], SG[snew][:],
                                            ALU.add).then_inc(SVC, 1)
                        P.add("vector", facc)
                        P.inc("s_vc")
                    out_prev = P.val("s_out") if tc >= 1 and dt == 0 else 0

                    def fsnb(v, dt=dt, out_prev=out_prev):
                        v.tensor_scalar(SNB[:], ACCB[:], 0.1, halfcol[:, 0:1],
                                        ALU.mult, ALU.add).then_inc(SVC, 1)
                    P.add("vector", fsnb)
                    snb = P.inc("s_vc")

                    def ftsn(sq, dt=dt, snb=snb, out_prev=out_prev):
                        sq.wait_ge(SVC, snb)
                        if out_prev:
                            sq.wait_ge(SOUT, out_prev)
                        for ti in range(4):
                            sq.dma_start_transpose(
                                SNT[:, ti * 1024 + dt * 128:ti * 1024 + (dt + 1) * 128],
                                SNB[:, ti * 128:(ti + 1) * 128]).then_inc(STP, 16)
                    P.add("scalar", ftsn)
                    P.inc("s_tp", 64)
                snt_all = P.val("s_tp")

                for ti in range(4):
                    tt = tc * 4 + ti
                    vprev = P.val("s_vc") if tt >= 1 else 0
                    oprev = P.val("s_out") if tt >= 1 else 0

                    def fx1r(sy, tt=tt, vprev=vprev):
                        if vprev:
                            sy.wait_ge(SVC, vprev)
                        sy.dma_start(out=X1T[:],
                                     in_=x1s[tt * 128:(tt + 1) * 128, :]
                                     ).then_inc(SX, 16)
                    P.add("sync", fx1r)
                    x1r = P.inc("s_x", 16)

                    def fout(v, ti=ti, x1r=x1r, snt_all=snt_all, oprev=oprev):
                        v.wait_ge(SX, x1r)
                        v.wait_ge(STP, snt_all)
                        if oprev:
                            v.wait_ge(SOUT, oprev)
                        v.tensor_tensor(OUTB[:],
                                        SNT[:, ti * 1024:(ti + 1) * 1024],
                                        GBR[:], ALU.mult)
                        v.tensor_tensor(OUTB[:], OUTB[:], X1T[:],
                                        ALU.add).then_inc(SVC, 1)
                    P.add("vector", fout)
                    ov = P.inc("s_vc")

                    def fst_(sy, tt=tt, ov=ov):
                        sy.wait_ge(SVC, ov)
                        sy.dma_start(out=y_d[tt * 128:(tt + 1) * 128, :],
                                     in_=OUTB[:]).then_inc(SOUT, 16)
                    P.add("sync", fst_)
                    P.inc("s_out", 16)

        out_total = P.val("s_out")

        def ffinal(sy, out_total=out_total):
            sy.wait_ge(SOUT, out_total)
        P.add("sync", ffinal)

        with nc.Block() as block:
            P.emit(block)
    return nc


def host_inputs(inputs):
    f = np.ascontiguousarray
    x = np.asarray(inputs["L0_attn_x"])
    cc = np.asarray(inputs["L0_c"])
    shared = {
        "wmod": f(np.asarray(inputs["L0_wmod"])),
        "wqT": f(np.asarray(inputs["L0_attn_wq"]).T),
        "wkT": f(np.asarray(inputs["L0_attn_wk"]).T),
        "wvT": f(np.asarray(inputs["L0_attn_wv"]).T),
        "wgT": f(np.asarray(inputs["L0_attn_wg"]).T),
        "woT": f(np.asarray(inputs["L0_attn_wo"]).T),
        "wdT": f(np.asarray(inputs["L0_attn_wd"]).T),
        "bdrow": f(np.asarray(inputs["L0_attn_bd"]).reshape(1, H)),
        "bvrow": f(np.asarray(inputs["L0_attn_bv"]).reshape(1, E)),
        "wfcT": f(np.asarray(inputs["L0_ffn_W_fc"]).T),
        "wprT": f(np.asarray(inputs["L0_ffn_W_proj"]).T),
    }
    for nm, key in (("bq", "L0_attn_bq"), ("bk", "L0_attn_bk"),
                    ("bg", "L0_attn_bg"), ("bo", "L0_attn_bo"),
                    ("gamma", "L0_attn_gamma"), ("beta", "L0_attn_beta")):
        shared[nm + "col"] = f(np.asarray(inputs[key]).reshape(NCH, 128).T)
    maps = []
    for b in range(8):
        m = dict(shared)
        m["x"] = f(x[b])
        m["ccol"] = f(cc[b].reshape(NCH, 128).T)
        maps.append(m)
    return maps


_NC_CACHE = {}


def kernel(**inputs):
    taps = ()
    if taps not in _NC_CACHE:
        _NC_CACHE[taps] = build_nc(taps)
    nc = _NC_CACHE[taps]
    maps = host_inputs(inputs)
    res = bass_utils.run_bass_kernel_spmd(nc, maps, core_ids=list(range(8)))
    out = np.stack([res.results[b]["y"] for b in range(8)], axis=0)
    return out.astype(np.float32)


# version 08
# speedup vs baseline: 1.0786x; 1.0786x over previous
import contextlib
import numpy as np
import ml_dtypes
import concourse.bass as bass
import concourse.mybir as mybir
import concourse.bass_utils as bass_utils

F32 = mybir.dt.float32
BF16 = mybir.dt.bfloat16
AX = mybir.AxisListType.X
ALU = mybir.AluOpType
ACT = mybir.ActivationFunctionType

T = 4096
E = 1024
H = 16
HD = 64
NCH = 8
NTT = 32
NG = 8
NQC = 16
NJB = 8
FF = 4096
LN2C = float(np.log(2.0))


def consts_np():
    c = {}
    invf = (1.0 / 10000.0 ** (np.arange(0, HD, 2) / HD)).astype(np.float64)
    t = np.arange(T, dtype=np.float64)
    d = np.arange(HD)
    ang = t[None, :] * invf[d % 32][:, None]
    cos64, sin64 = np.cos(ang), np.sin(ang)
    c["COS"] = np.concatenate([cos64, cos64], 0).astype(ml_dtypes.bfloat16)
    c["SIN"] = np.concatenate([sin64, sin64], 0).astype(ml_dtypes.bfloat16)
    rot = np.zeros((HD, HD), np.float32)
    for i in range(HD // 2):
        rot[2 * i + 1, 2 * i] = -1.0
        rot[2 * i, 2 * i + 1] = 1.0
    z = np.zeros_like(rot)
    c["ROT"] = np.block([[rot, z], [z, rot]]).astype(ml_dtypes.bfloat16)
    kap = np.arange(128)[:, None]
    qr = np.arange(256)[None, :]
    c["DTAB"] = np.concatenate(
        [np.maximum(kap + 128 * dd - qr, 0) for dd in range(4)],
        axis=1).astype(np.float32)
    c["QREL"] = np.tile(np.arange(256, dtype=np.float32)[None, :], (65, 1))
    c["KP"] = (kap + 256 * np.arange(1, 16)[None, :]).astype(np.float32)
    c["ONESF"] = np.ones((1, 128), np.float32)
    return c


class Prog:
    def __init__(self, nc):
        self.nc = nc
        self.ops = []
        self.sem = {}

    def new_sem(self, stack, name):
        h = stack.enter_context(self.nc.semaphore(name))
        self.sem[name] = [h, 0]

    def inc(self, name, n=1):
        s = self.sem[name]
        s[1] += n
        return s[1]

    def val(self, name):
        return self.sem[name][1]

    def hdl(self, name):
        return self.sem[name][0]

    def add(self, eng, fn):
        self.ops.append((eng, fn))

    def emit(self, block):
        for eng_name in ("sync", "gpsimd", "scalar", "vector", "tensor"):
            ops = [f for e, f in self.ops if e == eng_name]

            def body(eng, ops=ops):
                for f in ops:
                    f(eng)
            getattr(block, eng_name)(body)


def build_nc(taps=(), upto="full", ngroups=NG):
    nc = bass.Bass("TRN2", target_bir_lowering=False)
    CN = consts_np()

    x_d = nc.dram_tensor("x", [T, E], F32, kind="ExternalInput").ap()
    y_d = nc.dram_tensor("y", [T, E], F32, kind="ExternalOutput").ap()
    ccol_d = nc.dram_tensor("ccol", [128, NCH], F32, kind="ExternalInput").ap()
    wmod_d = nc.dram_tensor("wmod", [E, 6 * E], F32, kind="ExternalInput").ap()
    wT_d = {w: nc.dram_tensor(w + "T", [E, E], F32, kind="ExternalInput").ap()
            for w in ("wq", "wk", "wv", "wg", "wo")}
    wdT_d = nc.dram_tensor("wdT", [E, H], F32, kind="ExternalInput").ap()
    bd_d = nc.dram_tensor("bdrow", [1, H], F32, kind="ExternalInput").ap()
    vec_d = {v: nc.dram_tensor(v + "col", [128, NCH], F32, kind="ExternalInput").ap()
             for v in ("bq", "bk", "bg", "bo", "gamma", "beta")}
    bvrow_d = nc.dram_tensor("bvrow", [1, E], F32, kind="ExternalInput").ap()
    wfcT_d = nc.dram_tensor("wfcT", [E, FF], F32, kind="ExternalInput").ap()
    wprT_d = nc.dram_tensor("wprT", [FF, E], F32, kind="ExternalInput").ap()

    def scr_t(name, shape, dt):
        kind = "ExternalOutput" if name in taps else "Internal"
        return nc.dram_tensor(name, shape, dt, kind=kind).ap()

    g2s = scr_t("g2s", [E, T], BF16)
    x1s = scr_t("x1s", [T, E], BF16)
    h2s = scr_t("h2s", [E, T], BF16)

    tap_d = {}

    def tap(name, shape, dt):
        tap_d[name] = nc.dram_tensor(name, shape, dt, kind="ExternalOutput").ap()
        return tap_d[name]

    cst = {k: nc.inline_tensor(v, name="c_" + k).ap() for k, v in CN.items()}

    P = Prog(nc)
    stack = contextlib.ExitStack()
    with stack:
        for s in ("s_x", "s_w", "s_tp", "s_scr", "s_out",
                  "s_mm", "s_sc", "s_vc", "s_av"):
            P.new_sem(stack, s)
        SX, SW, STP, SSCR, SOUT = (P.hdl(s) for s in
                                   ("s_x", "s_w", "s_tp", "s_scr", "s_out"))
        SMM, SSC, SVC, SAV = (P.hdl(s) for s in ("s_mm", "s_sc", "s_vc", "s_av"))

        psA0 = stack.enter_context(nc.psum_tensor("psA0", [128, 512], F32))
        psA1 = stack.enter_context(nc.psum_tensor("psA1", [128, 512], F32))
        psB0 = stack.enter_context(nc.psum_tensor("psB0", [128, 512], F32))
        psB1 = stack.enter_context(nc.psum_tensor("psB1", [128, 512], F32))
        psSA = (psA0, psA1)
        psSB = (psB0, psB1)
        psOA = stack.enter_context(nc.psum_tensor("psOA", [128, 512], F32))
        psOB = stack.enter_context(nc.psum_tensor("psOB", [128, 512], F32))
        psG1 = stack.enter_context(nc.psum_tensor("psG1", [128, 512], F32))
        psG2 = stack.enter_context(nc.psum_tensor("psG2", [128, 512], F32))

        sb = lambda n, sh, dt: stack.enter_context(nc.sbuf_tensor("sb_" + n, sh, dt))
        ccol = sb("ccol", [128, NCH], F32)
        pvec = {v: sb(v, [128, NCH], F32) for v in
                ("bq", "bk", "bg", "bo", "gamma", "beta")}
        bvrow = sb("bvrow", [1, E], F32)
        wdT = sb("wdT", [128, NCH * H], F32)
        bdrow = sb("bdrow", [1, H], F32)
        ONESF = sb("ONESF", [1, 128], F32)
        onescol = sb("onescol", [128, 1], F32)
        epscol = sb("epscol", [128, 1], F32)
        negones = sb("negones", [128, 1], F32)
        halfcol = sb("halfcol", [128, 1], F32)
        n5l2 = sb("n5l2", [1, 1], F32)
        modT = sb("modT", [128, 48], F32)
        gbrow = sb("gbrow", [1, E], F32)
        cap1 = sb("cap1", [128, NCH], F32)
        cbp1 = sb("cbp1", [128, NCH], F32)
        gacl = sb("gacl", [128, NCH], F32)
        gbcl = sb("gbcl", [128, NCH], F32)
        negbg = sb("negbg", [128, NCH], F32)
        halfbg = sb("halfbg", [128, NCH], F32)
        hmean = sb("hmean", [128, NCH], F32)
        ldrow = sb("ldrow", [1, H], F32)
        pdtmp = sb("pdtmp", [1, H], F32)
        smallp = sb("smallp", [128, 16], F32)
        silu_c = sb("silu_c", [128, NCH], BF16)

        def mcol(vi, c):
            return modT[:, vi * 8 + c: vi * 8 + c + 1]

        def f0(sy):
            for name, dst in (("COS", COS), ("SIN", SIN), ("ROT", ROTm),
                              ("DTAB", DTAB), ("QREL", QREL), ("KP", KP),
                              ("ONESF", ONESF)):
                sy.dma_start(out=dst[:], in_=cst[name]).then_inc(SX, 16)
            sy.dma_start(out=ccol[:], in_=ccol_d).then_inc(SX, 16)
            for v in pvec:
                sy.dma_start(out=pvec[v][:], in_=vec_d[v]).then_inc(SX, 16)
            sy.dma_start(out=bvrow[:], in_=bvrow_d).then_inc(SX, 16)
            sy.dma_start(out=bdrow[:], in_=bd_d).then_inc(SX, 16)
            for c in range(NCH):
                sy.dma_start(out=wdT[:, c * H:(c + 1) * H],
                             in_=wdT_d[c * 128:(c + 1) * 128, :]).then_inc(SX, 16)
        P.add("sync", f0)
        ld_done = P.inc("s_x", 16 * (7 + 1 + 6 + 2 + NCH))

        def f0v(v):
            v.memset(onescol[:], 1.0)
            v.memset(epscol[:], 1e-5)
            v.memset(negones[:], -1.0)
            v.memset(halfcol[:], 0.5)
            v.memset(n5l2[:], -5.0 * LN2C).then_inc(SVC, 1)
        P.add("vector", f0v)
        P.inc("s_vc")

        def f0s(sc):
            sc.wait_ge(SX, ld_done)
            sc.activation(smallp[:, 0:8], ccol[:], ACT.Exp, bias=0.0,
                          scale=-1.0).then_inc(SSC, 1)
        P.add("scalar", f0s)
        e_done = P.inc("s_sc")

        def f0v2(v):
            v.wait_ge(SSC, e_done)
            v.tensor_scalar(smallp[:, 0:8], smallp[:, 0:8], 1.0, None, ALU.add)
            v.drain()
            v.reciprocal(smallp[:, 8:16], smallp[:, 0:8])
            v.drain()
            v.tensor_tensor(silu_c[:], ccol[:], smallp[:, 8:16],
                            ALU.mult).then_inc(SVC, 1)
        P.add("vector", f0v2)
        silu_done = P.inc("s_vc")

        if "dbg0" in taps:
            t_cc = tap("ccolv", [128, NCH], F32)
            t_sp = tap("smallpv", [128, 16], F32)
            t_si = tap("siluv", [128, NCH], BF16)

            def ftap0(sy, n=silu_done, t_cc=t_cc, t_sp=t_sp, t_si=t_si):
                sy.wait_ge(SVC, n)
                sy.dma_start(out=t_cc, in_=ccol[:]).then_inc(SOUT, 16)
                sy.dma_start(out=t_sp, in_=smallp[:]).then_inc(SOUT, 16)
                sy.dma_start(out=t_si, in_=silu_c[:]).then_inc(SOUT, 16)
            P.add("sync", ftap0)
            P.inc("s_out", 48)

        with (
            nc.sbuf_tensor("sb_hT", [128, NCH * T], BF16) as hT,
            contextlib.ExitStack() as gstk,
            contextlib.ExitStack() as p1s,
        ):
            ga_ = lambda n, sh, dt: gstk.enter_context(nc.sbuf_tensor("sb_" + n, sh, dt))
            COS = ga_("COS", [128, T], BF16)
            SIN = ga_("SIN", [128, T], BF16)
            ROTm = ga_("ROTm", [128, 128], BF16)
            DTAB = ga_("DTAB", [128, 1024], F32)
            QREL = ga_("QREL", [65, 256], F32)
            KP = ga_("KP", [128, 15], F32)
            a1 = lambda n, sh, dt: p1s.enter_context(nc.sbuf_tensor("sb_" + n, sh, dt))
            xf = a1("xf", [128, 2 * E], F32)
            xn = a1("xn", [128, 2 * E], BF16)
            lntmp = a1("lntmp", [128, E], F32)
            lnv = a1("lnv", [128, 8], F32)
            wmst = a1("wmst", [128, 2 * 3072], BF16)
            modrow = a1("modrow", [1, 6 * E], F32)
            gbtmp = a1("gbtmp", [1, E], F32)

            slots6 = [psA0[:], psA1[:], psB0[:],
                      psB1[:], psOA[:, 0:512], psOB[:, 0:512]]
            for half in range(2):
                off = half * 3072
                for c in range(NCH):
                    par = c % 2
                    dst = wmst[:, par * 3072:(par + 1) * 3072]
                    mm_need = P.val("s_mm")

                    def fw(g, dst=dst, c=c, off=off, mm_need=mm_need):
                        if mm_need:
                            g.wait_ge(SMM, mm_need)
                        g.dma_start(out=dst,
                                    in_=wmod_d[c * 128:(c + 1) * 128, off:off + 3072]
                                    ).then_inc(SW, 16)
                    P.add("gpsimd", fw)
                    w_done = P.inc("s_w", 16)
                    ev_prev = P.val("s_vc")

                    def fm(t, c=c, par=par, w_done=w_done, ev_prev=ev_prev,
                           silu_done=silu_done):
                        t.wait_ge(SW, w_done)
                        t.wait_ge(SVC, max(silu_done, ev_prev if c == 0 else 0))
                        for n in range(6):
                            ins = t.matmul(
                                slots6[n][0:1, :], silu_c[:, c:c + 1],
                                wmst[:, par * 3072 + n * 512:par * 3072 + (n + 1) * 512],
                                start=(c == 0), stop=(c == NCH - 1))
                            if n == 5:
                                ins.then_inc(SMM, 1)
                    P.add("tensor", fm)
                    P.inc("s_mm")
                mm_done = P.val("s_mm")

                def fe(v, off=off, mm_done=mm_done):
                    v.wait_ge(SMM, mm_done)
                    for n in range(6):
                        ins = v.tensor_copy(modrow[:, off + n * 512:off + (n + 1) * 512],
                                            slots6[n][0:1, :])
                        if n == 5:
                            ins.then_inc(SVC, 1)
                P.add("vector", fe)
                P.inc("s_vc")
            ev_done = P.val("s_vc")

            def fmt(t, ev_done=ev_done):
                t.wait_ge(SVC, ev_done)
                for n in range(48):
                    ins = t.matmul(psG1[:, n:n + 1],
                                   modrow[0:1, n * 128:(n + 1) * 128],
                                   ONESF[0:1, 0:1], start=True, stop=True)
                    if n == 47:
                        ins.then_inc(SMM, 1)
            P.add("tensor", fmt)
            mt_done = P.inc("s_mm")

            def fmod(v, mt_done=mt_done):
                v.wait_ge(SMM, mt_done)
                v.tensor_copy(modT[:], psG1[:, 0:48])
                v.drain()
                v.tensor_scalar(cap1[:], modT[:, 8:16], 1.0, None, ALU.add)
                v.tensor_scalar(cbp1[:], modT[:, 32:40], 1.0, None, ALU.add)
                v.drain()
                for src, dstv in ((modT[:, 16:24], gacl), (modT[:, 40:48], gbcl)):
                    v.tensor_single_scalar(smallp[:, 0:8], src, 0.0, ALU.is_ge)
                    v.drain()
                    v.tensor_scalar(smallp[:, 0:8], smallp[:, 0:8], 2.0,
                                    negones[:, 0:1], ALU.mult, ALU.add)
                    v.drain()
                    v.tensor_single_scalar(smallp[:, 8:16], src, -1.0, ALU.mult)
                    v.drain()
                    v.tensor_tensor(smallp[:, 8:16], smallp[:, 8:16], src, ALU.max)
                    v.drain()
                    v.tensor_single_scalar(smallp[:, 8:16], smallp[:, 8:16],
                                           0.25, ALU.max)
                    v.drain()
                    v.tensor_tensor(dstv[:], smallp[:, 0:8], smallp[:, 8:16],
                                    ALU.mult)
                    v.drain()
                v.tensor_single_scalar(negbg[:], pvec["bg"][:], -1.0,
                                       ALU.mult)
                v.tensor_single_scalar(halfbg[:], pvec["bg"][:], 0.5, ALU.mult)
                src_r = modrow[0:1, 5 * E:6 * E]
                v.tensor_single_scalar(gbrow[:], src_r, 0.0, ALU.is_ge)
                v.drain()
                v.tensor_scalar(gbrow[:], gbrow[:], 2.0, negones[0:1, 0:1],
                                ALU.mult, ALU.add)
                v.drain()
                v.tensor_single_scalar(gbtmp[:], src_r, -1.0, ALU.mult)
                v.drain()
                v.tensor_tensor(gbtmp[:], gbtmp[:], src_r, ALU.max)
                v.drain()
                v.tensor_single_scalar(gbtmp[:], gbtmp[:], 0.25, ALU.max)
                v.drain()
                v.tensor_tensor(gbrow[:], gbrow[:], gbtmp[:],
                                ALU.mult).then_inc(SVC, 1)
            P.add("vector", fmod)
            mod_done = P.inc("s_vc")

            for tt in range(NTT):
                par = tt % 2
                xfp = xf[:, par * E:(par + 1) * E]
                xnp = xn[:, par * E:(par + 1) * E]
                vc_need = P.val("s_vc") if tt >= 2 else 0

                def fl(sy, tt=tt, xfp=xfp, vc_need=vc_need):
                    if vc_need:
                        sy.wait_ge(SVC, vc_need)
                    sy.dma_start(out=xfp, in_=x_d[tt * 128:(tt + 1) * 128, :]
                                 ).then_inc(SX, 16)
                P.add("sync", fl)
                x_done = P.inc("s_x", 16)

                def fln(v, xfp=xfp, x_done=x_done, tt=tt):
                    v.wait_ge(SX, x_done)
                    v.reduce_sum(lnv[:, 0:1], xfp, axis=AX)
                    v.tensor_tensor(lntmp[:], xfp, xfp, ALU.mult)
                    v.reduce_sum(lnv[:, 1:2], lntmp[:], axis=AX).then_inc(SVC, 1)
                P.add("vector", fln)
                st_done = P.inc("s_vc")

                def frs_a(sc, st_done=st_done):
                    sc.wait_ge(SVC, st_done)
                    sc.activation(lnv[:, 5:6], lnv[:, 0:1], ACT.Copy,
                                  bias=0.0, scale=1.0 / E)
                    sc.activation(lnv[:, 2:3], lnv[:, 0:1], ACT.Square,
                                  bias=0.0, scale=1.0 / E).then_inc(SSC, 1)
                P.add("scalar", frs_a)
                rsa = P.inc("s_sc")

                def fem(v, rsa=rsa):
                    v.wait_ge(SSC, rsa)
                    v.tensor_scalar(lnv[:, 3:4], lnv[:, 2:3], -1.0,
                                    epscol[:, 0:1], ALU.mult,
                                    ALU.add).then_inc(SVC, 1)
                P.add("vector", fem)
                emv = P.inc("s_vc")

                def frs_b(sc, emv=emv):
                    sc.wait_ge(SVC, emv)
                    sc.activation(lnv[:, 3:4], lnv[:, 1:2], ACT.Ln,
                                  bias=lnv[:, 3:4], scale=1.0 / E)
                    sc.drain()
                    sc.activation(lnv[:, 4:5], lnv[:, 3:4], ACT.Exp,
                                  bias=0.0, scale=-0.5).then_inc(SSC, 1)
                P.add("scalar", frs_b)
                rs_done = P.inc("s_sc")
                tp_prev = P.val("s_tp") if tt >= 2 else 0

                def fxn(v, xfp=xfp, xnp=xnp, rs_done=rs_done, tp_prev=tp_prev):
                    v.wait_ge(SSC, rs_done)
                    if tp_prev:
                        v.wait_ge(STP, tp_prev)
                    v.tensor_scalar(xnp, xfp, lnv[:, 5:6], lnv[:, 4:5],
                                    ALU.subtract, ALU.mult).then_inc(SVC, 1)
                P.add("vector", fxn)
                xn_done = P.inc("s_vc")

                def ftp(sq, tt=tt, xnp=xnp, xn_done=xn_done):
                    sq.wait_ge(SVC, xn_done)
                    for c in range(NCH):
                        sq.dma_start_transpose(
                            hT[:, c * T + tt * 128:c * T + (tt + 1) * 128],
                            xnp[:, c * 128:(c + 1) * 128]).then_inc(STP, 16)
                P.add("scalar", ftp)
                P.inc("s_tp", 16 * NCH)
            tp_all = P.val("s_tp")

            for c in range(NCH):
                def fa(v, c=c, tp_all=tp_all):
                    if c == 0:
                        v.wait_ge(STP, tp_all)
                    v.tensor_scalar(hT[:, c * T:(c + 1) * T],
                                    hT[:, c * T:(c + 1) * T],
                                    cap1[:, c:c + 1], mcol(0, c),
                                    ALU.mult, ALU.add)
                    v.reduce_sum(hmean[:, c:c + 1], hT[:, c * T:(c + 1) * T], axis=AX)
                    v.drain()
                    v.tensor_single_scalar(hmean[:, c:c + 1], hmean[:, c:c + 1],
                                           1.0 / T, ALU.mult).then_inc(SVC, 1)
                P.add("vector", fa)
                P.inc("s_vc")
            h_done = P.val("s_vc")

            if "hT" in taps:
                tpp = tap("hT", [128, NCH * T], BF16)

                def ftap(sy, n=h_done, tpp=tpp):
                    sy.wait_ge(SVC, n)
                    sy.dma_start(out=tpp, in_=hT[:]).then_inc(SOUT, 16)
                P.add("sync", ftap)
                P.inc("s_out", 16)

            def fpd_mm(t, h_done=h_done):
                t.wait_ge(SVC, h_done)
                for c in range(NCH):
                    ins = t.matmul(psG1[0:1, 0:H], hmean[:, c:c + 1],
                                   wdT[:, c * H:(c + 1) * H],
                                   start=(c == 0), stop=(c == NCH - 1))
                    if c == NCH - 1:
                        ins.then_inc(SMM, 1)
            P.add("tensor", fpd_mm)
            pdm = P.inc("s_mm")

            def fz(v, pdm=pdm):
                v.wait_ge(SMM, pdm)
                v.tensor_tensor(pdtmp[:], psG1[0:1, 0:H], bdrow[:],
                                ALU.add).then_inc(SVC, 1)
            P.add("vector", fz)
            z_done = P.inc("s_vc")

            def fpe(sc, z_done=z_done):
                sc.wait_ge(SVC, z_done)
                sc.activation(ldrow[:], pdtmp[:], ACT.Exp, bias=0.0,
                              scale=-1.0).then_inc(SSC, 1)
            P.add("scalar", fpe)
            pe_done = P.inc("s_sc")

            def fsg(v, pe_done=pe_done):
                v.wait_ge(SSC, pe_done)
                v.tensor_scalar(ldrow[:], ldrow[:], 1.0, None, ALU.add)
                v.drain()
                v.reciprocal(pdtmp[:], ldrow[:])
                v.drain()
                v.nop().then_inc(SVC, 1)
            P.add("vector", fsg)
            pd_done = P.inc("s_vc")

            def fu(sc, pd_done=pd_done):
                sc.wait_ge(SVC, pd_done)
                sc.activation(ldrow[:], pdtmp[:], ACT.Exp, bias=n5l2[0:1, 0:1],
                              scale=-8.0 * LN2C).then_inc(SSC, 1)
            P.add("scalar", fu)
            u_done = P.inc("s_sc")

            def fv2(v, u_done=u_done):
                v.wait_ge(SSC, u_done)
                v.tensor_scalar(ldrow[:], ldrow[:], -1.0, onescol[0:1, 0:1],
                                ALU.mult, ALU.add).then_inc(SVC, 1)
            P.add("vector", fv2)
            v_done = P.inc("s_vc")

            def fl2(sc, v_done=v_done):
                sc.wait_ge(SVC, v_done)
                sc.activation(ldrow[:], ldrow[:], ACT.Ln, bias=0.0,
                              scale=1.0).then_inc(SSC, 1)
            P.add("scalar", fl2)
            ld_chain = P.inc("s_sc")

            if "ld" in taps:
                tpp = tap("ld", [1, H], F32)

                def ftap(sy, n=ld_chain, tpp=tpp):
                    sy.wait_ge(SSC, n)
                    sy.dma_start(out=tpp, in_=ldrow[:]).then_inc(SOUT, 16)
                P.add("sync", ftap)
                P.inc("s_out", 16)
            if "dbg1" in taps:
                t_mod = tap("modT", [128, 48], F32)
                t_hm = tap("hmean", [128, NCH], F32)
                t_pd = tap("pdt", [1, H], F32)
                t_sc = tap("siluc", [128, NCH], BF16)

                def ftap2(sy, n=ld_chain):
                    sy.wait_ge(SSC, n)
                    sy.dma_start(out=t_mod, in_=modT[:]).then_inc(SOUT, 16)
                    sy.dma_start(out=t_hm, in_=hmean[:]).then_inc(SOUT, 16)
                    sy.dma_start(out=t_pd, in_=pdtmp[:]).then_inc(SOUT, 16)
                    sy.dma_start(out=t_sc, in_=silu_c[:]).then_inc(SOUT, 16)
                P.add("sync", ftap2)
                P.inc("s_out", 64)

            p1s.close()

            NPAR = 1
            with contextlib.ExitStack() as ast:
                aa = lambda n, sh, dt: ast.enter_context(nc.sbuf_tensor("sb_" + n, sh, dt))
                QT = [aa(f"QT{i}", [128, T], BF16) for i in range(NPAR)]
                KT = [aa(f"KT{i}", [128, T], BF16) for i in range(NPAR)]
                GT = [aa(f"GT{i}", [128, T], BF16) for i in range(NPAR)]
                VT = [aa(f"VT{i}", [128, 2 * 48 * 65], BF16) for i in range(NPAR)]
                WSL = aa("WSL", [128, 4 * E], BF16)
                UB = [aa(f"UB{i}", [128, 2 * 1024], BF16) for i in range(2)]
                Opack = aa("Opack", [128, T], BF16)
                DensA = aa("DensA", [1, T], F32)
                DensB = aa("DensB", [1, T], F32)
                tmp65 = aa("tmp65", [65, 256], F32)
                g2b = aa("g2b", [128, 2 * 512], BF16)
                EB = [aa(f"EB{i}", [128, 2 * 2 * 256], BF16) for i in range(NPAR)]
                Bb = [aa(f"Bb{i}", [65, 2 * 256], F32) for i in range(NPAR)]
                BV = [aa(f"BV{i}", [128, 2 * 15], F32) for i in range(NPAR)]
                ldbc = [aa(f"ldbc{i}", [128, 2], F32) for i in range(NPAR)]
                facm = [aa(f"facm{i}", [128, 2], F32) for i in range(NPAR)]
                rtmp = aa("rtmp", [128, 512], F32)
                qraw = aa("qraw", [128, 512], BF16)
                gtmp = aa("gtmp", [128, 512], F32)
                gtmp2 = aa("gtmp2", [128, 512], BF16)
                bvrep = aa("bvrep", [128, 128], F32)
                ebt = aa("ebt", [128, 256], F32)
                stat = aa("stat", [128, 36], F32)
                coef = aa("coef", [128, 2], F32)
                rdenA_t = aa("rdenA", [1, 256], F32)
                rdenB_t = aa("rdenB", [1, 256], F32)
                lnden_t = aa("lnden", [1, 256], F32)
                rdenA = rdenA_t[:]
                rdenB = rdenB_t[:]
                scl1 = aa("scl1", [1, 8], F32)

                def gen_group(g, par):
                    av_need = P.val("s_av")

                    def fw(gp, g=g, av_need=av_need, ld_chain=ld_chain):
                        if g == 0:
                            gp.wait_ge(SSC, ld_chain)
                        if av_need:
                            gp.wait_ge(SAV, av_need)
                        for wi, w in enumerate(("wq", "wk", "wv", "wg")):
                            for c in range(NCH):
                                gp.dma_start(
                                    out=WSL[:, wi * E + c * 128:wi * E + (c + 1) * 128],
                                    in_=wT_d[w][c * 128:(c + 1) * 128,
                                                g * 128:(g + 1) * 128]).then_inc(SW, 16)
                    P.add("gpsimd", fw)
                    w_done = P.inc("s_w", 16 * 32)
                    vc_prev = P.val("s_vc")

                    def fld(t, g=g, vc_prev=vc_prev, ld_chain=ld_chain):
                        t.wait_ge(SSC, ld_chain)
                        t.wait_ge(SVC, vc_prev)
                        for hh in range(2):
                            ins = t.matmul(psG2[:, hh:hh + 1], ONESF[:],
                                           ldrow[0:1, 2 * g + hh:2 * g + hh + 1],
                                           start=True, stop=True)
                            if hh == 1:
                                ins.then_inc(SMM, 1)
                    P.add("tensor", fld)
                    ldm = P.inc("s_mm")

                    def fldv(v, par=par, ldm=ldm):
                        v.wait_ge(SMM, ldm)
                        v.tensor_copy(ldbc[par][:], psG2[:, 0:2]).then_inc(SVC, 1)
                    P.add("vector", fldv)
                    ldv = P.inc("s_vc")

                    for hh in range(2):
                        sc_tab = P.val("s_sc")

                        def ft_v(v, par=par, hh=hh, ldv=ldv, sc_tab=sc_tab):
                            if hh == 0:
                                v.wait_ge(SVC, ldv)
                            else:
                                v.wait_ge(SSC, sc_tab)
                            v.tensor_scalar(BV[par][:, hh * 15:(hh + 1) * 15],
                                            KP[:], ldbc[par][:, hh:hh + 1], None,
                                            ALU.mult)
                            v.tensor_single_scalar(coef[:, 0:1],
                                                   ldbc[par][:, hh:hh + 1], -1.0,
                                                   ALU.mult)
                            v.drain()
                            v.tensor_scalar(tmp65[:], QREL[:], coef[0:65, 0:1],
                                            None, ALU.mult).then_inc(SVC, 1)
                        P.add("vector", ft_v)
                        tv = P.inc("s_vc")

                        def ft_s(sc, par=par, hh=hh, tv=tv):
                            sc.wait_ge(SVC, tv)
                            sc.activation(Bb[par][:, hh * 256:(hh + 1) * 256],
                                          tmp65[:], ACT.Exp, bias=0.0, scale=1.0)
                            sc.activation(facm[par][:, hh:hh + 1],
                                          ldbc[par][:, hh:hh + 1], ACT.Exp,
                                          bias=0.0,
                                          scale=128.0).then_inc(SSC, 1)
                        P.add("scalar", ft_s)
                        P.inc("s_sc")
                        for dd in range(2):
                            sc_need = P.val("s_sc")

                            def feb_v(v, par=par, hh=hh, dd=dd, sc_need=sc_need):
                                if dd > 0 or hh > 0:
                                    v.wait_ge(SSC, sc_need)
                                v.tensor_scalar(ebt[:], DTAB[:, dd * 256:(dd + 1) * 256],
                                                ldbc[par][:, hh:hh + 1], None,
                                                ALU.mult).then_inc(SVC, 1)
                            P.add("vector", feb_v)
                            ebv = P.inc("s_vc")

                            def feb_s(sc, par=par, hh=hh, dd=dd, ebv=ebv):
                                sc.wait_ge(SVC, ebv)
                                sc.activation(
                                    EB[par][:, (hh * 2 + dd) * 256:(hh * 2 + dd + 1) * 256],
                                    ebt[:], ACT.Exp, bias=0.0,
                                    scale=1.0).then_inc(SSC, 1)
                            P.add("scalar", feb_s)
                            P.inc("s_sc")

                    def fbv(t, g=g):
                        t.matmul(psG1[:, 0:128], ONESF[:],
                                 bvrow[0:1, g * 128:(g + 1) * 128],
                                 start=True, stop=True).then_inc(SMM, 1)
                    P.add("tensor", fbv)
                    bvm = P.inc("s_mm")

                    def fbvv(v, bvm=bvm, par=par):
                        v.wait_ge(SMM, bvm)
                        v.tensor_copy(bvrep[:], psG1[:, 0:128])
                        v.memset(VT[par][:], 1.0).then_inc(SVC, 1)
                    P.add("vector", fbvv)
                    bvv = P.inc("s_vc")

                    for wi, dstl in ((0, QT), (1, KT), (3, GT)):
                        for n in range(8):
                            ev_prev = P.val("s_vc")

                            def fp(t, wi=wi, n=n, w_done=w_done, h_done=h_done,
                                   ev_prev=ev_prev):
                                t.wait_ge(SW, w_done)
                                t.wait_ge(SVC, ev_prev)
                                for c in range(NCH):
                                    ins = t.matmul(
                                        psG1[:],
                                        WSL[:, wi * E + c * 128:wi * E + (c + 1) * 128],
                                        hT[:, c * T + n * 512:c * T + (n + 1) * 512],
                                        start=(c == 0), stop=(c == NCH - 1))
                                    if c == NCH - 1:
                                        ins.then_inc(SMM, 1)
                            P.add("tensor", fp)
                            pm = P.inc("s_mm")

                            if wi < 2:
                                bias = pvec["bq"] if wi == 0 else pvec["bk"]

                                def fev(v, g=g, pm=pm, bias=bias):
                                    v.wait_ge(SMM, pm)
                                    v.tensor_scalar(qraw[:], psG1[:],
                                                    bias[:, g:g + 1], None,
                                                    ALU.add).then_inc(SVC, 1)
                                P.add("vector", fev)
                                ev = P.inc("s_vc")

                                def frot(t, ev=ev):
                                    t.wait_ge(SVC, ev)
                                    t.matmul(psG2[:], ROTm[:], qraw[:],
                                             start=True, stop=True).then_inc(SMM, 1)
                                P.add("tensor", frot)
                                rm = P.inc("s_mm")

                                def fcomb(v, par=par, n=n, dstl=dstl, rm=rm):
                                    v.wait_ge(SMM, rm)
                                    dst = dstl[par][:, n * 512:(n + 1) * 512]
                                    v.tensor_tensor(dst, qraw[:],
                                                    COS[:, n * 512:(n + 1) * 512],
                                                    ALU.mult)
                                    v.tensor_tensor(rtmp[:], psG2[:],
                                                    SIN[:, n * 512:(n + 1) * 512],
                                                    ALU.mult)
                                    v.tensor_tensor(dst, dst, rtmp[:],
                                                    ALU.add).then_inc(SVC, 1)
                                P.add("vector", fcomb)
                                P.inc("s_vc")
                            else:
                                def fge(sc, g=g, pm=pm):
                                    sc.wait_ge(SMM, pm)
                                    sc.activation(gtmp2[:], psG1[:], ACT.Tanh,
                                                  bias=halfbg[:, g:g + 1],
                                                  scale=0.5).then_inc(SSC, 1)
                                P.add("scalar", fge)
                                ge = P.inc("s_sc")

                                def fgv(v, par=par, n=n, g=g, ge=ge):
                                    v.wait_ge(SSC, ge)
                                    v.tensor_scalar(gtmp[:], gtmp2[:], 0.5,
                                                    halfcol[:, 0:1], ALU.mult,
                                                    ALU.add)
                                    v.scalar_tensor_tensor(
                                        GT[par][:, n * 512:(n + 1) * 512],
                                        psG1[:], pvec["bg"][:, g:g + 1], gtmp[:],
                                        ALU.add, ALU.mult).then_inc(SVC, 1)
                                P.add("vector", fgv)
                                P.inc("s_vc")

                    for tt in range(NTT):
                        ev_prev = P.val("s_vc")

                        def fvm(t, tt=tt, w_done=w_done, ev_prev=ev_prev):
                            if tt == 0:
                                t.wait_ge(SW, w_done)
                            t.wait_ge(SVC, ev_prev)
                            for c in range(NCH):
                                ins = t.matmul(
                                    psG2[:, 0:128],
                                    hT[:, c * T + tt * 128:c * T + (tt + 1) * 128],
                                    WSL[:, 2 * E + c * 128:2 * E + (c + 1) * 128],
                                    start=(c == 0), stop=(c == NCH - 1))
                                if c == NCH - 1:
                                    ins.then_inc(SMM, 1)
                        P.add("tensor", fvm)
                        vm = P.inc("s_mm")

                        def fve(v, par=par, tt=tt, vm=vm, bvv=bvv):
                            v.wait_ge(SMM, vm)
                            if tt == 0:
                                v.wait_ge(SVC, bvv)
                            for hh in range(2):
                                slot = hh * 48 + tt
                                ins = v.tensor_tensor(
                                    VT[par][:, slot * 65:slot * 65 + 64],
                                    psG2[:, hh * 64:(hh + 1) * 64],
                                    bvrep[:, hh * 64:(hh + 1) * 64], ALU.add)
                                if hh == 1:
                                    ins.then_inc(SVC, 1)
                        P.add("vector", fve)
                        P.inc("s_vc")

                    tabs_done = P.val("s_sc")

                    def fvs(v, par=par, tabs_done=tabs_done):
                        v.wait_ge(SSC, tabs_done)
                        for hh in range(2):
                            for jo in range(16):
                                j = 2 * jo + 1
                                src = (hh * 48 + j) * 65
                                dstc = (hh * 48 + 32 + jo) * 65
                                ins = v.tensor_scalar(
                                    VT[par][:, dstc:dstc + 65],
                                    VT[par][:, src:src + 65],
                                    facm[par][:, hh:hh + 1],
                                    None, ALU.mult)
                                if hh == 1 and jo == 15:
                                    ins.then_inc(SVC, 1)
                    P.add("vector", fvs)
                    P.inc("s_vc")
                    return P.val("s_vc")

                def flash_group(g, par, gen_done):
                    evac_last = {0: 0, 1: 0}
                    first_av = {}
                    exp_done = {0: [], 1: []}
                    av_done = {0: [], 1: []}
                    pending_av = []
                    gb = 0
                    for qc in range(NQC):
                        for b in range(16):
                            j0 = 2 * b
                            kind = ("past" if b < qc else
                                    "band" if b == qc else "fut")
                            bank = gb % 2
                            sc_need = 0
                            if gb >= 1:
                                sc_need = max(exp_done[0][gb - 1],
                                              exp_done[1][gb - 1])

                            def fS(t, par=par, qc=qc, j0=j0, bank=bank,
                                   sc_need=sc_need,
                                   first=(qc == 0 and b == 0),
                                   gen_done=gen_done):
                                if first:
                                    t.wait_ge(SVC, gen_done)
                                if sc_need:
                                    t.wait_ge(SSC, sc_need)
                                for m in range(2):
                                    j = j0 + m
                                    for hh, pss in ((0, psSA), (1, psSB)):
                                        ps = pss[bank]
                                        for kh in range(2):
                                            ins = t.matmul(
                                                ps[kh * 64:(kh + 1) * 64,
                                                   m * 256:(m + 1) * 256],
                                                KT[par][hh * 64:(hh + 1) * 64,
                                                        j * 128 + kh * 64:
                                                        j * 128 + (kh + 1) * 64],
                                                QT[par][hh * 64:(hh + 1) * 64,
                                                        qc * 256:(qc + 1) * 256],
                                                start=True, stop=True,
                                                tile_position=(hh * 64, kh * 64))
                                            if m == 1 and hh == 1 and kh == 1:
                                                ins.then_inc(SMM, 1)
                            P.add("tensor", fS)
                            smm = P.inc("s_mm")

                            for hh, pss, psO in ((0, psSA, psOA), (1, psSB, psOB)):
                                avn = av_done[hh][gb - 1] if gb >= 1 else 0

                                def fexp(sc, par=par, hh=hh, pss=pss, qc=qc, b=b,
                                         bank=bank, kind=kind, smm=smm, avn=avn):
                                    sc.wait_ge(SMM, smm)
                                    if avn:
                                        sc.wait_ge(SAV, avn)
                                    dst = UB[hh][:, bank * 512:(bank + 1) * 512]
                                    if kind == "fut":
                                        s = b - qc
                                        sc.activation(
                                            dst, pss[bank][:], ACT.Exp,
                                            bias=BV[par][:, hh * 15 + s - 1:hh * 15 + s],
                                            scale=0.125).then_inc(SSC, 1)
                                    else:
                                        sc.activation(
                                            dst, pss[bank][:], ACT.Exp, bias=0.0,
                                            scale=0.125).then_inc(SSC, 1)
                                P.add("scalar", fexp)
                                edone = P.inc("s_sc")
                                exp_done[hh].append(edone)

                                eb_done = 0
                                if kind == "band":
                                    def feb(v, par=par, hh=hh, bank=bank,
                                            edone=edone):
                                        v.wait_ge(SSC, edone)
                                        for m in range(2):
                                            u = UB[hh][:, bank * 512 + m * 256:
                                                       bank * 512 + (m + 1) * 256]
                                            ins = v.tensor_tensor(
                                                u, u,
                                                EB[par][:, (hh * 2 + m) * 256:
                                                        (hh * 2 + m + 1) * 256],
                                                ALU.mult)
                                            if m == 1:
                                                ins.then_inc(SVC, 1)
                                    P.add("vector", feb)
                                    eb_done = P.inc("s_vc")

                                ev_need = evac_last[hh]

                                def fav(t, par=par, hh=hh, psO=psO, qc=qc, b=b,
                                        j0=j0, bank=bank, kind=kind, edone=edone,
                                        eb_done=eb_done, ev_need=ev_need,
                                        sp=first_av):
                                    t.wait_ge(SSC, edone)
                                    if eb_done:
                                        t.wait_ge(SVC, eb_done)
                                    if ev_need:
                                        t.wait_ge(SVC, ev_need)
                                    for m in range(2):
                                        j = j0 + m
                                        u = UB[hh][:, bank * 512 + m * 256:
                                                   bank * 512 + (m + 1) * 256]
                                        if kind == "fut":
                                            slot = (hh * 48 + 32 + j // 2
                                                    if (j % 2) else hh * 48 + j)
                                            dst = psO[0:65, 256:512]
                                            fk = (hh, "f", qc)
                                        else:
                                            slot = hh * 48 + j
                                            dst = psO[0:65, 0:256]
                                            fk = (hh, "p", qc)
                                        st = fk not in sp
                                        sp[fk] = True
                                        ins = t.matmul(
                                            dst, VT[par][:, slot * 65:(slot + 1) * 65],
                                            u, start=st, stop=True)
                                        if m == 1:
                                            ins.then_inc(SAV, 1)
                                pending_av.append(fav)
                                av_done[hh].append(P.inc("s_av"))
                            while len(pending_av) > 2:
                                P.add("tensor", pending_av.pop(0))
                            gb += 1
                        while pending_av:
                            P.add("tensor", pending_av.pop(0))
                        avm = P.val("s_av")

                        for hh, psO, dens in ((0, psOA, DensA), (1, psOB, DensB)):
                            has_fut = qc < NQC - 1

                            def fevac(v, par=par, hh=hh, psO=psO, dens=dens,
                                      qc=qc, avm=avm, has_fut=has_fut):
                                v.wait_ge(SAV, avm)
                                num = Opack[hh * 64:(hh + 1) * 64,
                                            qc * 256:(qc + 1) * 256]
                                dn = dens[0:1, qc * 256:(qc + 1) * 256]
                                if has_fut:
                                    v.tensor_tensor(tmp65[:], psO[0:65, 256:512],
                                                    Bb[par][:, hh * 256:(hh + 1) * 256],
                                                    ALU.mult)
                                    v.tensor_tensor(num, tmp65[0:64, :],
                                                    psO[0:64, 0:256], ALU.add)
                                    v.tensor_tensor(dn, tmp65[64:65, :],
                                                    psO[64:65, 0:256],
                                                    ALU.add).then_inc(SVC, 1)
                                else:
                                    v.tensor_copy(num, psO[0:64, 0:256])
                                    v.tensor_copy(dn, psO[64:65, 0:256]).then_inc(SVC, 1)
                            P.add("vector", fevac)
                            evac_last[hh] = P.inc("s_vc")
                    return max(evac_last.values())

                def division_group(g, par, evac_done):
                    def fz0(v, evac_done=evac_done):
                        v.wait_ge(SVC, evac_done)
                        v.memset(stat[:], 0.0).then_inc(SVC, 1)
                    P.add("vector", fz0)
                    P.inc("s_vc")
                    for ch in range(16):
                        bc_prev = P.val("s_mm")

                        def fsrd(sc, ch=ch, evac_done=evac_done, bc_prev=bc_prev):
                            if ch == 0:
                                sc.wait_ge(SVC, evac_done)
                            else:
                                sc.wait_ge(SMM, bc_prev)
                            lntmp_r = lnden_t[:]
                            sc.activation(lntmp_r, DensA[0:1, ch * 256:(ch + 1) * 256],
                                          ACT.Ln, bias=0.0, scale=1.0)
                            sc.drain()
                            sc.activation(rdenA, lntmp_r, ACT.Exp, bias=0.0,
                                          scale=-1.0)
                            sc.drain()
                            sc.activation(lntmp_r, DensB[0:1, ch * 256:(ch + 1) * 256],
                                          ACT.Ln, bias=0.0, scale=1.0)
                            sc.drain()
                            sc.activation(rdenB, lntmp_r, ACT.Exp, bias=0.0,
                                          scale=-1.0).then_inc(SSC, 1)
                        P.add("scalar", fsrd)
                        rdv = P.inc("s_sc")

                        od_prev = P.val("s_vc")

                        def fbc(t, rdv=rdv, od_prev=od_prev):
                            t.wait_ge(SSC, rdv)
                            t.wait_ge(SVC, od_prev)
                            t.matmul(psG2[0:64, 0:256], ONESF[0:1, 0:64],
                                     rdenA, start=True, stop=True)
                            t.matmul(psG2[64:128, 0:256], ONESF[0:1, 0:64],
                                     rdenB, start=True, stop=True,
                                     tile_position=(0, 64)).then_inc(SMM, 1)
                        P.add("tensor", fbc)
                        bcm = P.inc("s_mm")

                        def fod(v, ch=ch, bcm=bcm):
                            v.wait_ge(SMM, bcm)
                            sl = Opack[:, ch * 256:(ch + 1) * 256]
                            v.tensor_tensor(sl, sl, psG2[:, 0:256], ALU.mult)
                            v.reduce_sum(stat[:, 2 + 2 * ch:3 + 2 * ch], sl, axis=AX)
                            v.tensor_tensor(rtmp[:, 0:256], sl, sl, ALU.mult)
                            v.reduce_sum(stat[:, 3 + 2 * ch:4 + 2 * ch],
                                         rtmp[:, 0:256], axis=AX).then_inc(SVC, 1)
                        P.add("vector", fod)
                        P.inc("s_vc")

                    def fstat(v):
                        v.drain()
                        v.reduce_sum(stat[:, 0:1], stat[:, 2:34:2], axis=AX)
                        v.reduce_sum(stat[:, 1:2], stat[:, 3:35:2],
                                     axis=AX).then_inc(SVC, 1)
                    P.add("vector", fstat)
                    p1done = P.inc("s_vc")

                    def fsm(t, p1done=p1done):
                        t.wait_ge(SVC, p1done)
                        t.matmul(psG2[0:1, 0:2], onescol[0:64, 0:1],
                                 stat[0:64, 0:2], start=True, stop=True)
                        t.matmul(psG2[0:1, 2:4], onescol[64:128, 0:1],
                                 stat[64:128, 0:2], start=True, stop=True,
                                 tile_position=(64, 0)).then_inc(SMM, 1)
                    P.add("tensor", fsm)
                    smm2 = P.inc("s_mm")

                    def fmu(v, smm2=smm2):
                        v.wait_ge(SMM, smm2)
                        v.tensor_single_scalar(scl1[0:1, 0:1], psG2[0:1, 0:1],
                                               1.0 / (T * HD), ALU.mult)
                        v.tensor_single_scalar(scl1[0:1, 2:3], psG2[0:1, 1:2],
                                               1.0 / (T * HD), ALU.mult)
                        v.tensor_single_scalar(scl1[0:1, 1:2], psG2[0:1, 2:3],
                                               1.0 / (T * HD), ALU.mult)
                        v.tensor_single_scalar(scl1[0:1, 3:4], psG2[0:1, 3:4],
                                               1.0 / (T * HD), ALU.mult)
                        v.drain()
                        v.tensor_tensor(scl1[0:1, 4:5], scl1[0:1, 0:1],
                                        scl1[0:1, 0:1], ALU.mult)
                        v.tensor_tensor(scl1[0:1, 5:6], scl1[0:1, 1:2],
                                        scl1[0:1, 1:2], ALU.mult)
                        v.drain()
                        v.tensor_tensor(scl1[0:1, 4:5], scl1[0:1, 2:3],
                                        scl1[0:1, 4:5], ALU.subtract)
                        v.tensor_tensor(scl1[0:1, 5:6], scl1[0:1, 3:4],
                                        scl1[0:1, 5:6], ALU.subtract)
                        v.drain()
                        v.nop().then_inc(SVC, 1)
                    P.add("vector", fmu)
                    muv = P.inc("s_vc")

                    def frs2(sc, muv=muv):
                        sc.wait_ge(SVC, muv)
                        sc.activation(scl1[0:1, 6:7], scl1[0:1, 4:5], ACT.Ln,
                                      bias=epscol[0:1, 0:1], scale=1.0)
                        sc.activation(scl1[0:1, 7:8], scl1[0:1, 5:6], ACT.Ln,
                                      bias=epscol[0:1, 0:1], scale=1.0)
                        sc.drain()
                        sc.activation(scl1[0:1, 6:7], scl1[0:1, 6:7], ACT.Exp,
                                      bias=0.0, scale=-0.5)
                        sc.activation(scl1[0:1, 7:8], scl1[0:1, 7:8], ACT.Exp,
                                      bias=0.0, scale=-0.5).then_inc(SSC, 1)
                    P.add("scalar", frs2)
                    rsv = P.inc("s_sc")

                    def fcb(t, rsv=rsv):
                        t.wait_ge(SSC, rsv)
                        t.matmul(psG2[0:64, 0:1], ONESF[0:1, 0:64],
                                 scl1[0:1, 0:1], start=True, stop=True,
                                 tile_position=(0, 0))
                        t.matmul(psG2[64:128, 0:1], ONESF[0:1, 0:64],
                                 scl1[0:1, 1:2], start=True, stop=True,
                                 tile_position=(0, 64))
                        t.matmul(psG2[0:64, 1:2], ONESF[0:1, 0:64],
                                 scl1[0:1, 6:7], start=True, stop=True,
                                 tile_position=(0, 0))
                        t.matmul(psG2[64:128, 1:2], ONESF[0:1, 0:64],
                                 scl1[0:1, 7:8], start=True, stop=True,
                                 tile_position=(0, 64)).then_inc(SMM, 1)
                    P.add("tensor", fcb)
                    cbm = P.inc("s_mm")

                    def fcoef(v, g=g, cbm=cbm):
                        v.wait_ge(SMM, cbm)
                        v.tensor_tensor(coef[:, 0:1], pvec["gamma"][:, g:g + 1],
                                        psG2[:, 1:2], ALU.mult)
                        v.drain()
                        v.tensor_tensor(coef[:, 1:2], psG2[:, 0:1], coef[:, 0:1],
                                        ALU.mult)
                        v.drain()
                        v.tensor_tensor(coef[:, 1:2], pvec["beta"][:, g:g + 1],
                                        coef[:, 1:2], ALU.subtract)
                        v.drain()
                        v.nop().then_inc(SVC, 1)
                    P.add("vector", fcoef)
                    cov = P.inc("s_vc")

                    for ch in range(8):
                        gp = ch % 2
                        scr_need = P.val("s_scr") if ch >= 2 else 0

                        def fg2(v, g=g, par=par, ch=ch, gp=gp, cov=cov,
                                scr_need=scr_need):
                            if ch == 0:
                                v.wait_ge(SVC, cov)
                            if scr_need:
                                v.wait_ge(SSCR, scr_need)
                            sl = Opack[:, ch * 512:(ch + 1) * 512]
                            v.tensor_scalar(rtmp[:], sl, coef[:, 0:1],
                                            coef[:, 1:2], ALU.mult, ALU.add)
                            v.tensor_tensor(g2b[:, gp * 512:(gp + 1) * 512],
                                            rtmp[:],
                                            GT[par][:, ch * 512:(ch + 1) * 512],
                                            ALU.mult).then_inc(SVC, 1)
                        P.add("vector", fg2)
                        g2v = P.inc("s_vc")

                        def fsc(sy, g=g, ch=ch, gp=gp, g2v=g2v):
                            sy.wait_ge(SVC, g2v)
                            sy.dma_start(out=g2s[g * 128:(g + 1) * 128,
                                                 ch * 512:(ch + 1) * 512],
                                         in_=g2b[:, gp * 512:(gp + 1) * 512]
                                         ).then_inc(SSCR, 16)
                        P.add("sync", fsc)
                        P.inc("s_scr", 16)

                groups = range(ngroups) if upto != "p1" else range(0)
                for g in groups:
                    par = 0
                    gen_done = gen_group(g, par)
                    if g == 0 and "dbg2" in taps:
                        t_wsl = tap("WSL", [128, 4 * E], BF16)
                        t_ldbc = tap("ldbc", [128, 2], F32)
                        t_bb = tap("Bb", [65, 2 * 256], F32)
                        t_bv = tap("BVt", [128, 2 * 15], F32)
                        t_eb = tap("EBt", [128, 2 * 2 * 256], BF16)

                        def ftapg(sy, n=gen_done, t_wsl=t_wsl, t_ldbc=t_ldbc,
                                  t_bb=t_bb, t_bv=t_bv, t_eb=t_eb):
                            sy.wait_ge(SVC, n)
                            sy.dma_start(out=t_wsl, in_=WSL[:]).then_inc(SOUT, 16)
                            sy.dma_start(out=t_ldbc, in_=ldbc[0][:]).then_inc(SOUT, 16)
                            sy.dma_start(out=t_bb, in_=Bb[0][:]).then_inc(SOUT, 16)
                            sy.dma_start(out=t_bv, in_=BV[0][:]).then_inc(SOUT, 16)
                            sy.dma_start(out=t_eb, in_=EB[0][:]).then_inc(SOUT, 16)
                        P.add("sync", ftapg)
                        P.inc("s_out", 80)
                    ev_done = flash_group(g, par, gen_done)
                    if g == 0 and "Opre" in taps:
                        t_opre = tap("Opre", [128, T], BF16)
                        t_dpre = tap("Dpre", [2, T], F32)

                        def ftappre(sy, n=ev_done, t_opre=t_opre, t_dpre=t_dpre):
                            sy.wait_ge(SVC, n)
                            sy.dma_start(out=t_opre, in_=Opack[:]).then_inc(SOUT, 16)
                            sy.dma_start(out=t_dpre[0:1, :], in_=DensA[:]).then_inc(SOUT, 16)
                            sy.dma_start(out=t_dpre[1:2, :], in_=DensB[:]).then_inc(SOUT, 16)
                        P.add("sync", ftappre)
                        tapn = P.inc("s_out", 48)

                        def fgate(v, tapn=tapn):
                            v.wait_ge(SOUT, tapn)
                        P.add("vector", fgate)
                        P.add("scalar", lambda sc, tapn=tapn: sc.wait_ge(SOUT, tapn))
                    division_group(g, par, ev_done)

                    if g == 0:
                        for nm, buf in (("QT", QT[0]), ("KT", KT[0]),
                                        ("GT", GT[0]), ("VT", VT[0])):
                            if nm in taps:
                                shp = [128, T] if nm != "VT" else [128, 2 * 48 * 65]
                                tpp = tap(nm, shp, BF16)

                                def ftap(sy, tpp=tpp, buf=buf, n=gen_done):
                                    sy.wait_ge(SVC, n)
                                    sy.dma_start(out=tpp, in_=buf[:]).then_inc(SOUT, 16)
                                P.add("sync", ftap)
                                P.inc("s_out", 16)
                        if "Opack" in taps:
                            tpo = tap("Opack", [128, T], BF16)
                            tpd = tap("Dens", [2, T], F32)
                            nvc = P.val("s_vc")

                            def ftap(sy, n=nvc, tpo=tpo, tpd=tpd):
                                sy.wait_ge(SVC, n)
                                sy.dma_start(out=tpo, in_=Opack[:]).then_inc(SOUT, 16)
                                sy.dma_start(out=tpd[0:1, :], in_=DensA[:]).then_inc(SOUT, 16)
                                sy.dma_start(out=tpd[1:2, :], in_=DensB[:]).then_inc(SOUT, 16)
                            P.add("sync", ftap)
                            P.inc("s_out", 48)
                g2_all = P.val("s_scr")

            with contextlib.ExitStack() as wst:
                wa = lambda n, sh, dt: wst.enter_context(nc.sbuf_tensor("sb_" + n, sh, dt))
                woT = wa("woT", [128, NCH * E], BF16)
                g2rd = wa("g2rd", [128, 2 * NCH * 512], BF16)
                attb = wa("attb", [128, 2 * 512], BF16)
                attT = wa("attT", [128, 2 * 4096], BF16)
                xfw = wa("xfw", [128, 2 * E], F32)
                x1f = wa("x1f", [128, E], F32)
                xn2 = wa("xn2", [128, E], BF16)
                x1bf = wa("x1bf", [128, 2 * E], BF16)
                h2blk = wa("h2blk", [128, 2 * E], BF16)
                lnw = wa("lnw", [128, 8], F32)
                lntw = wa("lntw", [128, E], F32)

                do_wo = upto in ("wo", "full")
                if do_wo:
                    vcg = P.val("s_vc")

                    def fdummy(t, vcg=vcg):
                        t.wait_ge(SVC, vcg)
                        t.matmul(psG1[0:1, 0:1], ONESF[0:1, 0:1], ONESF[0:1, 0:1],
                                 start=True, stop=True).then_inc(SMM, 1)
                    P.add("tensor", fdummy)
                    P.inc("s_mm")

                    def fwo_w(gp, n=g2_all):
                        gp.wait_ge(SSCR, n)
                        gp.dma_start(out=woT[:].rearrange("p (c j) -> p c j", c=NCH),
                                     in_=wT_d["wo"].rearrange("(c p) j -> p c j",
                                                              p=128)).then_inc(SW, 16)
                    P.add("gpsimd", fwo_w)
                    wo_w = P.inc("s_w", 16)

                for tc in (range(NCH) if do_wo else range(0)):
                    gpar = tc % 2
                    g2need = P.val("s_mm") if tc >= 2 else 0

                    def fg2r(gp, tc=tc, gpar=gpar, g2need=g2need, n=g2_all):
                        if tc == 0:
                            gp.wait_ge(SSCR, n)
                        if g2need:
                            gp.wait_ge(SMM, g2need)
                        gp.dma_start(
                            out=g2rd[:, gpar * NCH * 512:(gpar + 1) * NCH * 512
                                     ].rearrange("p (c t) -> p c t", c=NCH),
                            in_=g2s[:, tc * 512:(tc + 1) * 512].rearrange(
                                "(c p) t -> p c t", p=128)).then_inc(SW, 16)
                    P.add("gpsimd", fg2r)
                    g2r = P.inc("s_w", 16)

                    if tc == 0 and "dbg3" in taps:
                        t_wo = tap("woTv", [128, NCH * E], BF16)
                        t_g2r = tap("g2rdv", [128, NCH * 512], BF16)

                        def ftap3(sy, g2r=g2r, wo_w=wo_w, t_wo=t_wo, t_g2r=t_g2r):
                            sy.wait_ge(SW, max(g2r, wo_w))
                            sy.dma_start(out=t_wo, in_=woT[:]).then_inc(SOUT, 16)
                            sy.dma_start(out=t_g2r, in_=g2rd[:, 0:NCH * 512]).then_inc(SOUT, 16)
                        P.add("sync", ftap3)
                        P.inc("s_out", 32)

                    for dt in range(NCH):
                        apar = dt % 2
                        vc_prev = P.val("s_vc")

                        def fwm(t, tc=tc, dt=dt, gpar=gpar, g2r=g2r, wo_w=wo_w,
                                vc_prev=vc_prev):
                            t.wait_ge(SW, max(g2r, wo_w))
                            t.wait_ge(SVC, vc_prev)
                            dstp = psG1[:] if dt % 2 == 0 else psG2[:]
                            for c in range(NCH):
                                ins = t.matmul(
                                    dstp,
                                    woT[:, c * E + dt * 128:c * E + (dt + 1) * 128],
                                    g2rd[:, gpar * NCH * 512 + c * 512:
                                         gpar * NCH * 512 + (c + 1) * 512],
                                    start=(c == 0), stop=(c == NCH - 1))
                                if c == NCH - 1:
                                    ins.then_inc(SMM, 1)
                        P.add("tensor", fwm)
                        wm = P.inc("s_mm")

                        tp_need = P.val("s_tp") if (tc * NCH + dt) >= 2 else 0

                        def fae(v, dt=dt, apar=apar, wm=wm, tp_need=tp_need):
                            v.wait_ge(SMM, wm)
                            if tp_need:
                                v.wait_ge(STP, tp_need)
                            dstp = psG1[:] if dt % 2 == 0 else psG2[:]
                            v.tensor_scalar(attb[:, apar * 512:(apar + 1) * 512],
                                            dstp, pvec["bo"][:, dt:dt + 1],
                                            gacl[:, dt:dt + 1], ALU.add,
                                            ALU.mult).then_inc(SVC, 1)
                        P.add("vector", fae)
                        ae = P.inc("s_vc")

                        if tc == 0 and dt == 0 and "dbg5" in taps:
                            t_ab = tap("attbv", [128, 512], BF16)

                            def ftap5(sy, ae=ae, t_ab=t_ab):
                                sy.wait_ge(SVC, ae)
                                sy.dma_start(out=t_ab, in_=attb[:, 0:512]).then_inc(SOUT, 16)
                            P.add("sync", ftap5)
                            P.inc("s_out", 16)

                        def ftr(sq, tc=tc, dt=dt, apar=apar, ae=ae):
                            sq.wait_ge(SVC, ae)
                            for ti in range(4):
                                sq.dma_start_transpose(
                                    attT[:, (tc % 2) * 4096 + ti * 1024 + dt * 128:
                                         (tc % 2) * 4096 + ti * 1024 + (dt + 1) * 128],
                                    attb[:, apar * 512 + ti * 128:
                                         apar * 512 + (ti + 1) * 128]
                                ).then_inc(STP, 16)
                        P.add("sync", ftr)
                        P.inc("s_tp", 64)
                    tr_all = P.val("s_tp")

                    if tc == 0 and "dbg4" in taps:
                        t_att = tap("attTv", [128, 4096], BF16)

                        def ftap4(sy, tr_all=tr_all, t_att=t_att):
                            sy.wait_ge(STP, tr_all)
                            sy.dma_start(out=t_att, in_=attT[:, 0:4096]).then_inc(SOUT, 16)
                        P.add("sync", ftap4)
                        P.inc("s_out", 16)

                    for ti in range(4):
                        tt = tc * 4 + ti
                        xpar = tt % 2
                        vc_prev = P.val("s_vc") if tt >= 2 else 0

                        def fxl(sy, tt=tt, xpar=xpar, vc_prev=vc_prev):
                            if vc_prev:
                                sy.wait_ge(SVC, vc_prev)
                            sy.dma_start(out=xfw[:, xpar * E:(xpar + 1) * E],
                                         in_=x_d[tt * 128:(tt + 1) * 128, :]
                                         ).then_inc(SX, 16)
                        P.add("sync", fxl)
                        xd = P.inc("s_x", 16)

                        def fx1(v, tc=tc, ti=ti, xpar=xpar, xd=xd, tr_all=tr_all):
                            v.wait_ge(SX, xd)
                            v.wait_ge(STP, tr_all)
                            v.tensor_tensor(x1f[:], xfw[:, xpar * E:(xpar + 1) * E],
                                            attT[:, (tc % 2) * 4096 + ti * 1024:
                                                 (tc % 2) * 4096 + (ti + 1) * 1024],
                                            ALU.add)
                            v.reduce_sum(lnw[:, 0:1], x1f[:], axis=AX)
                            v.tensor_tensor(lntw[:], x1f[:], x1f[:], ALU.mult)
                            v.reduce_sum(lnw[:, 1:2], lntw[:],
                                         axis=AX).then_inc(SVC, 1)
                        P.add("vector", fx1)
                        stv = P.inc("s_vc")

                        def fr2a(sc, stv=stv):
                            sc.wait_ge(SVC, stv)
                            sc.activation(lnw[:, 5:6], lnw[:, 0:1], ACT.Copy,
                                          bias=0.0, scale=1.0 / E)
                            sc.activation(lnw[:, 2:3], lnw[:, 0:1], ACT.Square,
                                          bias=0.0, scale=1.0 / E).then_inc(SSC, 1)
                        P.add("scalar", fr2a)
                        r2a = P.inc("s_sc")

                        def fem2(v, r2a=r2a):
                            v.wait_ge(SSC, r2a)
                            v.tensor_scalar(lnw[:, 3:4], lnw[:, 2:3], -1.0,
                                            epscol[:, 0:1], ALU.mult,
                                            ALU.add).then_inc(SVC, 1)
                        P.add("vector", fem2)
                        em2 = P.inc("s_vc")

                        def fr2b(sc, em2=em2):
                            sc.wait_ge(SVC, em2)
                            sc.activation(lnw[:, 3:4], lnw[:, 1:2], ACT.Ln,
                                          bias=lnw[:, 3:4], scale=1.0 / E)
                            sc.drain()
                            sc.activation(lnw[:, 4:5], lnw[:, 3:4], ACT.Exp,
                                          bias=0.0, scale=-0.5).then_inc(SSC, 1)
                        P.add("scalar", fr2b)
                        rs2 = P.inc("s_sc")
                        tp_prev = P.val("s_tp")

                        scr_rd = P.val("s_scr") if tt >= 2 else 0

                        def fn2(v, rs2=rs2, tp_prev=tp_prev, tt=tt, scr_rd=scr_rd):
                            v.wait_ge(SSC, rs2)
                            if tt >= 2:
                                v.wait_ge(STP, tp_prev)
                            if scr_rd:
                                v.wait_ge(SSCR, scr_rd)
                            v.tensor_scalar(xn2[:], x1f[:], lnw[:, 5:6],
                                            lnw[:, 4:5], ALU.subtract, ALU.mult)
                            v.tensor_copy(x1bf[:, (tt % 2) * E:(tt % 2 + 1) * E], x1f[:]).then_inc(SVC, 1)
                        P.add("vector", fn2)
                        n2 = P.inc("s_vc")

                        bpar = tt % 2
                        scr_prev = P.val("s_scr") if tt >= 2 else 0

                        def ftp2(sq, bpar=bpar, n2=n2, scr_prev=scr_prev):
                            sq.wait_ge(SVC, n2)
                            if scr_prev:
                                sq.wait_ge(SSCR, scr_prev)
                            for c in range(NCH):
                                sq.dma_start_transpose(
                                    h2blk[:, bpar * E + c * 128:bpar * E + (c + 1) * 128],
                                    xn2[:, c * 128:(c + 1) * 128]).then_inc(STP, 16)
                        P.add("scalar", ftp2)
                        tp2 = P.inc("s_tp", 16 * 8)

                        def faff(v, bpar=bpar, tp2=tp2):
                            v.wait_ge(STP, tp2)
                            for c in range(NCH):
                                ins = v.tensor_scalar(
                                    h2blk[:, bpar * E + c * 128:bpar * E + (c + 1) * 128],
                                    h2blk[:, bpar * E + c * 128:bpar * E + (c + 1) * 128],
                                    cbp1[:, c:c + 1], mcol(3, c), ALU.mult, ALU.add)
                                if c == NCH - 1:
                                    ins.then_inc(SVC, 1)
                        P.add("vector", faff)
                        aff = P.inc("s_vc")

                        def fsw(sy, tt=tt, bpar=bpar, aff=aff):
                            sy.wait_ge(SVC, aff)
                            sy.dma_start(
                                out=h2s[:, tt * 128:(tt + 1) * 128].rearrange(
                                    "(c p) t -> p c t", p=128),
                                in_=h2blk[:, bpar * E:(bpar + 1) * E].rearrange(
                                    "p (c t) -> p c t", c=NCH)).then_inc(SSCR, 16)
                            sy.dma_start(
                                out=x1s[tt * 128:(tt + 1) * 128, :],
                                in_=x1bf[:, (tt % 2) * E:(tt % 2 + 1) * E]).then_inc(SSCR, 16)
                        P.add("sync", fsw)
                        P.inc("s_scr", 32)
                if "dbg6" in taps:
                    t_wo2 = tap("woTv2", [128, NCH * E], BF16)

                    def ftap6(sy, n=P.val("s_scr"), t_wo2=t_wo2):
                        sy.wait_ge(SSCR, n)
                        sy.dma_start(out=t_wo2, in_=woT[:]).then_inc(SOUT, 16)
                    P.add("sync", ftap6)
                    P.inc("s_out", 16)
                wo_all = P.val("s_scr")

        with contextlib.ExitStack() as fst:
            fa_ = lambda n, sh, dt: fst.enter_context(nc.sbuf_tensor("sb_" + n, sh, dt))
            WFC = fa_("WFC", [128, NCH * FF], BF16)
            WPR = fa_("WPR", [128, 32 * E], BF16)
            H2R = fa_("H2R", [128, NCH * 512], BF16)
            A2 = fa_("A2", [128, 32 * 512], BF16)
            RL = fa_("RL", [128, 512], BF16)
            YB = fa_("YB", [128, 512], F32)
            YH = fa_("YH", [128, 512], F32)
            SG = [fa_(f"SG{i}", [128, 512], F32) for i in range(2)]
            ACCB = fa_("ACCB", [128, 512], F32)
            SNT = fa_("SNT", [128, 4096], BF16)
            SNB = RL
            GBR = fa_("GBR", [128, E], F32)
            X1T = fa_("X1T", [128, E], BF16)
            OUTB = fa_("OUTB", [128, E], F32)

            do_ffn = upto == "full"

            def ffw(gp, n=wo_all):
                gp.wait_ge(SSCR, n)
                gp.dma_start(out=WFC[:].rearrange("p (c f) -> p c f", c=NCH),
                             in_=wfcT_d.rearrange("(c p) f -> p c f", p=128)
                             ).then_inc(SW, 16)
                gp.dma_start(out=WPR[:].rearrange("p (f d) -> p f d", f=32),
                             in_=wprT_d.rearrange("(f p) d -> p f d", p=128)
                             ).then_inc(SW, 16)
            if do_ffn:
                P.add("gpsimd", ffw)
                ff_w = P.inc("s_w", 32)
            else:
                ff_w = 0

            vc_gate = P.val("s_vc")

            def fgb(t, n=wo_all, vc_gate=vc_gate):
                t.wait_ge(SVC, vc_gate)
                t.matmul(psG1[:], ONESF[:], gbrow[0:1, 0:512],
                         start=True, stop=True)
                t.matmul(psG2[:], ONESF[:], gbrow[0:1, 512:1024],
                         start=True, stop=True).then_inc(SMM, 1)
            if do_ffn:
                P.add("tensor", fgb)
                gbm = P.inc("s_mm")

                def fgbv(v, gbm=gbm):
                    v.wait_ge(SMM, gbm)
                    v.tensor_copy(GBR[:, 0:512], psG1[:])
                    v.tensor_copy(GBR[:, 512:1024], psG2[:]).then_inc(SVC, 1)
                P.add("vector", fgbv)
                P.inc("s_vc")
            else:
                vall = P.val("s_vc")
                sall = P.val("s_sc")
                mall = P.val("s_mm")
                scrall = P.val("s_scr")

                def fdz(v, vall=vall):
                    v.wait_ge(SVC, vall)
                    v.memset(OUTB[:], 0.0).then_inc(SVC, 1)
                P.add("vector", fdz)
                dz = P.inc("s_vc")

                def fdy(sy, dz=dz, sall=sall, mall=mall, scrall=scrall):
                    sy.wait_ge(SVC, dz)
                    sy.wait_ge(SSC, sall)
                    sy.wait_ge(SMM, mall)
                    if scrall:
                        sy.wait_ge(SSCR, scrall)
                    for tt in range(NTT):
                        sy.dma_start(out=y_d[tt * 128:(tt + 1) * 128, :],
                                     in_=OUTB[:]).then_inc(SOUT, 16)
                P.add("sync", fdy)
                P.inc("s_out", 16 * NTT)

            for tc in (range(NCH) if do_ffn else range(0)):
                hpar = 0
                h2need = P.val("s_mm") if tc >= 1 else 0

                def fh2(gp, tc=tc, hpar=hpar, h2need=h2need, n=wo_all):
                    if tc == 0:
                        gp.wait_ge(SSCR, n)
                    if h2need:
                        gp.wait_ge(SMM, h2need)
                    gp.dma_start(
                        out=H2R[:, hpar * NCH * 512:(hpar + 1) * NCH * 512
                                ].rearrange("p (c t) -> p c t", c=NCH),
                        in_=h2s[:, tc * 512:(tc + 1) * 512].rearrange(
                            "(c p) t -> p c t", p=128)).then_inc(SW, 16)
                P.add("gpsimd", fh2)
                h2r = P.inc("s_w", 16)

                rl_hist = []
                for ft in range(32):
                    fpar = ft % 4
                    sc_prev = rl_hist[ft - 4] if ft >= 4 else 0
                    ymm_prev = P.val("s_mm") if (tc >= 1 and ft == 0) else 0

                    def ffc(t, tc=tc, ft=ft, fpar=fpar, h2r=h2r, ff_w=ff_w,
                            hpar=hpar, sc_prev=sc_prev):
                        if ft == 0:
                            t.wait_ge(SW, max(h2r, ff_w))
                        if sc_prev:
                            t.wait_ge(SSC, sc_prev)
                        dstp = (psA0[:], psA1[:], psB0[:], psB1[:])[fpar]
                        for c in range(NCH):
                            ins = t.matmul(
                                dstp,
                                WFC[:, c * FF + ft * 128:c * FF + (ft + 1) * 128],
                                H2R[:, hpar * NCH * 512 + c * 512:
                                     hpar * NCH * 512 + (c + 1) * 512],
                                start=(c == 0), stop=(c == NCH - 1))
                            if c == NCH - 1:
                                ins.then_inc(SMM, 1)
                    P.add("tensor", ffc)
                    fmm = P.inc("s_mm")
                    vc_prev = P.val("s_vc") if ft >= 1 else 0

                    tp_need = P.val("s_tp") if (tc >= 1 and ft == 0) else 0

                    def frl(sc, fpar=fpar, fmm=fmm, vc_prev=vc_prev, tp_need=tp_need):
                        sc.wait_ge(SMM, fmm)
                        if vc_prev:
                            sc.wait_ge(SVC, vc_prev)
                        if tp_need:
                            sc.wait_ge(STP, tp_need)
                        dstp = (psA0[:], psA1[:], psB0[:], psB1[:])[fpar]
                        sc.activation(RL[:], dstp, ACT.Relu, bias=0.0,
                                      scale=1.0).then_inc(SSC, 1)
                    P.add("scalar", frl)
                    rl = P.inc("s_sc")
                    rl_hist.append(rl)
                    ym_prev = P.val("s_mm") if tc >= 1 and ft < 8 else 0

                    def fsq(v, ft=ft, rl=rl, ym_prev=ym_prev):
                        v.wait_ge(SSC, rl)
                        if ym_prev:
                            v.wait_ge(SMM, ym_prev)
                        v.tensor_tensor(A2[:, ft * 512:(ft + 1) * 512], RL[:],
                                        RL[:], ALU.mult).then_inc(SVC, 1)
                    P.add("vector", fsq)
                    P.inc("s_vc")
                a2done = P.val("s_vc")

                for dt in range(NCH):
                    opar = dt % 2
                    prevy = P.val("s_vc")

                    def fy2(t, dt=dt, a2done=a2done, prevy=prevy):
                        t.wait_ge(SVC, max(a2done, prevy))
                        for ft in range(32):
                            ins = t.matmul(
                                psG1[:],
                                WPR[:, ft * E + dt * 128:ft * E + (dt + 1) * 128],
                                A2[:, ft * 512:(ft + 1) * 512],
                                start=(ft == 0), stop=(ft == 31))
                            if ft == 31:
                                ins.then_inc(SMM, 1)
                    P.add("tensor", fy2)
                    ym = P.inc("s_mm")

                    def fs0(sc, ym=ym):
                        sc.wait_ge(SMM, ym)
                        sc.activation(SG[0][:], psG1[:], ACT.Sign,
                                      bias=negones[:, 0:1],
                                      scale=1.0).then_inc(SSC, 1)
                    P.add("scalar", fs0)
                    s0 = P.inc("s_sc")

                    def fv0(v, s0=s0):
                        v.wait_ge(SSC, s0)
                        v.tensor_copy(YB[:], psG1[:])
                        v.tensor_single_scalar(YH[:], YB[:], 0.5, ALU.subtract)
                        v.tensor_copy(ACCB[:], SG[0][:]).then_inc(SVC, 1)
                    P.add("vector", fv0)
                    P.inc("s_vc")

                    for st in range(4):
                        sprev = st % 2
                        snew = (st + 1) % 2

                        def fms(v, sprev=sprev):
                            v.scalar_tensor_tensor(YB[:], YB[:], 0.9, YH[:],
                                                   ALU.mult, ALU.add)
                            v.scalar_tensor_tensor(YB[:], SG[sprev][:], -0.5,
                                                   YB[:], ALU.mult,
                                                   ALU.add).then_inc(SVC, 1)
                        P.add("vector", fms)
                        mv = P.inc("s_vc")

                        def fss(sc, snew=snew, mv=mv):
                            sc.wait_ge(SVC, mv)
                            sc.activation(SG[snew][:], YB[:], ACT.Sign,
                                          bias=negones[:, 0:1],
                                          scale=1.0).then_inc(SSC, 1)
                        P.add("scalar", fss)
                        ss = P.inc("s_sc")

                        def facc(v, snew=snew, ss=ss):
                            v.wait_ge(SSC, ss)
                            v.tensor_tensor(ACCB[:], ACCB[:], SG[snew][:],
                                            ALU.add).then_inc(SVC, 1)
                        P.add("vector", facc)
                        P.inc("s_vc")
                    out_prev = P.val("s_out") if tc >= 1 and dt == 0 else 0

                    def fsnb(v, dt=dt, out_prev=out_prev):
                        v.tensor_scalar(SNB[:], ACCB[:], 0.1, halfcol[:, 0:1],
                                        ALU.mult, ALU.add).then_inc(SVC, 1)
                    P.add("vector", fsnb)
                    snb = P.inc("s_vc")

                    def ftsn(sq, dt=dt, snb=snb, out_prev=out_prev):
                        sq.wait_ge(SVC, snb)
                        if out_prev:
                            sq.wait_ge(SOUT, out_prev)
                        for ti in range(4):
                            sq.dma_start_transpose(
                                SNT[:, ti * 1024 + dt * 128:ti * 1024 + (dt + 1) * 128],
                                SNB[:, ti * 128:(ti + 1) * 128]).then_inc(STP, 16)
                    P.add("scalar", ftsn)
                    P.inc("s_tp", 64)
                snt_all = P.val("s_tp")

                for ti in range(4):
                    tt = tc * 4 + ti
                    vprev = P.val("s_vc") if tt >= 1 else 0
                    oprev = P.val("s_out") if tt >= 1 else 0

                    def fx1r(sy, tt=tt, vprev=vprev):
                        if vprev:
                            sy.wait_ge(SVC, vprev)
                        sy.dma_start(out=X1T[:],
                                     in_=x1s[tt * 128:(tt + 1) * 128, :]
                                     ).then_inc(SX, 16)
                    P.add("sync", fx1r)
                    x1r = P.inc("s_x", 16)

                    def fout(v, ti=ti, x1r=x1r, snt_all=snt_all, oprev=oprev):
                        v.wait_ge(SX, x1r)
                        v.wait_ge(STP, snt_all)
                        if oprev:
                            v.wait_ge(SOUT, oprev)
                        v.tensor_tensor(OUTB[:],
                                        SNT[:, ti * 1024:(ti + 1) * 1024],
                                        GBR[:], ALU.mult)
                        v.tensor_tensor(OUTB[:], OUTB[:], X1T[:],
                                        ALU.add).then_inc(SVC, 1)
                    P.add("vector", fout)
                    ov = P.inc("s_vc")

                    def fst_(sy, tt=tt, ov=ov):
                        sy.wait_ge(SVC, ov)
                        sy.dma_start(out=y_d[tt * 128:(tt + 1) * 128, :],
                                     in_=OUTB[:]).then_inc(SOUT, 16)
                    P.add("sync", fst_)
                    P.inc("s_out", 16)

        out_total = P.val("s_out")

        def ffinal(sy, out_total=out_total):
            sy.wait_ge(SOUT, out_total)
        P.add("sync", ffinal)

        with nc.Block() as block:
            P.emit(block)
    return nc


def host_inputs(inputs):
    f = np.ascontiguousarray
    x = np.asarray(inputs["L0_attn_x"])
    cc = np.asarray(inputs["L0_c"])
    shared = {
        "wmod": f(np.asarray(inputs["L0_wmod"])),
        "wqT": f(np.asarray(inputs["L0_attn_wq"]).T),
        "wkT": f(np.asarray(inputs["L0_attn_wk"]).T),
        "wvT": f(np.asarray(inputs["L0_attn_wv"]).T),
        "wgT": f(np.asarray(inputs["L0_attn_wg"]).T),
        "woT": f(np.asarray(inputs["L0_attn_wo"]).T),
        "wdT": f(np.asarray(inputs["L0_attn_wd"]).T),
        "bdrow": f(np.asarray(inputs["L0_attn_bd"]).reshape(1, H)),
        "bvrow": f(np.asarray(inputs["L0_attn_bv"]).reshape(1, E)),
        "wfcT": f(np.asarray(inputs["L0_ffn_W_fc"]).T),
        "wprT": f(np.asarray(inputs["L0_ffn_W_proj"]).T),
    }
    for nm, key in (("bq", "L0_attn_bq"), ("bk", "L0_attn_bk"),
                    ("bg", "L0_attn_bg"), ("bo", "L0_attn_bo"),
                    ("gamma", "L0_attn_gamma"), ("beta", "L0_attn_beta")):
        shared[nm + "col"] = f(np.asarray(inputs[key]).reshape(NCH, 128).T)
    maps = []
    for b in range(8):
        m = dict(shared)
        m["x"] = f(x[b])
        m["ccol"] = f(cc[b].reshape(NCH, 128).T)
        maps.append(m)
    return maps


_NC_CACHE = {}


def kernel(**inputs):
    taps = ()
    if taps not in _NC_CACHE:
        _NC_CACHE[taps] = build_nc(taps)
    nc = _NC_CACHE[taps]
    maps = host_inputs(inputs)
    res = bass_utils.run_bass_kernel_spmd(nc, maps, core_ids=list(range(8)))
    out = np.stack([res.results[b]["y"] for b in range(8)], axis=0)
    return out.astype(np.float32)


# version 10 (final)
# speedup vs baseline: 1.1047x; 1.0242x over previous
import contextlib
import numpy as np
import ml_dtypes
import concourse.bass as bass
import concourse.mybir as mybir
import concourse.bass_utils as bass_utils

F32 = mybir.dt.float32
BF16 = mybir.dt.bfloat16
AX = mybir.AxisListType.X
ALU = mybir.AluOpType
ACT = mybir.ActivationFunctionType

T = 4096
E = 1024
H = 16
HD = 64
NCH = 8
NTT = 32
NG = 8
NQC = 16
NJB = 8
FF = 4096
LN2C = float(np.log(2.0))


def consts_np():
    c = {}
    invf = (1.0 / 10000.0 ** (np.arange(0, HD, 2) / HD)).astype(np.float64)
    t = np.arange(T, dtype=np.float64)
    d = np.arange(HD)
    ang = t[None, :] * invf[d % 32][:, None]
    cos64, sin64 = np.cos(ang), np.sin(ang)
    c["COS"] = np.concatenate([cos64, cos64], 0).astype(ml_dtypes.bfloat16)
    c["SIN"] = np.concatenate([sin64, sin64], 0).astype(ml_dtypes.bfloat16)
    rot = np.zeros((HD, HD), np.float32)
    for i in range(HD // 2):
        rot[2 * i + 1, 2 * i] = -1.0
        rot[2 * i, 2 * i + 1] = 1.0
    z = np.zeros_like(rot)
    c["ROT"] = np.block([[rot, z], [z, rot]]).astype(ml_dtypes.bfloat16)
    kap = np.arange(128)[:, None]
    qr = np.arange(256)[None, :]
    c["DTAB"] = np.concatenate(
        [np.maximum(kap + 128 * dd - qr, 0) for dd in range(4)],
        axis=1).astype(np.float32)
    c["QREL"] = np.tile(np.arange(256, dtype=np.float32)[None, :], (65, 1))
    c["KP"] = (kap + 256 * np.arange(1, 16)[None, :]).astype(np.float32)
    c["ONESF"] = np.ones((1, 128), np.float32)
    return c


class Prog:
    def __init__(self, nc):
        self.nc = nc
        self.ops = []
        self.sem = {}

    def new_sem(self, stack, name):
        h = stack.enter_context(self.nc.semaphore(name))
        self.sem[name] = [h, 0]

    def inc(self, name, n=1):
        s = self.sem[name]
        s[1] += n
        return s[1]

    def val(self, name):
        return self.sem[name][1]

    def hdl(self, name):
        return self.sem[name][0]

    def add(self, eng, fn):
        self.ops.append((eng, fn))

    def emit(self, block):
        for eng_name in ("sync", "gpsimd", "scalar", "vector", "tensor"):
            ops = [f for e, f in self.ops if e == eng_name]

            def body(eng, ops=ops):
                for f in ops:
                    f(eng)
            getattr(block, eng_name)(body)


def build_nc(taps=(), upto="full", ngroups=NG):
    nc = bass.Bass("TRN2", target_bir_lowering=False)
    CN = consts_np()

    x_d = nc.dram_tensor("x", [T, E], F32, kind="ExternalInput").ap()
    y_d = nc.dram_tensor("y", [T, E], F32, kind="ExternalOutput").ap()
    ccol_d = nc.dram_tensor("ccol", [128, NCH], F32, kind="ExternalInput").ap()
    wmod_d = nc.dram_tensor("wmod", [E, 6 * E], F32, kind="ExternalInput").ap()
    wT_d = {w: nc.dram_tensor(w + "T", [E, E], F32, kind="ExternalInput").ap()
            for w in ("wq", "wk", "wv", "wg", "wo")}
    wdT_d = nc.dram_tensor("wdT", [E, H], F32, kind="ExternalInput").ap()
    bd_d = nc.dram_tensor("bdrow", [1, H], F32, kind="ExternalInput").ap()
    vec_d = {v: nc.dram_tensor(v + "col", [128, NCH], F32, kind="ExternalInput").ap()
             for v in ("bq", "bk", "bg", "bo", "gamma", "beta")}
    bvrow_d = nc.dram_tensor("bvrow", [1, E], F32, kind="ExternalInput").ap()
    wfcT_d = nc.dram_tensor("wfcT", [E, FF], F32, kind="ExternalInput").ap()
    wprT_d = nc.dram_tensor("wprT", [FF, E], F32, kind="ExternalInput").ap()

    def scr_t(name, shape, dt):
        kind = "ExternalOutput" if name in taps else "Internal"
        return nc.dram_tensor(name, shape, dt, kind=kind).ap()

    g2s = scr_t("g2s", [E, T], BF16)
    x1s = scr_t("x1s", [T, E], BF16)
    h2s = scr_t("h2s", [E, T], BF16)

    tap_d = {}

    def tap(name, shape, dt):
        tap_d[name] = nc.dram_tensor(name, shape, dt, kind="ExternalOutput").ap()
        return tap_d[name]

    cst = {k: nc.inline_tensor(v, name="c_" + k).ap() for k, v in CN.items()}

    P = Prog(nc)
    stack = contextlib.ExitStack()
    with stack:
        for s in ("s_x", "s_w", "s_tp", "s_scr", "s_out",
                  "s_mm", "s_sc", "s_vc", "s_av"):
            P.new_sem(stack, s)
        SX, SW, STP, SSCR, SOUT = (P.hdl(s) for s in
                                   ("s_x", "s_w", "s_tp", "s_scr", "s_out"))
        SMM, SSC, SVC, SAV = (P.hdl(s) for s in ("s_mm", "s_sc", "s_vc", "s_av"))

        psA0 = stack.enter_context(nc.psum_tensor("psA0", [128, 512], F32))
        psA1 = stack.enter_context(nc.psum_tensor("psA1", [128, 512], F32))
        psB0 = stack.enter_context(nc.psum_tensor("psB0", [128, 512], F32))
        psB1 = stack.enter_context(nc.psum_tensor("psB1", [128, 512], F32))
        psSA = (psA0, psA1)
        psSB = (psB0, psB1)
        psOA = stack.enter_context(nc.psum_tensor("psOA", [128, 512], F32))
        psOB = stack.enter_context(nc.psum_tensor("psOB", [128, 512], F32))
        psG1 = stack.enter_context(nc.psum_tensor("psG1", [128, 512], F32))
        psG2 = stack.enter_context(nc.psum_tensor("psG2", [128, 512], F32))

        sb = lambda n, sh, dt: stack.enter_context(nc.sbuf_tensor("sb_" + n, sh, dt))
        ccol = sb("ccol", [128, NCH], F32)
        pvec = {v: sb(v, [128, NCH], F32) for v in
                ("bq", "bk", "bg", "bo", "gamma", "beta")}
        bvrow = sb("bvrow", [1, E], F32)
        wdT = sb("wdT", [128, NCH * H], F32)
        bdrow = sb("bdrow", [1, H], F32)
        ONESF = sb("ONESF", [1, 128], F32)
        onescol = sb("onescol", [128, 1], F32)
        epscol = sb("epscol", [128, 1], F32)
        negones = sb("negones", [128, 1], F32)
        halfcol = sb("halfcol", [128, 1], F32)
        n5l2 = sb("n5l2", [1, 1], F32)
        modT = sb("modT", [128, 48], F32)
        gbrow = sb("gbrow", [1, E], F32)
        cap1 = sb("cap1", [128, NCH], F32)
        cbp1 = sb("cbp1", [128, NCH], F32)
        gacl = sb("gacl", [128, NCH], F32)
        gbcl = sb("gbcl", [128, NCH], F32)
        negbg = sb("negbg", [128, NCH], F32)
        halfbg = sb("halfbg", [128, NCH], F32)
        hmean = sb("hmean", [128, NCH], F32)
        ldrow = sb("ldrow", [1, H], F32)
        pdtmp = sb("pdtmp", [1, H], F32)
        smallp = sb("smallp", [128, 16], F32)
        silu_c = sb("silu_c", [128, NCH], BF16)

        def mcol(vi, c):
            return modT[:, vi * 8 + c: vi * 8 + c + 1]

        def f0(sy):
            for name, dst in (("COS", COS), ("SIN", SIN), ("ROT", ROTm),
                              ("DTAB", DTAB), ("QREL", QREL), ("KP", KP),
                              ("ONESF", ONESF)):
                sy.dma_start(out=dst[:], in_=cst[name]).then_inc(SX, 16)
            sy.dma_start(out=ccol[:], in_=ccol_d).then_inc(SX, 16)
            for v in pvec:
                sy.dma_start(out=pvec[v][:], in_=vec_d[v]).then_inc(SX, 16)
            sy.dma_start(out=bvrow[:], in_=bvrow_d).then_inc(SX, 16)
            sy.dma_start(out=bdrow[:], in_=bd_d).then_inc(SX, 16)
            for c in range(NCH):
                sy.dma_start(out=wdT[:, c * H:(c + 1) * H],
                             in_=wdT_d[c * 128:(c + 1) * 128, :]).then_inc(SX, 16)
        P.add("sync", f0)
        ld_done = P.inc("s_x", 16 * (7 + 1 + 6 + 2 + NCH))

        def f0v(v):
            v.memset(onescol[:], 1.0)
            v.memset(epscol[:], 1e-5)
            v.memset(negones[:], -1.0)
            v.memset(halfcol[:], 0.5)
            v.memset(n5l2[:], -5.0 * LN2C).then_inc(SVC, 1)
        P.add("vector", f0v)
        P.inc("s_vc")

        def f0s(sc):
            sc.wait_ge(SX, ld_done)
            sc.activation(smallp[:, 0:8], ccol[:], ACT.Exp, bias=0.0,
                          scale=-1.0).then_inc(SSC, 1)
        P.add("scalar", f0s)
        e_done = P.inc("s_sc")

        def f0v2(v):
            v.wait_ge(SSC, e_done)
            v.tensor_scalar(smallp[:, 0:8], smallp[:, 0:8], 1.0, None, ALU.add)
            v.drain()
            v.reciprocal(smallp[:, 8:16], smallp[:, 0:8])
            v.drain()
            v.tensor_tensor(silu_c[:], ccol[:], smallp[:, 8:16],
                            ALU.mult).then_inc(SVC, 1)
        P.add("vector", f0v2)
        silu_done = P.inc("s_vc")

        if "dbg0" in taps:
            t_cc = tap("ccolv", [128, NCH], F32)
            t_sp = tap("smallpv", [128, 16], F32)
            t_si = tap("siluv", [128, NCH], BF16)

            def ftap0(sy, n=silu_done, t_cc=t_cc, t_sp=t_sp, t_si=t_si):
                sy.wait_ge(SVC, n)
                sy.dma_start(out=t_cc, in_=ccol[:]).then_inc(SOUT, 16)
                sy.dma_start(out=t_sp, in_=smallp[:]).then_inc(SOUT, 16)
                sy.dma_start(out=t_si, in_=silu_c[:]).then_inc(SOUT, 16)
            P.add("sync", ftap0)
            P.inc("s_out", 48)

        with (
            nc.sbuf_tensor("sb_hT", [128, NCH * T], BF16) as hT,
            contextlib.ExitStack() as gstk,
            contextlib.ExitStack() as p1s,
        ):
            ga_ = lambda n, sh, dt: gstk.enter_context(nc.sbuf_tensor("sb_" + n, sh, dt))
            COS = ga_("COS", [128, T], BF16)
            SIN = ga_("SIN", [128, T], BF16)
            ROTm = ga_("ROTm", [128, 128], BF16)
            DTAB = ga_("DTAB", [128, 1024], F32)
            QREL = ga_("QREL", [65, 256], F32)
            KP = ga_("KP", [128, 15], F32)
            a1 = lambda n, sh, dt: p1s.enter_context(nc.sbuf_tensor("sb_" + n, sh, dt))
            xf = a1("xf", [128, 2 * E], F32)
            xn = a1("xn", [128, 2 * E], BF16)
            lntmp = a1("lntmp", [128, E], F32)
            lnv = a1("lnv", [128, 8], F32)
            wmst = a1("wmst", [128, 2 * 3072], BF16)
            modrow = a1("modrow", [1, 6 * E], F32)
            gbtmp = a1("gbtmp", [1, E], F32)

            slots6 = [psA0[:], psA1[:], psB0[:],
                      psB1[:], psOA[:, 0:512], psOB[:, 0:512]]
            for half in range(2):
                off = half * 3072
                for c in range(NCH):
                    par = c % 2
                    dst = wmst[:, par * 3072:(par + 1) * 3072]
                    mm_need = P.val("s_mm")

                    def fw(g, dst=dst, c=c, off=off, mm_need=mm_need):
                        if mm_need:
                            g.wait_ge(SMM, mm_need)
                        g.dma_start(out=dst,
                                    in_=wmod_d[c * 128:(c + 1) * 128, off:off + 3072]
                                    ).then_inc(SW, 16)
                    P.add("gpsimd", fw)
                    w_done = P.inc("s_w", 16)
                    ev_prev = P.val("s_vc")

                    def fm(t, c=c, par=par, w_done=w_done, ev_prev=ev_prev,
                           silu_done=silu_done):
                        t.wait_ge(SW, w_done)
                        t.wait_ge(SVC, max(silu_done, ev_prev if c == 0 else 0))
                        for n in range(6):
                            ins = t.matmul(
                                slots6[n][0:1, :], silu_c[:, c:c + 1],
                                wmst[:, par * 3072 + n * 512:par * 3072 + (n + 1) * 512],
                                start=(c == 0), stop=(c == NCH - 1))
                            if n == 5:
                                ins.then_inc(SMM, 1)
                    P.add("tensor", fm)
                    P.inc("s_mm")
                mm_done = P.val("s_mm")

                def fe(v, off=off, mm_done=mm_done):
                    v.wait_ge(SMM, mm_done)
                    for n in range(6):
                        ins = v.tensor_copy(modrow[:, off + n * 512:off + (n + 1) * 512],
                                            slots6[n][0:1, :])
                        if n == 5:
                            ins.then_inc(SVC, 1)
                P.add("vector", fe)
                P.inc("s_vc")
            ev_done = P.val("s_vc")

            def fmt(t, ev_done=ev_done):
                t.wait_ge(SVC, ev_done)
                for n in range(48):
                    ins = t.matmul(psG1[:, n:n + 1],
                                   modrow[0:1, n * 128:(n + 1) * 128],
                                   ONESF[0:1, 0:1], start=True, stop=True)
                    if n == 47:
                        ins.then_inc(SMM, 1)
            P.add("tensor", fmt)
            mt_done = P.inc("s_mm")

            def fmod(v, mt_done=mt_done):
                v.wait_ge(SMM, mt_done)
                v.tensor_copy(modT[:], psG1[:, 0:48])
                v.drain()
                v.tensor_scalar(cap1[:], modT[:, 8:16], 1.0, None, ALU.add)
                v.tensor_scalar(cbp1[:], modT[:, 32:40], 1.0, None, ALU.add)
                v.drain()
                for src, dstv in ((modT[:, 16:24], gacl), (modT[:, 40:48], gbcl)):
                    v.tensor_single_scalar(smallp[:, 0:8], src, 0.0, ALU.is_ge)
                    v.drain()
                    v.tensor_scalar(smallp[:, 0:8], smallp[:, 0:8], 2.0,
                                    negones[:, 0:1], ALU.mult, ALU.add)
                    v.drain()
                    v.tensor_single_scalar(smallp[:, 8:16], src, -1.0, ALU.mult)
                    v.drain()
                    v.tensor_tensor(smallp[:, 8:16], smallp[:, 8:16], src, ALU.max)
                    v.drain()
                    v.tensor_single_scalar(smallp[:, 8:16], smallp[:, 8:16],
                                           0.25, ALU.max)
                    v.drain()
                    v.tensor_tensor(dstv[:], smallp[:, 0:8], smallp[:, 8:16],
                                    ALU.mult)
                    v.drain()
                v.tensor_single_scalar(negbg[:], pvec["bg"][:], -1.0,
                                       ALU.mult)
                v.tensor_single_scalar(halfbg[:], pvec["bg"][:], 0.5, ALU.mult)
                src_r = modrow[0:1, 5 * E:6 * E]
                v.tensor_single_scalar(gbrow[:], src_r, 0.0, ALU.is_ge)
                v.drain()
                v.tensor_scalar(gbrow[:], gbrow[:], 2.0, negones[0:1, 0:1],
                                ALU.mult, ALU.add)
                v.drain()
                v.tensor_single_scalar(gbtmp[:], src_r, -1.0, ALU.mult)
                v.drain()
                v.tensor_tensor(gbtmp[:], gbtmp[:], src_r, ALU.max)
                v.drain()
                v.tensor_single_scalar(gbtmp[:], gbtmp[:], 0.25, ALU.max)
                v.drain()
                v.tensor_tensor(gbrow[:], gbrow[:], gbtmp[:],
                                ALU.mult).then_inc(SVC, 1)
            P.add("vector", fmod)
            mod_done = P.inc("s_vc")

            for tt in range(NTT):
                par = tt % 2
                xfp = xf[:, par * E:(par + 1) * E]
                xnp = xn[:, par * E:(par + 1) * E]
                vc_need = P.val("s_vc") if tt >= 2 else 0

                def fl(sy, tt=tt, xfp=xfp, vc_need=vc_need):
                    if vc_need:
                        sy.wait_ge(SVC, vc_need)
                    sy.dma_start(out=xfp, in_=x_d[tt * 128:(tt + 1) * 128, :]
                                 ).then_inc(SX, 16)
                P.add("sync", fl)
                x_done = P.inc("s_x", 16)

                def fln(v, xfp=xfp, x_done=x_done, tt=tt):
                    v.wait_ge(SX, x_done)
                    v.reduce_sum(lnv[:, 0:1], xfp, axis=AX)
                    v.tensor_tensor(lntmp[:], xfp, xfp, ALU.mult)
                    v.reduce_sum(lnv[:, 1:2], lntmp[:], axis=AX).then_inc(SVC, 1)
                P.add("vector", fln)
                st_done = P.inc("s_vc")

                def frs_a(sc, st_done=st_done):
                    sc.wait_ge(SVC, st_done)
                    sc.activation(lnv[:, 5:6], lnv[:, 0:1], ACT.Copy,
                                  bias=0.0, scale=1.0 / E)
                    sc.activation(lnv[:, 2:3], lnv[:, 0:1], ACT.Square,
                                  bias=0.0, scale=1.0 / E).then_inc(SSC, 1)
                P.add("scalar", frs_a)
                rsa = P.inc("s_sc")

                def fem(v, rsa=rsa):
                    v.wait_ge(SSC, rsa)
                    v.tensor_scalar(lnv[:, 3:4], lnv[:, 2:3], -1.0,
                                    epscol[:, 0:1], ALU.mult,
                                    ALU.add).then_inc(SVC, 1)
                P.add("vector", fem)
                emv = P.inc("s_vc")

                def frs_b(sc, emv=emv):
                    sc.wait_ge(SVC, emv)
                    sc.activation(lnv[:, 3:4], lnv[:, 1:2], ACT.Ln,
                                  bias=lnv[:, 3:4], scale=1.0 / E)
                    sc.drain()
                    sc.activation(lnv[:, 4:5], lnv[:, 3:4], ACT.Exp,
                                  bias=0.0, scale=-0.5).then_inc(SSC, 1)
                P.add("scalar", frs_b)
                rs_done = P.inc("s_sc")
                tp_prev = P.val("s_tp") if tt >= 2 else 0

                def fxn(v, xfp=xfp, xnp=xnp, rs_done=rs_done, tp_prev=tp_prev):
                    v.wait_ge(SSC, rs_done)
                    if tp_prev:
                        v.wait_ge(STP, tp_prev)
                    v.tensor_scalar(xnp, xfp, lnv[:, 5:6], lnv[:, 4:5],
                                    ALU.subtract, ALU.mult).then_inc(SVC, 1)
                P.add("vector", fxn)
                xn_done = P.inc("s_vc")

                def ftp(sq, tt=tt, xnp=xnp, xn_done=xn_done):
                    sq.wait_ge(SVC, xn_done)
                    for c in range(NCH):
                        sq.dma_start_transpose(
                            hT[:, c * T + tt * 128:c * T + (tt + 1) * 128],
                            xnp[:, c * 128:(c + 1) * 128]).then_inc(STP, 16)
                P.add("scalar", ftp)
                P.inc("s_tp", 16 * NCH)
            tp_all = P.val("s_tp")

            for c in range(NCH):
                def fa(v, c=c, tp_all=tp_all):
                    if c == 0:
                        v.wait_ge(STP, tp_all)
                    v.tensor_scalar(hT[:, c * T:(c + 1) * T],
                                    hT[:, c * T:(c + 1) * T],
                                    cap1[:, c:c + 1], mcol(0, c),
                                    ALU.mult, ALU.add)
                    v.reduce_sum(hmean[:, c:c + 1], hT[:, c * T:(c + 1) * T], axis=AX)
                    v.drain()
                    v.tensor_single_scalar(hmean[:, c:c + 1], hmean[:, c:c + 1],
                                           1.0 / T, ALU.mult).then_inc(SVC, 1)
                P.add("vector", fa)
                P.inc("s_vc")
            h_done = P.val("s_vc")

            if "hT" in taps:
                tpp = tap("hT", [128, NCH * T], BF16)

                def ftap(sy, n=h_done, tpp=tpp):
                    sy.wait_ge(SVC, n)
                    sy.dma_start(out=tpp, in_=hT[:]).then_inc(SOUT, 16)
                P.add("sync", ftap)
                P.inc("s_out", 16)

            def fpd_mm(t, h_done=h_done):
                t.wait_ge(SVC, h_done)
                for c in range(NCH):
                    ins = t.matmul(psG1[0:1, 0:H], hmean[:, c:c + 1],
                                   wdT[:, c * H:(c + 1) * H],
                                   start=(c == 0), stop=(c == NCH - 1))
                    if c == NCH - 1:
                        ins.then_inc(SMM, 1)
            P.add("tensor", fpd_mm)
            pdm = P.inc("s_mm")

            def fz(v, pdm=pdm):
                v.wait_ge(SMM, pdm)
                v.tensor_tensor(pdtmp[:], psG1[0:1, 0:H], bdrow[:],
                                ALU.add).then_inc(SVC, 1)
            P.add("vector", fz)
            z_done = P.inc("s_vc")

            def fpe(sc, z_done=z_done):
                sc.wait_ge(SVC, z_done)
                sc.activation(ldrow[:], pdtmp[:], ACT.Exp, bias=0.0,
                              scale=-1.0).then_inc(SSC, 1)
            P.add("scalar", fpe)
            pe_done = P.inc("s_sc")

            def fsg(v, pe_done=pe_done):
                v.wait_ge(SSC, pe_done)
                v.tensor_scalar(ldrow[:], ldrow[:], 1.0, None, ALU.add)
                v.drain()
                v.reciprocal(pdtmp[:], ldrow[:])
                v.drain()
                v.nop().then_inc(SVC, 1)
            P.add("vector", fsg)
            pd_done = P.inc("s_vc")

            def fu(sc, pd_done=pd_done):
                sc.wait_ge(SVC, pd_done)
                sc.activation(ldrow[:], pdtmp[:], ACT.Exp, bias=n5l2[0:1, 0:1],
                              scale=-8.0 * LN2C).then_inc(SSC, 1)
            P.add("scalar", fu)
            u_done = P.inc("s_sc")

            def fv2(v, u_done=u_done):
                v.wait_ge(SSC, u_done)
                v.tensor_scalar(ldrow[:], ldrow[:], -1.0, onescol[0:1, 0:1],
                                ALU.mult, ALU.add).then_inc(SVC, 1)
            P.add("vector", fv2)
            v_done = P.inc("s_vc")

            def fl2(sc, v_done=v_done):
                sc.wait_ge(SVC, v_done)
                sc.activation(ldrow[:], ldrow[:], ACT.Ln, bias=0.0,
                              scale=1.0).then_inc(SSC, 1)
            P.add("scalar", fl2)
            ld_chain = P.inc("s_sc")

            if "ld" in taps:
                tpp = tap("ld", [1, H], F32)

                def ftap(sy, n=ld_chain, tpp=tpp):
                    sy.wait_ge(SSC, n)
                    sy.dma_start(out=tpp, in_=ldrow[:]).then_inc(SOUT, 16)
                P.add("sync", ftap)
                P.inc("s_out", 16)
            if "dbg1" in taps:
                t_mod = tap("modT", [128, 48], F32)
                t_hm = tap("hmean", [128, NCH], F32)
                t_pd = tap("pdt", [1, H], F32)
                t_sc = tap("siluc", [128, NCH], BF16)

                def ftap2(sy, n=ld_chain):
                    sy.wait_ge(SSC, n)
                    sy.dma_start(out=t_mod, in_=modT[:]).then_inc(SOUT, 16)
                    sy.dma_start(out=t_hm, in_=hmean[:]).then_inc(SOUT, 16)
                    sy.dma_start(out=t_pd, in_=pdtmp[:]).then_inc(SOUT, 16)
                    sy.dma_start(out=t_sc, in_=silu_c[:]).then_inc(SOUT, 16)
                P.add("sync", ftap2)
                P.inc("s_out", 64)

            p1s.close()

            NPAR = 1
            with contextlib.ExitStack() as ast:
                aa = lambda n, sh, dt: ast.enter_context(nc.sbuf_tensor("sb_" + n, sh, dt))
                QT = [aa(f"QT{i}", [128, T], BF16) for i in range(NPAR)]
                KT = [aa(f"KT{i}", [128, T], BF16) for i in range(NPAR)]
                GT = [aa(f"GT{i}", [128, T], BF16) for i in range(NPAR)]
                VT = [aa(f"VT{i}", [128, 2 * 48 * 65], BF16) for i in range(NPAR)]
                WSL = aa("WSL", [128, 4 * E], BF16)
                UB = [aa(f"UB{i}", [128, 2 * 1024], BF16) for i in range(2)]
                Opack = aa("Opack", [128, T], BF16)
                DensA = aa("DensA", [1, T], F32)
                DensB = aa("DensB", [1, T], F32)
                tmp65 = aa("tmp65", [65, 256], F32)
                g2b = aa("g2b", [128, 2 * 512], BF16)
                EB = [aa(f"EB{i}", [128, 2 * 2 * 256], BF16) for i in range(NPAR)]
                Bb = [aa(f"Bb{i}", [65, 2 * 256], F32) for i in range(NPAR)]
                BV = [aa(f"BV{i}", [128, 2 * 15], F32) for i in range(NPAR)]
                ldbc = [aa(f"ldbc{i}", [128, 2], F32) for i in range(NPAR)]
                facm = [aa(f"facm{i}", [128, 2], F32) for i in range(NPAR)]
                rtmp = aa("rtmp", [128, 512], F32)
                qraw = aa("qraw", [128, 512], BF16)
                gtmp = aa("gtmp", [128, 512], F32)
                gtmp2 = aa("gtmp2", [128, 512], BF16)
                bvrep = aa("bvrep", [128, 128], F32)
                ebt = aa("ebt", [128, 256], F32)
                stat = aa("stat", [128, 36], F32)
                coef = aa("coef", [128, 2], F32)
                rdenA_t = aa("rdenA", [1, 256], F32)
                rdenB_t = aa("rdenB", [1, 256], F32)
                lnden_t = aa("lnden", [1, 256], F32)
                rdenA = rdenA_t[:]
                rdenB = rdenB_t[:]
                scl1 = aa("scl1", [1, 8], F32)

                def gen_group(g, par):
                    av_need = P.val("s_av")

                    def fw(gp, g=g, av_need=av_need, ld_chain=ld_chain):
                        if g == 0:
                            gp.wait_ge(SSC, ld_chain)
                        if av_need:
                            gp.wait_ge(SAV, av_need)
                        for wi, w in enumerate(("wq", "wk", "wv", "wg")):
                            for c in range(NCH):
                                gp.dma_start(
                                    out=WSL[:, wi * E + c * 128:wi * E + (c + 1) * 128],
                                    in_=wT_d[w][c * 128:(c + 1) * 128,
                                                g * 128:(g + 1) * 128]).then_inc(SW, 16)
                    P.add("gpsimd", fw)
                    w_done = P.inc("s_w", 16 * 32)
                    vc_prev = P.val("s_vc")

                    def fld(t, g=g, vc_prev=vc_prev, ld_chain=ld_chain):
                        t.wait_ge(SSC, ld_chain)
                        t.wait_ge(SVC, vc_prev)
                        for hh in range(2):
                            ins = t.matmul(psG2[:, hh:hh + 1], ONESF[:],
                                           ldrow[0:1, 2 * g + hh:2 * g + hh + 1],
                                           start=True, stop=True)
                            if hh == 1:
                                ins.then_inc(SMM, 1)
                    P.add("tensor", fld)
                    ldm = P.inc("s_mm")

                    def fldv(v, par=par, ldm=ldm):
                        v.wait_ge(SMM, ldm)
                        v.tensor_copy(ldbc[par][:], psG2[:, 0:2]).then_inc(SVC, 1)
                    P.add("vector", fldv)
                    ldv = P.inc("s_vc")

                    for hh in range(2):
                        sc_tab = P.val("s_sc")

                        def ft_v(v, par=par, hh=hh, ldv=ldv, sc_tab=sc_tab):
                            if hh == 0:
                                v.wait_ge(SVC, ldv)
                            else:
                                v.wait_ge(SSC, sc_tab)
                            v.tensor_scalar(BV[par][:, hh * 15:(hh + 1) * 15],
                                            KP[:], ldbc[par][:, hh:hh + 1], None,
                                            ALU.mult)
                            v.tensor_single_scalar(coef[:, 0:1],
                                                   ldbc[par][:, hh:hh + 1], -1.0,
                                                   ALU.mult)
                            v.drain()
                            v.tensor_scalar(tmp65[:], QREL[:], coef[0:65, 0:1],
                                            None, ALU.mult).then_inc(SVC, 1)
                        P.add("vector", ft_v)
                        tv = P.inc("s_vc")

                        def ft_s(sc, par=par, hh=hh, tv=tv):
                            sc.wait_ge(SVC, tv)
                            sc.activation(Bb[par][:, hh * 256:(hh + 1) * 256],
                                          tmp65[:], ACT.Exp, bias=0.0, scale=1.0)
                            sc.activation(facm[par][:, hh:hh + 1],
                                          ldbc[par][:, hh:hh + 1], ACT.Exp,
                                          bias=0.0,
                                          scale=128.0).then_inc(SSC, 1)
                        P.add("scalar", ft_s)
                        P.inc("s_sc")
                        for dd in range(2):
                            sc_need = P.val("s_sc")

                            def feb_v(v, par=par, hh=hh, dd=dd, sc_need=sc_need):
                                if dd > 0 or hh > 0:
                                    v.wait_ge(SSC, sc_need)
                                v.tensor_scalar(ebt[:], DTAB[:, dd * 256:(dd + 1) * 256],
                                                ldbc[par][:, hh:hh + 1], None,
                                                ALU.mult).then_inc(SVC, 1)
                            P.add("vector", feb_v)
                            ebv = P.inc("s_vc")

                            def feb_s(sc, par=par, hh=hh, dd=dd, ebv=ebv):
                                sc.wait_ge(SVC, ebv)
                                sc.activation(
                                    EB[par][:, (hh * 2 + dd) * 256:(hh * 2 + dd + 1) * 256],
                                    ebt[:], ACT.Exp, bias=0.0,
                                    scale=1.0).then_inc(SSC, 1)
                            P.add("scalar", feb_s)
                            P.inc("s_sc")

                    def fbv(t, g=g):
                        t.matmul(psG1[:, 0:128], ONESF[:],
                                 bvrow[0:1, g * 128:(g + 1) * 128],
                                 start=True, stop=True).then_inc(SMM, 1)
                    P.add("tensor", fbv)
                    bvm = P.inc("s_mm")

                    def fbvv(v, bvm=bvm, par=par):
                        v.wait_ge(SMM, bvm)
                        v.tensor_copy(bvrep[:], psG1[:, 0:128])
                        v.memset(VT[par][:], 1.0).then_inc(SVC, 1)
                    P.add("vector", fbvv)
                    bvv = P.inc("s_vc")

                    for wi, dstl in ((0, QT), (1, KT), (3, GT)):
                        for n in range(8):
                            ev_prev = P.val("s_vc")

                            def fp(t, wi=wi, n=n, w_done=w_done, h_done=h_done,
                                   ev_prev=ev_prev):
                                t.wait_ge(SW, w_done)
                                t.wait_ge(SVC, ev_prev)
                                for c in range(NCH):
                                    ins = t.matmul(
                                        psG1[:],
                                        WSL[:, wi * E + c * 128:wi * E + (c + 1) * 128],
                                        hT[:, c * T + n * 512:c * T + (n + 1) * 512],
                                        start=(c == 0), stop=(c == NCH - 1))
                                    if c == NCH - 1:
                                        ins.then_inc(SMM, 1)
                            P.add("tensor", fp)
                            pm = P.inc("s_mm")

                            if wi < 2:
                                bias = pvec["bq"] if wi == 0 else pvec["bk"]

                                def fev(v, g=g, pm=pm, bias=bias):
                                    v.wait_ge(SMM, pm)
                                    v.tensor_scalar(qraw[:], psG1[:],
                                                    bias[:, g:g + 1], None,
                                                    ALU.add).then_inc(SVC, 1)
                                P.add("vector", fev)
                                ev = P.inc("s_vc")

                                def frot(t, ev=ev):
                                    t.wait_ge(SVC, ev)
                                    t.matmul(psG2[:], ROTm[:], qraw[:],
                                             start=True, stop=True).then_inc(SMM, 1)
                                P.add("tensor", frot)
                                rm = P.inc("s_mm")

                                def fcomb(v, par=par, n=n, dstl=dstl, rm=rm):
                                    v.wait_ge(SMM, rm)
                                    dst = dstl[par][:, n * 512:(n + 1) * 512]
                                    v.tensor_tensor(dst, qraw[:],
                                                    COS[:, n * 512:(n + 1) * 512],
                                                    ALU.mult)
                                    v.tensor_tensor(rtmp[:], psG2[:],
                                                    SIN[:, n * 512:(n + 1) * 512],
                                                    ALU.mult)
                                    v.tensor_tensor(dst, dst, rtmp[:],
                                                    ALU.add).then_inc(SVC, 1)
                                P.add("vector", fcomb)
                                P.inc("s_vc")
                            else:
                                def fge(sc, g=g, pm=pm):
                                    sc.wait_ge(SMM, pm)
                                    sc.activation(gtmp2[:], psG1[:], ACT.Tanh,
                                                  bias=halfbg[:, g:g + 1],
                                                  scale=0.5).then_inc(SSC, 1)
                                P.add("scalar", fge)
                                ge = P.inc("s_sc")

                                def fgv(v, par=par, n=n, g=g, ge=ge):
                                    v.wait_ge(SSC, ge)
                                    v.tensor_scalar(gtmp[:], gtmp2[:], 0.5,
                                                    halfcol[:, 0:1], ALU.mult,
                                                    ALU.add)
                                    v.scalar_tensor_tensor(
                                        GT[par][:, n * 512:(n + 1) * 512],
                                        psG1[:], pvec["bg"][:, g:g + 1], gtmp[:],
                                        ALU.add, ALU.mult).then_inc(SVC, 1)
                                P.add("vector", fgv)
                                P.inc("s_vc")

                    for tt in range(NTT):
                        ev_prev = P.val("s_vc")

                        def fvm(t, tt=tt, w_done=w_done, ev_prev=ev_prev):
                            if tt == 0:
                                t.wait_ge(SW, w_done)
                            t.wait_ge(SVC, ev_prev)
                            for c in range(NCH):
                                ins = t.matmul(
                                    psG2[:, 0:128],
                                    hT[:, c * T + tt * 128:c * T + (tt + 1) * 128],
                                    WSL[:, 2 * E + c * 128:2 * E + (c + 1) * 128],
                                    start=(c == 0), stop=(c == NCH - 1))
                                if c == NCH - 1:
                                    ins.then_inc(SMM, 1)
                        P.add("tensor", fvm)
                        vm = P.inc("s_mm")

                        def fve(v, par=par, tt=tt, vm=vm, bvv=bvv):
                            v.wait_ge(SMM, vm)
                            if tt == 0:
                                v.wait_ge(SVC, bvv)
                            for hh in range(2):
                                slot = hh * 48 + tt
                                ins = v.tensor_tensor(
                                    VT[par][:, slot * 65:slot * 65 + 64],
                                    psG2[:, hh * 64:(hh + 1) * 64],
                                    bvrep[:, hh * 64:(hh + 1) * 64], ALU.add)
                                if hh == 1:
                                    ins.then_inc(SVC, 1)
                        P.add("vector", fve)
                        P.inc("s_vc")

                    tabs_done = P.val("s_sc")

                    def fvs(v, par=par, tabs_done=tabs_done):
                        v.wait_ge(SSC, tabs_done)
                        for hh in range(2):
                            for jo in range(16):
                                j = 2 * jo + 1
                                src = (hh * 48 + j) * 65
                                dstc = (hh * 48 + 32 + jo) * 65
                                ins = v.tensor_scalar(
                                    VT[par][:, dstc:dstc + 65],
                                    VT[par][:, src:src + 65],
                                    facm[par][:, hh:hh + 1],
                                    None, ALU.mult)
                                if hh == 1 and jo == 15:
                                    ins.then_inc(SVC, 1)
                    P.add("vector", fvs)
                    P.inc("s_vc")
                    return P.val("s_vc")

                def flash_group(g, par, gen_done):
                    evac_last = {0: 0, 1: 0}
                    first_av = {}
                    exp_done = {0: [], 1: []}
                    av_done = {0: [], 1: []}
                    pending_av = []
                    gb = 0
                    for qc in range(NQC):
                        for b in range(16):
                            j0 = 2 * b
                            kind = ("past" if b < qc else
                                    "band" if b == qc else "fut")
                            bank = gb % 2
                            sc_need = 0
                            if gb >= 1:
                                sc_need = max(exp_done[0][gb - 1],
                                              exp_done[1][gb - 1])

                            def fS(t, par=par, qc=qc, j0=j0, bank=bank,
                                   sc_need=sc_need,
                                   first=(qc == 0 and b == 0),
                                   gen_done=gen_done):
                                if first:
                                    t.wait_ge(SVC, gen_done)
                                if sc_need:
                                    t.wait_ge(SSC, sc_need)
                                for m in range(2):
                                    j = j0 + m
                                    for hh, pss in ((0, psSA), (1, psSB)):
                                        ps = pss[bank]
                                        for kh in range(2):
                                            ins = t.matmul(
                                                ps[kh * 64:(kh + 1) * 64,
                                                   m * 256:(m + 1) * 256],
                                                KT[par][hh * 64:(hh + 1) * 64,
                                                        j * 128 + kh * 64:
                                                        j * 128 + (kh + 1) * 64],
                                                QT[par][hh * 64:(hh + 1) * 64,
                                                        qc * 256:(qc + 1) * 256],
                                                start=True, stop=True,
                                                tile_position=(hh * 64, kh * 64))
                                            if m == 1 and hh == 1 and kh == 1:
                                                ins.then_inc(SMM, 1)
                            P.add("tensor", fS)
                            smm = P.inc("s_mm")

                            for hh, pss, psO in ((0, psSA, psOA), (1, psSB, psOB)):
                                avn = av_done[hh][gb - 1] if gb >= 1 else 0

                                def fexp(sc, par=par, hh=hh, pss=pss, qc=qc, b=b,
                                         bank=bank, kind=kind, smm=smm, avn=avn):
                                    sc.wait_ge(SMM, smm)
                                    if avn:
                                        sc.wait_ge(SAV, avn)
                                    dst = UB[hh][:, bank * 512:(bank + 1) * 512]
                                    if kind == "fut":
                                        s = b - qc
                                        sc.activation(
                                            dst, pss[bank][:], ACT.Exp,
                                            bias=BV[par][:, hh * 15 + s - 1:hh * 15 + s],
                                            scale=0.125).then_inc(SSC, 1)
                                    else:
                                        sc.activation(
                                            dst, pss[bank][:], ACT.Exp, bias=0.0,
                                            scale=0.125).then_inc(SSC, 1)
                                P.add("scalar", fexp)
                                edone = P.inc("s_sc")
                                exp_done[hh].append(edone)

                                eb_done = 0
                                if kind == "band":
                                    def feb(v, par=par, hh=hh, bank=bank,
                                            edone=edone):
                                        v.wait_ge(SSC, edone)
                                        for m in range(2):
                                            u = UB[hh][:, bank * 512 + m * 256:
                                                       bank * 512 + (m + 1) * 256]
                                            ins = v.tensor_tensor(
                                                u, u,
                                                EB[par][:, (hh * 2 + m) * 256:
                                                        (hh * 2 + m + 1) * 256],
                                                ALU.mult)
                                            if m == 1:
                                                ins.then_inc(SVC, 1)
                                    P.add("vector", feb)
                                    eb_done = P.inc("s_vc")

                                ev_need = evac_last[hh]

                                def fav(t, par=par, hh=hh, psO=psO, qc=qc, b=b,
                                        j0=j0, bank=bank, kind=kind, edone=edone,
                                        eb_done=eb_done, ev_need=ev_need,
                                        sp=first_av):
                                    t.wait_ge(SSC, edone)
                                    if eb_done:
                                        t.wait_ge(SVC, eb_done)
                                    if ev_need:
                                        t.wait_ge(SVC, ev_need)
                                    for m in range(2):
                                        j = j0 + m
                                        u = UB[hh][:, bank * 512 + m * 256:
                                                   bank * 512 + (m + 1) * 256]
                                        if kind == "fut":
                                            slot = (hh * 48 + 32 + j // 2
                                                    if (j % 2) else hh * 48 + j)
                                            dst = psO[0:65, 256:512]
                                            fk = (hh, "f", qc)
                                        else:
                                            slot = hh * 48 + j
                                            dst = psO[0:65, 0:256]
                                            fk = (hh, "p", qc)
                                        st = fk not in sp
                                        sp[fk] = True
                                        ins = t.matmul(
                                            dst, VT[par][:, slot * 65:(slot + 1) * 65],
                                            u, start=st, stop=True)
                                        if m == 1:
                                            ins.then_inc(SAV, 1)
                                pending_av.append(fav)
                                av_done[hh].append(P.inc("s_av"))
                            while len(pending_av) > 2:
                                P.add("tensor", pending_av.pop(0))
                            gb += 1
                        while pending_av:
                            P.add("tensor", pending_av.pop(0))
                        avm = P.val("s_av")

                        for hh, psO, dens in ((0, psOA, DensA), (1, psOB, DensB)):
                            has_fut = qc < NQC - 1

                            def fevac(v, par=par, hh=hh, psO=psO, dens=dens,
                                      qc=qc, avm=avm, has_fut=has_fut):
                                v.wait_ge(SAV, avm)
                                num = Opack[hh * 64:(hh + 1) * 64,
                                            qc * 256:(qc + 1) * 256]
                                dn = dens[0:1, qc * 256:(qc + 1) * 256]
                                if has_fut:
                                    v.tensor_tensor(tmp65[:], psO[0:65, 256:512],
                                                    Bb[par][:, hh * 256:(hh + 1) * 256],
                                                    ALU.mult)
                                    v.tensor_tensor(num, tmp65[0:64, :],
                                                    psO[0:64, 0:256], ALU.add)
                                    v.tensor_tensor(dn, tmp65[64:65, :],
                                                    psO[64:65, 0:256],
                                                    ALU.add).then_inc(SVC, 1)
                                else:
                                    v.tensor_copy(num, psO[0:64, 0:256])
                                    v.tensor_copy(dn, psO[64:65, 0:256]).then_inc(SVC, 1)
                            P.add("vector", fevac)
                            evac_last[hh] = P.inc("s_vc")
                    return max(evac_last.values())

                def division_group(g, par, evac_done):
                    def fz0(v, evac_done=evac_done):
                        v.wait_ge(SVC, evac_done)
                        v.memset(stat[:], 0.0).then_inc(SVC, 1)
                    P.add("vector", fz0)
                    P.inc("s_vc")
                    for ch in range(16):
                        bc_prev = P.val("s_mm")

                        def fsrd(sc, ch=ch, evac_done=evac_done, bc_prev=bc_prev):
                            if ch == 0:
                                sc.wait_ge(SVC, evac_done)
                            else:
                                sc.wait_ge(SMM, bc_prev)
                            lntmp_r = lnden_t[:]
                            sc.activation(lntmp_r, DensA[0:1, ch * 256:(ch + 1) * 256],
                                          ACT.Ln, bias=0.0, scale=1.0)
                            sc.drain()
                            sc.activation(rdenA, lntmp_r, ACT.Exp, bias=0.0,
                                          scale=-1.0)
                            sc.drain()
                            sc.activation(lntmp_r, DensB[0:1, ch * 256:(ch + 1) * 256],
                                          ACT.Ln, bias=0.0, scale=1.0)
                            sc.drain()
                            sc.activation(rdenB, lntmp_r, ACT.Exp, bias=0.0,
                                          scale=-1.0).then_inc(SSC, 1)
                        P.add("scalar", fsrd)
                        rdv = P.inc("s_sc")

                        od_prev = P.val("s_vc")

                        def fbc(t, rdv=rdv, od_prev=od_prev):
                            t.wait_ge(SSC, rdv)
                            t.wait_ge(SVC, od_prev)
                            t.matmul(psG2[0:64, 0:256], ONESF[0:1, 0:64],
                                     rdenA, start=True, stop=True)
                            t.matmul(psG2[64:128, 0:256], ONESF[0:1, 0:64],
                                     rdenB, start=True, stop=True,
                                     tile_position=(0, 64)).then_inc(SMM, 1)
                        P.add("tensor", fbc)
                        bcm = P.inc("s_mm")

                        def fod(v, ch=ch, bcm=bcm):
                            v.wait_ge(SMM, bcm)
                            sl = Opack[:, ch * 256:(ch + 1) * 256]
                            v.tensor_tensor(sl, sl, psG2[:, 0:256], ALU.mult)
                            v.reduce_sum(stat[:, 2 + 2 * ch:3 + 2 * ch], sl, axis=AX)
                            v.tensor_tensor(rtmp[:, 0:256], sl, sl, ALU.mult)
                            v.reduce_sum(stat[:, 3 + 2 * ch:4 + 2 * ch],
                                         rtmp[:, 0:256], axis=AX).then_inc(SVC, 1)
                        P.add("vector", fod)
                        P.inc("s_vc")

                    def fstat(v):
                        v.drain()
                        v.reduce_sum(stat[:, 0:1], stat[:, 2:34:2], axis=AX)
                        v.reduce_sum(stat[:, 1:2], stat[:, 3:35:2],
                                     axis=AX).then_inc(SVC, 1)
                    P.add("vector", fstat)
                    p1done = P.inc("s_vc")

                    def fsm(t, p1done=p1done):
                        t.wait_ge(SVC, p1done)
                        t.matmul(psG2[0:1, 0:2], onescol[0:64, 0:1],
                                 stat[0:64, 0:2], start=True, stop=True)
                        t.matmul(psG2[0:1, 2:4], onescol[64:128, 0:1],
                                 stat[64:128, 0:2], start=True, stop=True,
                                 tile_position=(64, 0)).then_inc(SMM, 1)
                    P.add("tensor", fsm)
                    smm2 = P.inc("s_mm")

                    def fmu(v, smm2=smm2):
                        v.wait_ge(SMM, smm2)
                        v.tensor_single_scalar(scl1[0:1, 0:1], psG2[0:1, 0:1],
                                               1.0 / (T * HD), ALU.mult)
                        v.tensor_single_scalar(scl1[0:1, 2:3], psG2[0:1, 1:2],
                                               1.0 / (T * HD), ALU.mult)
                        v.tensor_single_scalar(scl1[0:1, 1:2], psG2[0:1, 2:3],
                                               1.0 / (T * HD), ALU.mult)
                        v.tensor_single_scalar(scl1[0:1, 3:4], psG2[0:1, 3:4],
                                               1.0 / (T * HD), ALU.mult)
                        v.drain()
                        v.tensor_tensor(scl1[0:1, 4:5], scl1[0:1, 0:1],
                                        scl1[0:1, 0:1], ALU.mult)
                        v.tensor_tensor(scl1[0:1, 5:6], scl1[0:1, 1:2],
                                        scl1[0:1, 1:2], ALU.mult)
                        v.drain()
                        v.tensor_tensor(scl1[0:1, 4:5], scl1[0:1, 2:3],
                                        scl1[0:1, 4:5], ALU.subtract)
                        v.tensor_tensor(scl1[0:1, 5:6], scl1[0:1, 3:4],
                                        scl1[0:1, 5:6], ALU.subtract)
                        v.drain()
                        v.nop().then_inc(SVC, 1)
                    P.add("vector", fmu)
                    muv = P.inc("s_vc")

                    def frs2(sc, muv=muv):
                        sc.wait_ge(SVC, muv)
                        sc.activation(scl1[0:1, 6:7], scl1[0:1, 4:5], ACT.Ln,
                                      bias=epscol[0:1, 0:1], scale=1.0)
                        sc.activation(scl1[0:1, 7:8], scl1[0:1, 5:6], ACT.Ln,
                                      bias=epscol[0:1, 0:1], scale=1.0)
                        sc.drain()
                        sc.activation(scl1[0:1, 6:7], scl1[0:1, 6:7], ACT.Exp,
                                      bias=0.0, scale=-0.5)
                        sc.activation(scl1[0:1, 7:8], scl1[0:1, 7:8], ACT.Exp,
                                      bias=0.0, scale=-0.5).then_inc(SSC, 1)
                    P.add("scalar", frs2)
                    rsv = P.inc("s_sc")

                    def fcb(t, rsv=rsv):
                        t.wait_ge(SSC, rsv)
                        t.matmul(psG2[0:64, 0:1], ONESF[0:1, 0:64],
                                 scl1[0:1, 0:1], start=True, stop=True,
                                 tile_position=(0, 0))
                        t.matmul(psG2[64:128, 0:1], ONESF[0:1, 0:64],
                                 scl1[0:1, 1:2], start=True, stop=True,
                                 tile_position=(0, 64))
                        t.matmul(psG2[0:64, 1:2], ONESF[0:1, 0:64],
                                 scl1[0:1, 6:7], start=True, stop=True,
                                 tile_position=(0, 0))
                        t.matmul(psG2[64:128, 1:2], ONESF[0:1, 0:64],
                                 scl1[0:1, 7:8], start=True, stop=True,
                                 tile_position=(0, 64)).then_inc(SMM, 1)
                    P.add("tensor", fcb)
                    cbm = P.inc("s_mm")

                    def fcoef(v, g=g, cbm=cbm):
                        v.wait_ge(SMM, cbm)
                        v.tensor_tensor(coef[:, 0:1], pvec["gamma"][:, g:g + 1],
                                        psG2[:, 1:2], ALU.mult)
                        v.drain()
                        v.tensor_tensor(coef[:, 1:2], psG2[:, 0:1], coef[:, 0:1],
                                        ALU.mult)
                        v.drain()
                        v.tensor_tensor(coef[:, 1:2], pvec["beta"][:, g:g + 1],
                                        coef[:, 1:2], ALU.subtract)
                        v.drain()
                        v.nop().then_inc(SVC, 1)
                    P.add("vector", fcoef)
                    cov = P.inc("s_vc")

                    for ch in range(8):
                        gp = ch % 2
                        scr_need = P.val("s_scr") if ch >= 2 else 0

                        def fg2(v, g=g, par=par, ch=ch, gp=gp, cov=cov,
                                scr_need=scr_need):
                            if ch == 0:
                                v.wait_ge(SVC, cov)
                            if scr_need:
                                v.wait_ge(SSCR, scr_need)
                            sl = Opack[:, ch * 512:(ch + 1) * 512]
                            v.tensor_scalar(rtmp[:], sl, coef[:, 0:1],
                                            coef[:, 1:2], ALU.mult, ALU.add)
                            v.tensor_tensor(g2b[:, gp * 512:(gp + 1) * 512],
                                            rtmp[:],
                                            GT[par][:, ch * 512:(ch + 1) * 512],
                                            ALU.mult).then_inc(SVC, 1)
                        P.add("vector", fg2)
                        g2v = P.inc("s_vc")

                        def fsc(sy, g=g, ch=ch, gp=gp, g2v=g2v):
                            sy.wait_ge(SVC, g2v)
                            sy.dma_start(out=g2s[g * 128:(g + 1) * 128,
                                                 ch * 512:(ch + 1) * 512],
                                         in_=g2b[:, gp * 512:(gp + 1) * 512]
                                         ).then_inc(SSCR, 16)
                        P.add("sync", fsc)
                        P.inc("s_scr", 16)

                groups = range(ngroups) if upto != "p1" else range(0)
                for g in groups:
                    par = 0
                    gen_done = gen_group(g, par)
                    if g == 0 and "dbg2" in taps:
                        t_wsl = tap("WSL", [128, 4 * E], BF16)
                        t_ldbc = tap("ldbc", [128, 2], F32)
                        t_bb = tap("Bb", [65, 2 * 256], F32)
                        t_bv = tap("BVt", [128, 2 * 15], F32)
                        t_eb = tap("EBt", [128, 2 * 2 * 256], BF16)

                        def ftapg(sy, n=gen_done, t_wsl=t_wsl, t_ldbc=t_ldbc,
                                  t_bb=t_bb, t_bv=t_bv, t_eb=t_eb):
                            sy.wait_ge(SVC, n)
                            sy.dma_start(out=t_wsl, in_=WSL[:]).then_inc(SOUT, 16)
                            sy.dma_start(out=t_ldbc, in_=ldbc[0][:]).then_inc(SOUT, 16)
                            sy.dma_start(out=t_bb, in_=Bb[0][:]).then_inc(SOUT, 16)
                            sy.dma_start(out=t_bv, in_=BV[0][:]).then_inc(SOUT, 16)
                            sy.dma_start(out=t_eb, in_=EB[0][:]).then_inc(SOUT, 16)
                        P.add("sync", ftapg)
                        P.inc("s_out", 80)
                    ev_done = flash_group(g, par, gen_done)
                    if g == 0 and "Opre" in taps:
                        t_opre = tap("Opre", [128, T], BF16)
                        t_dpre = tap("Dpre", [2, T], F32)

                        def ftappre(sy, n=ev_done, t_opre=t_opre, t_dpre=t_dpre):
                            sy.wait_ge(SVC, n)
                            sy.dma_start(out=t_opre, in_=Opack[:]).then_inc(SOUT, 16)
                            sy.dma_start(out=t_dpre[0:1, :], in_=DensA[:]).then_inc(SOUT, 16)
                            sy.dma_start(out=t_dpre[1:2, :], in_=DensB[:]).then_inc(SOUT, 16)
                        P.add("sync", ftappre)
                        tapn = P.inc("s_out", 48)

                        def fgate(v, tapn=tapn):
                            v.wait_ge(SOUT, tapn)
                        P.add("vector", fgate)
                        P.add("scalar", lambda sc, tapn=tapn: sc.wait_ge(SOUT, tapn))
                    division_group(g, par, ev_done)

                    if g == 0:
                        for nm, buf in (("QT", QT[0]), ("KT", KT[0]),
                                        ("GT", GT[0]), ("VT", VT[0])):
                            if nm in taps:
                                shp = [128, T] if nm != "VT" else [128, 2 * 48 * 65]
                                tpp = tap(nm, shp, BF16)

                                def ftap(sy, tpp=tpp, buf=buf, n=gen_done):
                                    sy.wait_ge(SVC, n)
                                    sy.dma_start(out=tpp, in_=buf[:]).then_inc(SOUT, 16)
                                P.add("sync", ftap)
                                P.inc("s_out", 16)
                        if "Opack" in taps:
                            tpo = tap("Opack", [128, T], BF16)
                            tpd = tap("Dens", [2, T], F32)
                            nvc = P.val("s_vc")

                            def ftap(sy, n=nvc, tpo=tpo, tpd=tpd):
                                sy.wait_ge(SVC, n)
                                sy.dma_start(out=tpo, in_=Opack[:]).then_inc(SOUT, 16)
                                sy.dma_start(out=tpd[0:1, :], in_=DensA[:]).then_inc(SOUT, 16)
                                sy.dma_start(out=tpd[1:2, :], in_=DensB[:]).then_inc(SOUT, 16)
                            P.add("sync", ftap)
                            P.inc("s_out", 48)
                g2_all = P.val("s_scr")

            with contextlib.ExitStack() as wst:
                wa = lambda n, sh, dt: wst.enter_context(nc.sbuf_tensor("sb_" + n, sh, dt))
                woT = wa("woT", [128, NCH * E], BF16)
                g2rd = wa("g2rd", [128, 2 * NCH * 512], BF16)
                attb = wa("attb", [128, 2 * 512], BF16)
                attT = wa("attT", [128, 2 * 4096], BF16)
                xfw = wa("xfw", [128, 2 * E], F32)
                x1f = wa("x1f", [128, E], F32)
                xn2 = wa("xn2", [128, E], BF16)
                x1bf = wa("x1bf", [128, 2 * E], BF16)
                h2blk = wa("h2blk", [128, 2 * E], BF16)
                lnw = wa("lnw", [128, 8], F32)
                lntw = wa("lntw", [128, E], F32)

                do_wo = upto in ("wo", "full")
                if do_wo:
                    vcg = P.val("s_vc")

                    def fdummy(t, vcg=vcg):
                        t.wait_ge(SVC, vcg)
                        t.matmul(psG1[0:1, 0:1], ONESF[0:1, 0:1], ONESF[0:1, 0:1],
                                 start=True, stop=True).then_inc(SMM, 1)
                    P.add("tensor", fdummy)
                    P.inc("s_mm")

                    def fwo_w(gp, n=g2_all):
                        gp.wait_ge(SSCR, n)
                        gp.dma_start(out=woT[:].rearrange("p (c j) -> p c j", c=NCH),
                                     in_=wT_d["wo"].rearrange("(c p) j -> p c j",
                                                              p=128)).then_inc(SW, 16)
                    P.add("gpsimd", fwo_w)
                    wo_w = P.inc("s_w", 16)
                    fae_hist = []

                for tc in (range(NCH) if do_wo else range(0)):
                    gpar = tc % 2
                    g2need = P.val("s_mm") if tc >= 2 else 0

                    def fg2r(gp, tc=tc, gpar=gpar, g2need=g2need, n=g2_all):
                        if tc == 0:
                            gp.wait_ge(SSCR, n)
                        if g2need:
                            gp.wait_ge(SMM, g2need)
                        gp.dma_start(
                            out=g2rd[:, gpar * NCH * 512:(gpar + 1) * NCH * 512
                                     ].rearrange("p (c t) -> p c t", c=NCH),
                            in_=g2s[:, tc * 512:(tc + 1) * 512].rearrange(
                                "(c p) t -> p c t", p=128)).then_inc(SW, 16)
                    P.add("gpsimd", fg2r)
                    g2r = P.inc("s_w", 16)

                    if tc == 0 and "dbg3" in taps:
                        t_wo = tap("woTv", [128, NCH * E], BF16)
                        t_g2r = tap("g2rdv", [128, NCH * 512], BF16)

                        def ftap3(sy, g2r=g2r, wo_w=wo_w, t_wo=t_wo, t_g2r=t_g2r):
                            sy.wait_ge(SW, max(g2r, wo_w))
                            sy.dma_start(out=t_wo, in_=woT[:]).then_inc(SOUT, 16)
                            sy.dma_start(out=t_g2r, in_=g2rd[:, 0:NCH * 512]).then_inc(SOUT, 16)
                        P.add("sync", ftap3)
                        P.inc("s_out", 32)

                    for dt in range(NCH):
                        apar = dt % 2
                        widx = tc * NCH + dt
                        vc_prev = (fae_hist[widx - 2] if widx >= 2
                                   else P.val("s_vc"))

                        def fwm(t, tc=tc, dt=dt, gpar=gpar, g2r=g2r, wo_w=wo_w,
                                vc_prev=vc_prev):
                            t.wait_ge(SW, max(g2r, wo_w))
                            t.wait_ge(SVC, vc_prev)
                            dstp = psG1[:] if dt % 2 == 0 else psG2[:]
                            for c in range(NCH):
                                ins = t.matmul(
                                    dstp,
                                    woT[:, c * E + dt * 128:c * E + (dt + 1) * 128],
                                    g2rd[:, gpar * NCH * 512 + c * 512:
                                         gpar * NCH * 512 + (c + 1) * 512],
                                    start=(c == 0), stop=(c == NCH - 1))
                                if c == NCH - 1:
                                    ins.then_inc(SMM, 1)
                        P.add("tensor", fwm)
                        wm = P.inc("s_mm")

                        tp_need = P.val("s_tp") if (tc * NCH + dt) >= 2 else 0

                        def fae(v, dt=dt, apar=apar, wm=wm, tp_need=tp_need):
                            v.wait_ge(SMM, wm)
                            if tp_need:
                                v.wait_ge(STP, tp_need)
                            dstp = psG1[:] if dt % 2 == 0 else psG2[:]
                            v.tensor_scalar(attb[:, apar * 512:(apar + 1) * 512],
                                            dstp, pvec["bo"][:, dt:dt + 1],
                                            gacl[:, dt:dt + 1], ALU.add,
                                            ALU.mult).then_inc(SVC, 1)
                        P.add("vector", fae)
                        ae = P.inc("s_vc")
                        fae_hist.append(ae)

                        if tc == 0 and dt == 0 and "dbg5" in taps:
                            t_ab = tap("attbv", [128, 512], BF16)

                            def ftap5(sy, ae=ae, t_ab=t_ab):
                                sy.wait_ge(SVC, ae)
                                sy.dma_start(out=t_ab, in_=attb[:, 0:512]).then_inc(SOUT, 16)
                            P.add("sync", ftap5)
                            P.inc("s_out", 16)

                        def ftr(sq, tc=tc, dt=dt, apar=apar, ae=ae):
                            sq.wait_ge(SVC, ae)
                            for ti in range(4):
                                sq.dma_start_transpose(
                                    attT[:, (tc % 2) * 4096 + ti * 1024 + dt * 128:
                                         (tc % 2) * 4096 + ti * 1024 + (dt + 1) * 128],
                                    attb[:, apar * 512 + ti * 128:
                                         apar * 512 + (ti + 1) * 128]
                                ).then_inc(STP, 16)
                        P.add("sync", ftr)
                        P.inc("s_tp", 64)
                    tr_all = P.val("s_tp")

                    if tc == 0 and "dbg4" in taps:
                        t_att = tap("attTv", [128, 4096], BF16)

                        def ftap4(sy, tr_all=tr_all, t_att=t_att):
                            sy.wait_ge(STP, tr_all)
                            sy.dma_start(out=t_att, in_=attT[:, 0:4096]).then_inc(SOUT, 16)
                        P.add("sync", ftap4)
                        P.inc("s_out", 16)

                    for ti in range(4):
                        tt = tc * 4 + ti
                        xpar = tt % 2
                        vc_prev = P.val("s_vc") if tt >= 2 else 0

                        def fxl(sy, tt=tt, xpar=xpar, vc_prev=vc_prev):
                            if vc_prev:
                                sy.wait_ge(SVC, vc_prev)
                            sy.dma_start(out=xfw[:, xpar * E:(xpar + 1) * E],
                                         in_=x_d[tt * 128:(tt + 1) * 128, :]
                                         ).then_inc(SX, 16)
                        P.add("sync", fxl)
                        xd = P.inc("s_x", 16)

                        def fx1(v, tc=tc, ti=ti, xpar=xpar, xd=xd, tr_all=tr_all):
                            v.wait_ge(SX, xd)
                            v.wait_ge(STP, tr_all)
                            v.tensor_tensor(x1f[:], xfw[:, xpar * E:(xpar + 1) * E],
                                            attT[:, (tc % 2) * 4096 + ti * 1024:
                                                 (tc % 2) * 4096 + (ti + 1) * 1024],
                                            ALU.add)
                            v.reduce_sum(lnw[:, 0:1], x1f[:], axis=AX)
                            v.tensor_tensor(lntw[:], x1f[:], x1f[:], ALU.mult)
                            v.reduce_sum(lnw[:, 1:2], lntw[:],
                                         axis=AX).then_inc(SVC, 1)
                        P.add("vector", fx1)
                        stv = P.inc("s_vc")

                        def fr2a(sc, stv=stv):
                            sc.wait_ge(SVC, stv)
                            sc.activation(lnw[:, 5:6], lnw[:, 0:1], ACT.Copy,
                                          bias=0.0, scale=1.0 / E)
                            sc.activation(lnw[:, 2:3], lnw[:, 0:1], ACT.Square,
                                          bias=0.0, scale=1.0 / E).then_inc(SSC, 1)
                        P.add("scalar", fr2a)
                        r2a = P.inc("s_sc")

                        def fem2(v, r2a=r2a):
                            v.wait_ge(SSC, r2a)
                            v.tensor_scalar(lnw[:, 3:4], lnw[:, 2:3], -1.0,
                                            epscol[:, 0:1], ALU.mult,
                                            ALU.add).then_inc(SVC, 1)
                        P.add("vector", fem2)
                        em2 = P.inc("s_vc")

                        def fr2b(sc, em2=em2):
                            sc.wait_ge(SVC, em2)
                            sc.activation(lnw[:, 3:4], lnw[:, 1:2], ACT.Ln,
                                          bias=lnw[:, 3:4], scale=1.0 / E)
                            sc.drain()
                            sc.activation(lnw[:, 4:5], lnw[:, 3:4], ACT.Exp,
                                          bias=0.0, scale=-0.5).then_inc(SSC, 1)
                        P.add("scalar", fr2b)
                        rs2 = P.inc("s_sc")
                        tp_prev = P.val("s_tp")

                        scr_rd = P.val("s_scr") if tt >= 2 else 0

                        def fn2(v, rs2=rs2, tp_prev=tp_prev, tt=tt, scr_rd=scr_rd):
                            v.wait_ge(SSC, rs2)
                            if tt >= 2:
                                v.wait_ge(STP, tp_prev)
                            if scr_rd:
                                v.wait_ge(SSCR, scr_rd)
                            v.tensor_scalar(xn2[:], x1f[:], lnw[:, 5:6],
                                            lnw[:, 4:5], ALU.subtract, ALU.mult)
                            v.tensor_copy(x1bf[:, (tt % 2) * E:(tt % 2 + 1) * E], x1f[:]).then_inc(SVC, 1)
                        P.add("vector", fn2)
                        n2 = P.inc("s_vc")

                        bpar = tt % 2
                        scr_prev = P.val("s_scr") if tt >= 2 else 0

                        def ftp2(sq, bpar=bpar, n2=n2, scr_prev=scr_prev):
                            sq.wait_ge(SVC, n2)
                            if scr_prev:
                                sq.wait_ge(SSCR, scr_prev)
                            for c in range(NCH):
                                sq.dma_start_transpose(
                                    h2blk[:, bpar * E + c * 128:bpar * E + (c + 1) * 128],
                                    xn2[:, c * 128:(c + 1) * 128]).then_inc(STP, 16)
                        P.add("scalar", ftp2)
                        tp2 = P.inc("s_tp", 16 * 8)

                        def faff(v, bpar=bpar, tp2=tp2):
                            v.wait_ge(STP, tp2)
                            for c in range(NCH):
                                ins = v.tensor_scalar(
                                    h2blk[:, bpar * E + c * 128:bpar * E + (c + 1) * 128],
                                    h2blk[:, bpar * E + c * 128:bpar * E + (c + 1) * 128],
                                    cbp1[:, c:c + 1], mcol(3, c), ALU.mult, ALU.add)
                                if c == NCH - 1:
                                    ins.then_inc(SVC, 1)
                        P.add("vector", faff)
                        aff = P.inc("s_vc")

                        def fsw(sy, tt=tt, bpar=bpar, aff=aff):
                            sy.wait_ge(SVC, aff)
                            sy.dma_start(
                                out=h2s[:, tt * 128:(tt + 1) * 128].rearrange(
                                    "(c p) t -> p c t", p=128),
                                in_=h2blk[:, bpar * E:(bpar + 1) * E].rearrange(
                                    "p (c t) -> p c t", c=NCH)).then_inc(SSCR, 16)
                            sy.dma_start(
                                out=x1s[tt * 128:(tt + 1) * 128, :],
                                in_=x1bf[:, (tt % 2) * E:(tt % 2 + 1) * E]).then_inc(SSCR, 16)
                        P.add("sync", fsw)
                        P.inc("s_scr", 32)
                if "dbg6" in taps:
                    t_wo2 = tap("woTv2", [128, NCH * E], BF16)

                    def ftap6(sy, n=P.val("s_scr"), t_wo2=t_wo2):
                        sy.wait_ge(SSCR, n)
                        sy.dma_start(out=t_wo2, in_=woT[:]).then_inc(SOUT, 16)
                    P.add("sync", ftap6)
                    P.inc("s_out", 16)
                wo_all = P.val("s_scr")

        with contextlib.ExitStack() as fst:
            fa_ = lambda n, sh, dt: fst.enter_context(nc.sbuf_tensor("sb_" + n, sh, dt))
            WFC = fa_("WFC", [128, NCH * FF], BF16)
            WPR = fa_("WPR", [128, 32 * E], BF16)
            H2R = fa_("H2R", [128, NCH * 512], BF16)
            A2 = fa_("A2", [128, 32 * 512], BF16)
            RL = fa_("RL", [128, 512], BF16)
            YB = fa_("YB", [128, 512], F32)
            YH = fa_("YH", [128, 512], F32)
            SG = [fa_(f"SG{i}", [128, 512], F32) for i in range(2)]
            ACCB = fa_("ACCB", [128, 512], F32)
            SNT = fa_("SNT", [128, 4096], BF16)
            SNB = RL
            GBR = fa_("GBR", [128, E], F32)
            X1T = fa_("X1T", [128, E], BF16)
            OUTB = fa_("OUTB", [128, E], F32)

            do_ffn = upto == "full"

            def ffw(gp, n=wo_all):
                gp.wait_ge(SSCR, n)
                gp.dma_start(out=WFC[:].rearrange("p (c f) -> p c f", c=NCH),
                             in_=wfcT_d.rearrange("(c p) f -> p c f", p=128)
                             ).then_inc(SW, 16)
                gp.dma_start(out=WPR[:].rearrange("p (f d) -> p f d", f=32),
                             in_=wprT_d.rearrange("(f p) d -> p f d", p=128)
                             ).then_inc(SW, 16)
            if do_ffn:
                P.add("gpsimd", ffw)
                ff_w = P.inc("s_w", 32)
            else:
                ff_w = 0

            vc_gate = P.val("s_vc")

            def fgb(t, n=wo_all, vc_gate=vc_gate):
                t.wait_ge(SVC, vc_gate)
                t.matmul(psG1[:], ONESF[:], gbrow[0:1, 0:512],
                         start=True, stop=True)
                t.matmul(psG2[:], ONESF[:], gbrow[0:1, 512:1024],
                         start=True, stop=True).then_inc(SMM, 1)
            if do_ffn:
                P.add("tensor", fgb)
                gbm = P.inc("s_mm")

                def fgbv(v, gbm=gbm):
                    v.wait_ge(SMM, gbm)
                    v.tensor_copy(GBR[:, 0:512], psG1[:])
                    v.tensor_copy(GBR[:, 512:1024], psG2[:]).then_inc(SVC, 1)
                P.add("vector", fgbv)
                P.inc("s_vc")
            else:
                vall = P.val("s_vc")
                sall = P.val("s_sc")
                mall = P.val("s_mm")
                scrall = P.val("s_scr")

                def fdz(v, vall=vall):
                    v.wait_ge(SVC, vall)
                    v.memset(OUTB[:], 0.0).then_inc(SVC, 1)
                P.add("vector", fdz)
                dz = P.inc("s_vc")

                def fdy(sy, dz=dz, sall=sall, mall=mall, scrall=scrall):
                    sy.wait_ge(SVC, dz)
                    sy.wait_ge(SSC, sall)
                    sy.wait_ge(SMM, mall)
                    if scrall:
                        sy.wait_ge(SSCR, scrall)
                    for tt in range(NTT):
                        sy.dma_start(out=y_d[tt * 128:(tt + 1) * 128, :],
                                     in_=OUTB[:]).then_inc(SOUT, 16)
                P.add("sync", fdy)
                P.inc("s_out", 16 * NTT)

            v0_hist = []
            acc_hist = []
            for tc in (range(NCH) if do_ffn else range(0)):
                hpar = 0
                h2need = P.val("s_mm") if tc >= 1 else 0

                def fh2(gp, tc=tc, hpar=hpar, h2need=h2need, n=wo_all):
                    if tc == 0:
                        gp.wait_ge(SSCR, n)
                    if h2need:
                        gp.wait_ge(SMM, h2need)
                    gp.dma_start(
                        out=H2R[:, hpar * NCH * 512:(hpar + 1) * NCH * 512
                                ].rearrange("p (c t) -> p c t", c=NCH),
                        in_=h2s[:, tc * 512:(tc + 1) * 512].rearrange(
                            "(c p) t -> p c t", p=128)).then_inc(SW, 16)
                P.add("gpsimd", fh2)
                h2r = P.inc("s_w", 16)

                rl_hist = []
                for ft in range(32):
                    fpar = ft % 4
                    sc_prev = rl_hist[ft - 4] if ft >= 4 else 0
                    ymm_prev = P.val("s_mm") if (tc >= 1 and ft == 0) else 0

                    def ffc(t, tc=tc, ft=ft, fpar=fpar, h2r=h2r, ff_w=ff_w,
                            hpar=hpar, sc_prev=sc_prev):
                        if ft == 0:
                            t.wait_ge(SW, max(h2r, ff_w))
                        if sc_prev:
                            t.wait_ge(SSC, sc_prev)
                        dstp = (psA0[:], psA1[:], psB0[:], psB1[:])[fpar]
                        for c in range(NCH):
                            ins = t.matmul(
                                dstp,
                                WFC[:, c * FF + ft * 128:c * FF + (ft + 1) * 128],
                                H2R[:, hpar * NCH * 512 + c * 512:
                                     hpar * NCH * 512 + (c + 1) * 512],
                                start=(c == 0), stop=(c == NCH - 1))
                            if c == NCH - 1:
                                ins.then_inc(SMM, 1)
                    P.add("tensor", ffc)
                    fmm = P.inc("s_mm")
                    vc_prev = P.val("s_vc") if ft >= 1 else 0

                    tp_need = P.val("s_tp") if (tc >= 1 and ft == 0) else 0

                    def frl(sc, fpar=fpar, fmm=fmm, vc_prev=vc_prev, tp_need=tp_need):
                        sc.wait_ge(SMM, fmm)
                        if vc_prev:
                            sc.wait_ge(SVC, vc_prev)
                        if tp_need:
                            sc.wait_ge(STP, tp_need)
                        dstp = (psA0[:], psA1[:], psB0[:], psB1[:])[fpar]
                        sc.activation(RL[:], dstp, ACT.Relu, bias=0.0,
                                      scale=1.0).then_inc(SSC, 1)
                    P.add("scalar", frl)
                    rl = P.inc("s_sc")
                    rl_hist.append(rl)
                    ym_prev = P.val("s_mm") if tc >= 1 and ft < 8 else 0

                    def fsq(v, ft=ft, rl=rl, ym_prev=ym_prev):
                        v.wait_ge(SSC, rl)
                        if ym_prev:
                            v.wait_ge(SMM, ym_prev)
                        v.tensor_tensor(A2[:, ft * 512:(ft + 1) * 512], RL[:],
                                        RL[:], ALU.mult).then_inc(SVC, 1)
                    P.add("vector", fsq)
                    P.inc("s_vc")
                a2done = P.val("s_vc")

                for dt in range(NCH):
                    opar = dt % 2
                    prevy = v0_hist[-1] if v0_hist else P.val("s_vc")

                    def fy2(t, dt=dt, a2done=a2done, prevy=prevy):
                        t.wait_ge(SVC, max(a2done, prevy))
                        for ft in range(32):
                            ins = t.matmul(
                                psG1[:],
                                WPR[:, ft * E + dt * 128:ft * E + (dt + 1) * 128],
                                A2[:, ft * 512:(ft + 1) * 512],
                                start=(ft == 0), stop=(ft == 31))
                            if ft == 31:
                                ins.then_inc(SMM, 1)
                    P.add("tensor", fy2)
                    ym = P.inc("s_mm")

                    accp = acc_hist[-1] if acc_hist else 0

                    def fs0(sc, ym=ym, accp=accp):
                        sc.wait_ge(SMM, ym)
                        if accp:
                            sc.wait_ge(SVC, accp)
                        sc.activation(SG[0][:], psG1[:], ACT.Sign,
                                      bias=negones[:, 0:1],
                                      scale=1.0).then_inc(SSC, 1)
                    P.add("scalar", fs0)
                    s0 = P.inc("s_sc")

                    def fv0(v, s0=s0):
                        v.wait_ge(SSC, s0)
                        v.tensor_copy(YB[:], psG1[:])
                        v.tensor_single_scalar(YH[:], YB[:], 0.5, ALU.subtract)
                        v.tensor_copy(ACCB[:], SG[0][:]).then_inc(SVC, 1)
                    P.add("vector", fv0)
                    v0_hist.append(P.inc("s_vc"))

                    for st in range(4):
                        sprev = st % 2
                        snew = (st + 1) % 2

                        def fms(v, sprev=sprev):
                            v.scalar_tensor_tensor(YB[:], YB[:], 0.9, YH[:],
                                                   ALU.mult, ALU.add)
                            v.scalar_tensor_tensor(YB[:], SG[sprev][:], -0.5,
                                                   YB[:], ALU.mult,
                                                   ALU.add).then_inc(SVC, 1)
                        P.add("vector", fms)
                        mv = P.inc("s_vc")

                        def fss(sc, snew=snew, mv=mv):
                            sc.wait_ge(SVC, mv)
                            sc.activation(SG[snew][:], YB[:], ACT.Sign,
                                          bias=negones[:, 0:1],
                                          scale=1.0).then_inc(SSC, 1)
                        P.add("scalar", fss)
                        ss = P.inc("s_sc")

                        def facc(v, snew=snew, ss=ss):
                            v.wait_ge(SSC, ss)
                            v.tensor_tensor(ACCB[:], ACCB[:], SG[snew][:],
                                            ALU.add).then_inc(SVC, 1)
                        P.add("vector", facc)
                        av = P.inc("s_vc")
                        if st == 3:
                            acc_hist.append(av)
                    out_prev = P.val("s_out") if tc >= 1 and dt == 0 else 0

                    def fsnb(v, dt=dt, out_prev=out_prev):
                        v.tensor_scalar(SNB[:], ACCB[:], 0.1, halfcol[:, 0:1],
                                        ALU.mult, ALU.add).then_inc(SVC, 1)
                    P.add("vector", fsnb)
                    snb = P.inc("s_vc")

                    def ftsn(sq, dt=dt, snb=snb, out_prev=out_prev):
                        sq.wait_ge(SVC, snb)
                        if out_prev:
                            sq.wait_ge(SOUT, out_prev)
                        for ti in range(4):
                            sq.dma_start_transpose(
                                SNT[:, ti * 1024 + dt * 128:ti * 1024 + (dt + 1) * 128],
                                SNB[:, ti * 128:(ti + 1) * 128]).then_inc(STP, 16)
                    P.add("scalar", ftsn)
                    P.inc("s_tp", 64)
                snt_all = P.val("s_tp")

                for ti in range(4):
                    tt = tc * 4 + ti
                    vprev = P.val("s_vc") if tt >= 1 else 0
                    oprev = P.val("s_out") if tt >= 1 else 0

                    def fx1r(sy, tt=tt, vprev=vprev):
                        if vprev:
                            sy.wait_ge(SVC, vprev)
                        sy.dma_start(out=X1T[:],
                                     in_=x1s[tt * 128:(tt + 1) * 128, :]
                                     ).then_inc(SX, 16)
                    P.add("sync", fx1r)
                    x1r = P.inc("s_x", 16)

                    def fout(v, ti=ti, x1r=x1r, snt_all=snt_all, oprev=oprev):
                        v.wait_ge(SX, x1r)
                        v.wait_ge(STP, snt_all)
                        if oprev:
                            v.wait_ge(SOUT, oprev)
                        v.tensor_tensor(OUTB[:],
                                        SNT[:, ti * 1024:(ti + 1) * 1024],
                                        GBR[:], ALU.mult)
                        v.tensor_tensor(OUTB[:], OUTB[:], X1T[:],
                                        ALU.add).then_inc(SVC, 1)
                    P.add("vector", fout)
                    ov = P.inc("s_vc")

                    def fst_(sy, tt=tt, ov=ov):
                        sy.wait_ge(SVC, ov)
                        sy.dma_start(out=y_d[tt * 128:(tt + 1) * 128, :],
                                     in_=OUTB[:]).then_inc(SOUT, 16)
                    P.add("sync", fst_)
                    P.inc("s_out", 16)

        out_total = P.val("s_out")

        def ffinal(sy, out_total=out_total):
            sy.wait_ge(SOUT, out_total)
        P.add("sync", ffinal)

        with nc.Block() as block:
            P.emit(block)
    return nc


def host_inputs(inputs):
    f = np.ascontiguousarray
    x = np.asarray(inputs["L0_attn_x"])
    cc = np.asarray(inputs["L0_c"])
    shared = {
        "wmod": f(np.asarray(inputs["L0_wmod"])),
        "wqT": f(np.asarray(inputs["L0_attn_wq"]).T),
        "wkT": f(np.asarray(inputs["L0_attn_wk"]).T),
        "wvT": f(np.asarray(inputs["L0_attn_wv"]).T),
        "wgT": f(np.asarray(inputs["L0_attn_wg"]).T),
        "woT": f(np.asarray(inputs["L0_attn_wo"]).T),
        "wdT": f(np.asarray(inputs["L0_attn_wd"]).T),
        "bdrow": f(np.asarray(inputs["L0_attn_bd"]).reshape(1, H)),
        "bvrow": f(np.asarray(inputs["L0_attn_bv"]).reshape(1, E)),
        "wfcT": f(np.asarray(inputs["L0_ffn_W_fc"]).T),
        "wprT": f(np.asarray(inputs["L0_ffn_W_proj"]).T),
    }
    for nm, key in (("bq", "L0_attn_bq"), ("bk", "L0_attn_bk"),
                    ("bg", "L0_attn_bg"), ("bo", "L0_attn_bo"),
                    ("gamma", "L0_attn_gamma"), ("beta", "L0_attn_beta")):
        shared[nm + "col"] = f(np.asarray(inputs[key]).reshape(NCH, 128).T)
    maps = []
    for b in range(8):
        m = dict(shared)
        m["x"] = f(x[b])
        m["ccol"] = f(cc[b].reshape(NCH, 128).T)
        maps.append(m)
    return maps


_NC_CACHE = {}


def kernel(**inputs):
    taps = ()
    if taps not in _NC_CACHE:
        _NC_CACHE[taps] = build_nc(taps)
    nc = _NC_CACHE[taps]
    maps = host_inputs(inputs)
    res = bass_utils.run_bass_kernel_spmd(nc, maps, core_ids=list(range(8)))
    out = np.stack([res.results[b]["y"] for b in range(8)], axis=0)
    return out.astype(np.float32)
